# version 16
# baseline (speedup 1.0000x reference)
import contextlib
import numpy as np
import concourse.bass as bass
import concourse.mybir as mybir
from concourse.bass_utils import run_bass_kernel_spmd

F32 = mybir.dt.float32
AF = mybir.ActivationFunctionType
OP = mybir.AluOpType
AX = mybir.AxisListType.X

C_RND = 12582912.0
N_CORES = 8


class Sem:
    def __init__(self, ctx, nc, name):
        self.h = ctx.enter_context(nc.semaphore(name))
        self.n = 0

    def inc(self, instr, k=1):
        instr.then_inc(self.h, k)
        self.n += k
        return instr

    def dinc(self, instr):
        return self.inc(instr, 16)


def build_program(S=2048, B=2, stop_after=None):
    _P0_STOP = [False]
    E, I, NH = 1024, 3072, 16
    T = B * S
    QB = 256
    KSPL = 5
    NQB = S // QB
    NKT = S // 128
    CH = min(256, S)
    NCH = S // CH
    ET = E // 128
    IT = I // 128
    CW = min(2048, T)
    NFT = IT * (T // CW)

    nc = bass.Bass("TRN2", target_bir_lowering=False)

    x_in = nc.dram_tensor("x", [T, E], F32, kind="ExternalInput").ap()
    wT = {p: nc.dram_tensor(f"w{p}T", [E, E], F32, kind="ExternalInput").ap()
          for p in ("q", "k", "v", "o")}
    bvec = {p: nc.dram_tensor(f"b{p}", [E], F32, kind="ExternalInput").ap()
            for p in ("v", "o")}
    bmat = {p: nc.dram_tensor(f"bm{p}", [128, E // 128], F32,
                              kind="ExternalInput").ap()
            for p in ("q", "k")}
    w_up = nc.dram_tensor("w_up", [I, E], F32, kind="ExternalInput").ap()
    w_gate = nc.dram_tensor("w_gate", [I, E], F32, kind="ExternalInput").ap()
    w_down = nc.dram_tensor("w_down", [E, I], F32, kind="ExternalInput").ap()
    ident_in = nc.dram_tensor("ident", [128, 128], F32, kind="ExternalInput").ap()
    out = nc.dram_tensor("out", [T, E], F32, kind="ExternalOutput").ap()

    q_s = nc.dram_tensor("q_s", [B, E, S], F32).ap()
    attn_s = nc.dram_tensor("attn_s", [B, E, S], F32).ap()
    x2_s = nc.dram_tensor("x2_s", [T, E], F32).ap()
    BF16 = mybir.dt.bfloat16
    wupf_h = nc.dram_tensor("wupf_h", [E, I], BF16).ap()
    wupf_l = nc.dram_tensor("wupf_l", [E, I], BF16).ap()
    wgatef_h = nc.dram_tensor("wgatef_h", [E, I], BF16).ap()
    wgatef_l = nc.dram_tensor("wgatef_l", [E, I], BF16).ap()
    wdownf_h = nc.dram_tensor("wdownf_h", [I, E], BF16).ap()
    wdownf_l = nc.dram_tensor("wdownf_l", [I, E], BF16).ap()
    up_s = nc.dram_tensor("up_s", [I, T], F32).ap()
    gate_s = nc.dram_tensor("gate_s", [I, T], F32).ap()
    t_s = nc.dram_tensor("t_s", [I, T], F32).ap()
    t2_s = nc.dram_tensor("t2_s", [I, T], F32).ap()
    o_h_s = nc.dram_tensor("o_h_s", [I, T], BF16).ap()
    o_l_s = nc.dram_tensor("o_l_s", [I, T], BF16).ap()
    cc_in = [nc.dram_tensor(f"cc_in{i}", [1, 32], F32).ap() for i in range(4)]
    cc_out = [nc.dram_tensor(f"cc_out{i}", [1, 32], F32).ap() for i in range(4)]

    ctx = contextlib.ExitStack()
    with ctx:
        _uid = [0]

        def sb(name, shape, c=None, dt=F32):
            _uid[0] += 1
            return (c or ctx).enter_context(
                nc.sbuf_tensor(f"{name}_u{_uid[0]}", shape, dt))

        def psum(name, shape, c=None):
            _uid[0] += 1
            return (c or ctx).enter_context(
                nc.psum_tensor(f"{name}_u{_uid[0]}", shape, F32))

        ident = sb("identt", [128, 128])
        ones1 = sb("ones1", [1, 128])
        btq = sb("btq", [128, ET])
        btk = sb("btk", [128, ET])
        brep = sb("brep", [128, E])
        mrow = sb("mrow", [128, 1])
        nmrow = sb("nmrow", [128, 1])
        vrow = sb("vrow", [128, 1])
        sdrow = sb("sdrow", [128, 1])
        rstd = sb("rstd", [128, 1])
        rsA = sb("rsA", [128, 1])
        tmax = sb("tmax", [128, 1])
        runmax = sb("runmax", [128, 4])
        gmax = sb("gmax", [128, 4])
        junk = sb("junkt", [128, E // 2])
        vrow2 = sb("vrow2", [128, 1])
        tinyA = sb("tinyA", [1, 32])
        tinyB = sb("tinyB", [1, 32])
        tred = sb("tred", [4, 128])
        tred2 = sb("tred2", [4, 1])
        epsrow = sb("epsrow", [128, 1])
        crow = sb("crow", [128, 1])
        ncrow = sb("ncrow", [128, 1])
        scrow = sb("scrow", [128, 8])

        ld = Sem(ctx, nc, "ld")
        lg = Sem(ctx, nc, "lg")
        pe = Sem(ctx, nc, "pe")
        ac = Sem(ctx, nc, "ac")
        ve = Sem(ctx, nc, "ve")
        ccs = Sem(ctx, nc, "ccs")
        pe2 = Sem(ctx, nc, "pe2")
        ac2 = Sem(ctx, nc, "ac2")
        ve2 = Sem(ctx, nc, "ve2")
        pe3 = Sem(ctx, nc, "pe3")
        st = Sem(ctx, nc, "st")
        bar_s = Sem(ctx, nc, "bar_s")
        bar_g = Sem(ctx, nc, "bar_g")
        bar_p = Sem(ctx, nc, "bar_p")
        bar_a = Sem(ctx, nc, "bar_a")
        bar_v = Sem(ctx, nc, "bar_v")
        lgw = [Sem(ctx, nc, f"lgw{i}") for i in range(2)]
        lgq = [Sem(ctx, nc, f"lgq{i}") for i in range(2)]
        lga = [Sem(ctx, nc, f"lga{i}") for i in range(2)]
        lgx = [Sem(ctx, nc, f"lgx{i}") for i in range(2)]
        lgf = [Sem(ctx, nc, f"lgf{i}") for i in range(2)]
        lgc = [Sem(ctx, nc, f"lgc{i}") for i in range(2)]
        lgcc = [Sem(ctx, nc, f"lgcc{i}") for i in range(2)]
        sts = [Sem(ctx, nc, f"sts{i}") for i in range(2)]
        ldps = [Sem(ctx, nc, f"ldp{i}") for i in range(2)]

        block = ctx.enter_context(nc.Block())
        ENGS = ("sync", "gpsimd", "tensor", "scalar", "vector")

        def sec(name, fn):
            getattr(block, name)(fn)

        def gate(e, *sems):
            for s_ in sems:
                if s_.n:
                    e.wait_ge(s_.h, s_.n)

        def barrier():
            def f_sy(e):
                e.wait_ge(ld.h, ld.n)
                for s_ in ldps:
                    e.wait_ge(s_.h, s_.n)
                bar_s.inc(e.nop())

            def f_gp(e):
                for s_ in (lg, *lgw, *lgq, *lga, *lgx, *lgf, *lgc, *lgcc, *sts):
                    if s_.n:
                        e.wait_ge(s_.h, s_.n)
                bar_g.inc(e.nop())
            def f_pe(e):
                e.drain()
                bar_p.inc(e.nop())

            def f_ac(e):
                e.drain()
                bar_a.inc(e.nop())

            def f_ve(e):
                e.drain()
                bar_v.inc(e.nop())
            for en, f in zip(ENGS, (f_sy, f_gp, f_pe, f_ac, f_ve)):
                sec(en, f)
            marks = (bar_s.n, bar_g.n, bar_p.n, bar_a.n, bar_v.n)

            def w(e):
                e.wait_ge(bar_s.h, marks[0]); e.wait_ge(bar_g.h, marks[1])
                e.wait_ge(bar_p.h, marks[2]); e.wait_ge(bar_a.h, marks[3])
                e.wait_ge(bar_v.h, marks[4])
            for en in ENGS:
                sec(en, w)

        def emit_ln(x_ap, xc_ap, wait_ld=None, wait_pe=None):
            def v1(e):
                if wait_ld is not None:
                    e.wait_ge(ld.h, wait_ld)
                if wait_pe is not None:
                    e.wait_ge(pe.h, wait_pe)
                e.drain()
                e.tensor_reduce(out=mrow[:], in_=x_ap, axis=AX, op=OP.add)
                e.drain()
                e.tensor_scalar(out=nmrow[:], in0=mrow[:], scalar1=1.0 / E,
                                scalar2=None, op0=OP.mult)
                e.drain()
                e.scalar_tensor_tensor(out=xc_ap, in0=x_ap, scalar=1.0,
                                       in1=nmrow[:].broadcast_to([128, E]),
                                       op0=OP.mult, op1=OP.subtract)
                for hh in range(2):
                    xh = xc_ap[:, hh * (E // 2):(hh + 1) * (E // 2)]
                    e.drain()
                    e.tensor_tensor(out=junk[:], in0=xh, in1=xh, op=OP.mult)
                    e.drain()
                    e.tensor_reduce(out=(vrow[:] if hh == 0 else vrow2[:]),
                                    in_=junk[:], axis=AX, op=OP.add)
                e.drain()
                ve.inc(e.tensor_tensor(out=vrow[:], in0=vrow[:], in1=vrow2[:],
                                       op=OP.add))
            sec("vector", v1)
            m1 = ve.n

            def s1(e):
                e.wait_ge(ve.h, m1)
                ac.inc(e.activation(out=sdrow[:], in_=vrow[:], func=AF.Sqrt,
                                    scale=1.0 / E, bias=epsrow[:]))
            sec("scalar", s1)
            a1 = ac.n

            def v2(e):
                e.wait_ge(ac.h, a1)
                e.drain()
                ve.inc(e.reciprocal(out=rstd[:], in_=sdrow[:]))
            sec("vector", v2)

        def emit_prt(ps_r, src_ap, n, dst_col):
            m_v0 = ve.n
            m_p0 = pe.n

            def t1(e):
                e.wait_ge(ve.h, m_v0)
                pe.inc(e.transpose(ps_r[0:n, 0:128], src_ap, ident[:]))
            sec("tensor", t1)
            m_a0 = ac.n

            def s1(e):
                e.wait_ge(pe.h, m_p0 + 1)
                ac.inc(e.copy(out=tred[0:n, :], in_=ps_r[0:n, 0:128]))
            sec("scalar", s1)

            def v1(e):
                e.wait_ge(ac.h, m_a0 + 1)
                e.drain()
                ve.inc(e.tensor_reduce(out=tred2[0:n, :], in_=tred[0:n, :],
                                       axis=AX, op=OP.max))
            sec("vector", v1)
            m_v1 = ve.n
            m_p1 = pe.n

            def t2(e):
                e.wait_ge(ve.h, m_v1)
                pe.inc(e.transpose(ps_r[0:1, 128:128 + n], tred2[0:n, :],
                                   ident[0:n, 0:n]))
            sec("tensor", t2)

            def s2(e):
                e.wait_ge(pe.h, m_p1 + 1)
                ac.inc(e.copy(out=tinyA[0:1, dst_col:dst_col + n],
                              in_=ps_r[0:1, 128:128 + n]))
            sec("scalar", s2)

        def emit_allreduce(ps_r, idx, nvals, scale_mults, scrow_cols):
            m_a0 = ac.n

            def g_cc(e):
                gate(e, *lgcc)
                e.wait_ge(ac.h, m_a0)
                lgcc[0].dinc(e.dma_start(out=cc_in[idx], in_=tinyA[:]))
                e.wait_ge(lgcc[0].h, lgcc[0].n)
                ccs.inc(e.collective_compute(
                    "AllReduce", OP.max,
                    replica_groups=[list(range(N_CORES))],
                    ins=[cc_in[idx].opt()], outs=[cc_out[idx].opt()]))
                e.wait_ge(ccs.h, ccs.n)
                lgcc[1].dinc(e.dma_start(out=tinyB[:], in_=cc_out[idx]))
            sec("gpsimd", g_cc)
            m_lg1 = lgcc[1].n

            def v_sc(e):
                e.wait_ge(lgcc[1].h, m_lg1)
                for i in range(nvals):
                    e.drain()
                    e.tensor_scalar(out=tinyB[0:1, 8 + 2 * i:9 + 2 * i],
                                    in0=tinyB[0:1, i:i + 1],
                                    scalar1=scale_mults[i], scalar2=1e-12,
                                    op0=OP.mult, op1=OP.max)
                    e.drain()
                    e.reciprocal(out=tinyB[0:1, 9 + 2 * i:10 + 2 * i],
                                 in_=tinyB[0:1, 8 + 2 * i:9 + 2 * i])
                    if scrow_cols[i][2] != 1.0:
                        e.drain()
                        e.tensor_scalar(out=tinyB[0:1, 9 + 2 * i:10 + 2 * i],
                                        in0=tinyB[0:1, 9 + 2 * i:10 + 2 * i],
                                        scalar1=scrow_cols[i][2], scalar2=None,
                                        op0=OP.mult)
                e.drain()
                ve.inc(e.nop())
            sec("vector", v_sc)
            m_v1 = ve.n
            m_p0 = pe.n

            def t_bc(e):
                e.wait_ge(ve.h, m_v1)
                for i in range(nvals):
                    pe.inc(e.matmul(ps_r[:, i * 2:i * 2 + 1], ones1[:],
                                    tinyB[0:1, 9 + 2 * i:10 + 2 * i],
                                    start=(i == 0), stop=False))
                    pe.inc(e.matmul(ps_r[:, i * 2 + 1:i * 2 + 2], ones1[:],
                                    tinyB[0:1, 8 + 2 * i:9 + 2 * i],
                                    start=False, stop=(i == nvals - 1)))
            sec("tensor", t_bc)
            m_p1 = pe.n

            def s_bc(e):
                e.wait_ge(pe.h, m_p1)
                for i in range(nvals):
                    ac.inc(e.copy(out=scrow[:, scrow_cols[i][0]:scrow_cols[i][0] + 1],
                                  in_=ps_r[:, i * 2:i * 2 + 1]))
                    ac.inc(e.copy(out=scrow[:, scrow_cols[i][1]:scrow_cols[i][1] + 1],
                                  in_=ps_r[:, i * 2 + 1:i * 2 + 2]))
            sec("scalar", s_bc)

        _arn = [0]

        def ar_round(src_ap, n, idx, scale_mults, scrow_cols):
            car = contextlib.ExitStack()
            _arn[0] += 1
            ps_r = psum(f"ps_ar{_arn[0]}", [128, 512], c=car)
            with car:
                emit_prt(ps_r, src_ap, n, 0)
                emit_allreduce(ps_r, idx, n, scale_mults, scrow_cols)
                barrier()

        def pa_sync(e):
            gate(e, ld)
            ld.dinc(e.dma_start(out=ident[:], in_=ident_in))
            ld.dinc(e.dma_start(out=btq[:], in_=bmat["q"]))
            ld.dinc(e.dma_start(out=btk[:], in_=bmat["k"]))
        sec("sync", pa_sync)
        m_lda = ld.n

        def pa_vec(e):
            e.wait_ge(ld.h, m_lda)
            e.memset(ones1[:], 1.0)
            e.memset(runmax[:], 0.0)
            e.memset(tinyA[:], 0.0)
            e.memset(epsrow[:], 1e-5)
            e.memset(crow[:], C_RND)
            e.memset(ncrow[:], -C_RND)
            ve.inc(e.memset(gmax[:], 0.0))
        sec("vector", pa_vec)
        m_vea = ve.n

        barrier()

        def fill_brep(vec_dram, row, ps):
            def s_l(e):
                gate(e, ld)
                ld.dinc(e.dma_start(out=row[0:1, 0:E], in_=vec_dram.unsqueeze(0)))
            sec("sync", s_l)
            m_row_ld = ld.n
            m_p0 = pe.n
            m_a0 = ac.n

            def t_f(e):
                e.wait_ge(ld.h, m_row_ld)
                e.wait_ge(ve.h, ve.n)
                e.wait_ge(ac.h, m_a0)
                for j in range(2):
                    if j == 1:
                        e.wait_ge(ac.h, m_a0 + 1)
                    pe.inc(e.matmul(ps[:], ones1[:],
                                    row[0:1, j * 512:(j + 1) * 512],
                                    start=True, stop=True))
            sec("tensor", t_f)

            def s_f(e):
                for j in range(2):
                    e.wait_ge(pe.h, m_p0 + j + 1)
                    ac.inc(e.copy(out=brep[:, j * 512:(j + 1) * 512], in_=ps[:]))
            sec("scalar", s_f)

        _PRE_P0 = [stop_after == "prep0"]
        c0 = contextlib.ExitStack()
        w_nat = sb("w_nat", [128, I], c=c0)
        w_bm = sb("w_bm", [128, 128], c=c0)
        w_sc = sb("w_sc", [128, 128], c=c0)
        w_fme = sb("w_fme", [128, 256], c=c0)
        w_fmh = sb("w_fmh", [128, 256], c=c0, dt=mybir.dt.bfloat16)
        w_fml = sb("w_fml", [128, 256], c=c0, dt=mybir.dt.bfloat16)
        ps_t = [psum(f"ps_t0{i}", [128, 512], c=c0) for i in range(2)]

        def p0_tile(wsrc, wdst_h, wdst_l, IN, oi):
            nb = IN // 32
            it = IN // 128
            m_pe_prev = pe.n

            def s_load(e):
                gate(e, ld)
                e.wait_ge(pe.h, m_pe_prev)
                ld.dinc(e.dma_start(out=w_nat[:, 0:IN],
                                    in_=wsrc[oi * 128:(oi + 1) * 128, :]))
            sec("sync", s_load)
            m_ld1 = ld.n

            def v_deq(e):
                e.wait_ge(ld.h, m_ld1)
                w3 = w_nat[:, 0:IN].rearrange("p (b k) -> p b k", k=32)
                e.drain()
                e.tensor_reduce(out=w_bm[:, 0:nb], in_=w3, axis=AX, op=OP.max,
                                apply_absolute_value=True)
                e.drain()
                e.tensor_scalar(out=w_bm[:, 0:nb], in0=w_bm[:, 0:nb],
                                scalar1=1.0 / 7.0, scalar2=1e-12,
                                op0=OP.mult, op1=OP.max)
                e.drain()
                e.reciprocal(out=w_sc[:, 0:nb], in_=w_bm[:, 0:nb])
                e.drain()
                rs3 = w_sc[:, 0:nb].unsqueeze(2).broadcast_to([128, nb, 32])
                s3 = w_bm[:, 0:nb].unsqueeze(2).broadcast_to([128, nb, 32])
                e.tensor_tensor(out=w3, in0=w3, in1=rs3, op=OP.mult)
                e.drain()
                e.tensor_scalar(out=w_nat[:, 0:IN], in0=w_nat[:, 0:IN],
                                scalar1=C_RND, scalar2=-C_RND,
                                op0=OP.add, op1=OP.add)
                e.drain()
                ve.inc(e.tensor_tensor(out=w3, in0=w3, in1=s3, op=OP.mult))
            sec("vector", v_deq)
            m_ve1 = ve.n
            m_pe0 = pe.n
            m_ac0 = ac.n
            m_lgw = (lgw[0].n, lgw[1].n)

            def t_tr(e):
                e.wait_ge(ve.h, m_ve1)
                e.wait_ge(ac.h, m_ac0)
                for ii in range(it):
                    if ii >= 2:
                        e.wait_ge(ac.h, m_ac0 + ii - 1)
                    pe.inc(e.transpose(
                        ps_t[ii % 2][:, 0:128],
                        w_nat[:, ii * 128:(ii + 1) * 128], ident[:]))
            sec("tensor", t_tr)

            def s_ev(e):
                for ii in range(it):
                    e.wait_ge(pe.h, m_pe0 + ii + 1)
                    e.wait_ge(lgw[ii % 2].h, m_lgw[ii % 2] + 32 * (ii // 2))
                    ac.inc(e.copy(
                        out=w_fme[:, (ii % 2) * 128:(ii % 2) * 128 + 128],
                        in_=ps_t[ii % 2][:, 0:128]))
            sec("scalar", s_ev)

            m_ve_sp0 = ve.n

            def v_split(e):
                for ii in range(it):
                    e.wait_ge(ac.h, m_ac0 + ii + 1)
                    sl = slice((ii % 2) * 128, (ii % 2) * 128 + 128)
                    e.drain()
                    e.tensor_copy(out=w_fmh[:, sl], in_=w_fme[:, sl])
                    e.drain()
                    ve.inc(e.tensor_tensor(out=w_fml[:, sl], in0=w_fme[:, sl],
                                           in1=w_fmh[:, sl], op=OP.subtract))
            sec("vector", v_split)

            def g_st(e):
                gate(e, *lgw)
                for ii in range(it):
                    e.wait_ge(ve.h, m_ve_sp0 + ii + 1)
                    sl = slice((ii % 2) * 128, (ii % 2) * 128 + 128)
                    lgw[ii % 2].dinc(e.dma_start(
                        out=wdst_h[ii * 128:(ii + 1) * 128, oi * 128:(oi + 1) * 128],
                        in_=w_fmh[:, sl]))
                    lgw[ii % 2].dinc(e.dma_start(
                        out=wdst_l[ii * 128:(ii + 1) * 128, oi * 128:(oi + 1) * 128],
                        in_=w_fml[:, sl]))
            sec("gpsimd", g_st)

        _p0_lim = None
        if stop_after and stop_after.startswith("p0t"):
            _p0_lim = int(stop_after[3:])
            _P0_STOP[0] = True
        with c0:
            if not _PRE_P0[0]:
                _n0 = 0
                for (wsrc, wdst_h, wdst_l, O, IN) in [
                        (w_up, wupf_h, wupf_l, I, E),
                        (w_gate, wgatef_h, wgatef_l, I, E),
                        (w_down, wdownf_h, wdownf_l, E, I)]:
                    for oi in range(O // 128):
                        if _p0_lim is not None and _n0 >= _p0_lim:
                            break
                        p0_tile(wsrc, wdst_h, wdst_l, IN, oi)
                        _n0 += 1
            barrier()
        if stop_after in ("p0", "prep0"):
            _P0_STOP[0] = True

        c_att = contextlib.ExitStack()
        kres = sb("kres", [128, ET * S], c=c_att)
        vres = sb("vres", [128, NKT * E], c=c_att)

        glob_pair = [0]
        glob_kt = [0]
        qp_mark = {}

        def p1_batch(b):
            p1w_lp = {}
            c1 = contextlib.ExitStack()
            wmain = sb("wmain", [128, (ET - KSPL) * E], c=c1)
            wpre = [sb(f"wpre{i}", [128, KSPL * E], c=c1)
                    for i in range(2)]
            xt = sb("xt", [128, (CH // 128) * E], c=c1)
            xlnT = sb("xlnT", [128, ET * CH], c=c1)
            pstage = sb("pstage", [128, 512], c=c1)
            ps_p = [psum(f"ps_p{i}", [128, 512], c=c1) for i in range(4)]
            ps_v = [psum(f"ps_v{i}", [128, 512], c=c1) for i in range(2)]

            brow1 = wmain
            with c1:
                fill_brep(bvec["v"], brow1, ps_p[0])
                m_tr_done = [0]
                for ci in range(NCH):
                    t0 = b * S + ci * CH
                    m_pe_pre = m_tr_done[0]

                    def s_loadx(e, t0=t0, m_pe_pre=m_pe_pre):
                        gate(e, ld)
                        e.wait_ge(pe.h, m_pe_pre)
                        for j in range(CH // 128):
                            ld.dinc(e.dma_start(
                                out=xt[:, j * E:(j + 1) * E],
                                in_=x_in[t0 + j * 128:t0 + (j + 1) * 128, :]))
                    sec("sync", s_loadx)
                    m_ldx = ld.n

                    for j in range(CH // 128):
                        xa = xt[:, j * E:(j + 1) * E]
                        emit_ln(xa, xa, wait_ld=m_ldx)

                        def v_fin(e, xa=xa):
                            e.drain()
                            ve.inc(e.tensor_tensor(
                                out=xa, in0=xa,
                                in1=rstd[:].broadcast_to([128, E]), op=OP.mult))
                        sec("vector", v_fin)
                    m_ln_ve = ve.n

                    m_pe0 = pe.n
                    m_ac1 = ac.n

                    def t_tr(e, m_ac1=m_ac1):
                        e.wait_ge(ve.h, m_ln_ve)
                        e.wait_ge(ac.h, m_ac1)
                        n = 0
                        for j in range(CH // 128):
                            for k in range(ET):
                                if n >= 4:
                                    e.wait_ge(ac.h, m_ac1 + n - 3)
                                pe.inc(e.transpose(
                                    ps_p[n % 4][:, 0:128],
                                    xt[:, j * E + k * 128:j * E + (k + 1) * 128],
                                    ident[:]))
                                n += 1
                    sec("tensor", t_tr)
                    m_tr_done[0] = pe.n

                    def s_trev(e, m_pe0=m_pe0):
                        n = 0
                        for j in range(CH // 128):
                            for k in range(ET):
                                e.wait_ge(pe.h, m_pe0 + n + 1)
                                ac.inc(e.copy(
                                    out=xlnT[:, k * CH + j * 128:k * CH + (j + 1) * 128],
                                    in_=ps_p[n % 4][:, 0:128]))
                                n += 1
                    sec("scalar", s_trev)
                    m_tr_ac = ac.n

                    for pi, proj in enumerate(("q", "k", "v")):
                        n_st = ci * 3 + pi
                        m_pe_wpre = pe.n
                        wpre_cur = wpre[n_st % 2]
                        nxt = ("q", "k", "v")[(pi + 1) % 3] \
                            if n_st + 1 < 3 * NCH else None

                        def s_loadw(e, proj=proj, m_pe_wpre=m_pe_wpre,
                                    n_st=n_st, nxt=nxt):
                            gate(e, ld)
                            gate(e, *ldps)
                            e.wait_ge(pe.h, m_pe_wpre)
                            if n_st not in p1w_lp:
                                for k in range(KSPL):
                                    ldps[n_st % 2].dinc(e.dma_start(
                                        out=wpre[n_st % 2][:, k * E:(k + 1) * E],
                                        in_=wT[proj][k * 128:(k + 1) * 128, :]))
                                p1w_lp[n_st] = ldps[n_st % 2].n
                            for k in range(KSPL, ET):
                                ld.dinc(e.dma_start(
                                    out=wmain[:, (k - KSPL) * E:(k - KSPL + 1) * E],
                                    in_=wT[proj][k * 128:(k + 1) * 128, :]))
                            if nxt is not None:
                                for k in range(KSPL):
                                    ldps[(n_st + 1) % 2].dinc(e.dma_start(
                                        out=wpre[(n_st + 1) % 2][:,
                                                                k * E:(k + 1) * E],
                                        in_=wT[nxt][k * 128:(k + 1) * 128, :]))
                                p1w_lp[n_st + 1] = ldps[(n_st + 1) % 2].n
                        sec("sync", s_loadw)
                        m_lw1 = ld.n
                        m_lp1 = p1w_lp[n_st]
                        lp_sem = ldps[n_st % 2]

                        if proj in ("q", "k"):
                            m_acp = ac.n
                            m_pe1 = pe.n
                            m_lgp = (lgq[0].n, lgq[1].n)

                            def t_proj(e, m_lw1=m_lw1, m_acp=m_acp,
                                       m_lp1=m_lp1, wpre_cur=wpre_cur):
                                e.wait_ge(ld.h, m_lw1)
                                e.wait_ge(lp_sem.h, m_lp1)
                                e.wait_ge(ac.h, m_acp)
                                for o in range(ET):
                                    if o >= 4:
                                        e.wait_ge(ac.h, m_acp + (o - 4) + 1)
                                    for k in range(ET):
                                        wsrc = wpre_cur if k < KSPL else wmain
                                        wb = k * E if k < KSPL \
                                            else (k - KSPL) * E
                                        ins = e.matmul(
                                            ps_p[o % 4][:, 0:CH],
                                            wsrc[:, wb + o * 128:wb + (o + 1) * 128],
                                            xlnT[:, k * CH:(k + 1) * CH],
                                            start=(k == 0), stop=(k == ET - 1))
                                    pe.inc(ins)
                            sec("tensor", t_proj)

                            if proj == "q":
                                def s_evq(e, m_pe1=m_pe1, m_lgp=m_lgp):
                                    for o in range(ET):
                                        e.wait_ge(pe.h, m_pe1 + o + 1)
                                        e.wait_ge(lgq[o % 2].h,
                                                  m_lgp[o % 2] + 16 * (o // 2))
                                        ac.inc(e.activation(
                                            out=pstage[:, (o % 2) * 256:(o % 2) * 256 + CH]
                                            if CH <= 256 else pstage[:],
                                            in_=ps_p[o % 4][:, 0:CH],
                                            func=AF.Identity,
                                            bias=btq[:, o:o + 1]))
                                sec("scalar", s_evq)

                                def g_stq(e, b=b, ci=ci, m_acp=m_acp):
                                    gate(e, *lgq)
                                    for o in range(ET):
                                        e.wait_ge(ac.h, m_acp + o + 1)
                                        lgq[o % 2].dinc(e.dma_start(
                                            out=q_s[b, o * 128:(o + 1) * 128,
                                                    ci * CH:(ci + 1) * CH],
                                            in_=pstage[:, (o % 2) * 256:(o % 2) * 256 + CH]
                                            if CH <= 256 else pstage[:]))
                                sec("gpsimd", g_stq)
                            else:
                                def s_evk(e, ci=ci, m_pe1=m_pe1):
                                    for o in range(ET):
                                        e.wait_ge(pe.h, m_pe1 + o + 1)
                                        ac.inc(e.activation(
                                            out=kres[:, o * S + ci * CH:o * S + (ci + 1) * CH],
                                            in_=ps_p[o % 4][:, 0:CH],
                                            func=AF.Identity,
                                            bias=btk[:, o:o + 1]))
                                sec("scalar", s_evk)
                        else:
                            m_pe1 = pe.n
                            m_vep = ve.n

                            def t_projv(e, m_lw1=m_lw1, m_vep=m_vep,
                                        m_lp1=m_lp1, wpre_cur=wpre_cur):
                                e.wait_ge(ld.h, m_lw1)
                                e.wait_ge(lp_sem.h, m_lp1)
                                e.wait_ge(ve.h, m_vep)
                                n = 0
                                for j in range(CH // 128):
                                    for oh in range(2):
                                        if n >= 2:
                                            e.wait_ge(ve.h, m_vep + n - 1)
                                        for k in range(ET):
                                            wsrc = wpre_cur if k < KSPL \
                                                else wmain
                                            wb = k * E if k < KSPL \
                                                else (k - KSPL) * E
                                            ins = e.matmul(
                                                ps_v[n % 2][:],
                                                xlnT[:, k * CH + j * 128:k * CH + (j + 1) * 128],
                                                wsrc[:, wb + oh * 512:wb + oh * 512 + 512],
                                                start=(k == 0), stop=(k == ET - 1))
                                        pe.inc(ins)
                                        n += 1
                            sec("tensor", t_projv)

                            def v_evv(e, ci=ci, m_pe1=m_pe1):
                                n = 0
                                for j in range(CH // 128):
                                    tt = ci * (CH // 128) + j
                                    for oh in range(2):
                                        e.wait_ge(pe.h, m_pe1 + n + 1)
                                        e.drain()
                                        ve.inc(e.tensor_tensor(
                                            out=vres[:, tt * E + oh * 512:tt * E + oh * 512 + 512],
                                            in0=ps_v[n % 2][:],
                                            in1=brep[:, oh * 512:oh * 512 + 512],
                                            op=OP.add))
                                        n += 1
                            sec("vector", v_evv)
                barrier()

        def p2_batch(b):
            c2 = contextlib.ExitStack()
            e_all = sb("e_all", [128, 3 * NH * QB], c=c2)
            qblk = sb("qblk", [128, ET * QB], c=c2)
            dena = sb("dena", [128, NH * QB // 2], c=c2)
            afm = qblk
            rden = dena[:, 4 * QB:5 * QB]
            ps_p = [psum(f"ps_q{i}", [128, 512], c=c2) for i in range(4)]
            ps_av = [psum(f"ps_av{i}", [128, 1024], c=c2) for i in range(2)]
            atm = dena

            psp_gate = ac.n
            av_gate = ac.n
            with c2:
                qsta_lg = {}
                for qb in range(NQB):
                    def s_loadq(e, qb=qb):
                        gate(e, ld)
                        if (b, qb - 1) in qp_mark:
                            e.wait_ge(pe2.h, qp_mark[(b, qb - 1)])
                            e.wait_ge(lga[(qb - 1) % 2].h, qsta_lg[qb - 1])
                        for o in range(ET):
                            ld.dinc(e.dma_start(
                                out=qblk[:, o * QB:(o + 1) * QB],
                                in_=q_s[b, o * 128:(o + 1) * 128,
                                        qb * QB:(qb + 1) * QB]))
                    sec("sync", s_loadq)
                    m_ldq = ld.n

                    av_q = []
                    for kt in range(NKT):
                        g = glob_kt[0]
                        eoff = (g % 3) * NH * QB

                        def t_scores(e, kt=kt, g=g, m_ldq=m_ldq,
                                     psp_gate=psp_gate):
                            if kt == 0:
                                e.wait_ge(ld.h, m_ldq)
                                e.wait_ge(ac.h, psp_gate)
                            for j in range(8):
                                gpi = glob_pair[0]
                                if gpi >= 2:
                                    e.wait_ge(ac2.h, gpi - 1)
                                for h2 in range(2):
                                    h = 2 * j + h2
                                    ins = e.matmul(
                                        ps_p[h % 4][:, 0:QB],
                                        kres[64 * h2:64 * h2 + 64,
                                             (h // 2) * S + kt * 128:
                                             (h // 2) * S + (kt + 1) * 128],
                                        qblk[64 * h2:64 * h2 + 64,
                                             (h // 2) * QB:(h // 2) * QB + QB],
                                        start=True, stop=True,
                                        tile_position=(64 * h2, 0))
                                pe2.inc(ins)
                                glob_pair[0] += 1
                        sec("tensor", t_scores)

                        if len(av_q) >= 2:
                            sec("tensor", av_q.pop(0))

                        def s_exp(e, g=g, eoff=eoff):
                            if g >= 3:
                                e.wait_ge(pe3.h, g - 2)
                            for j in range(8):
                                e.wait_ge(pe2.h, g * 8 + j + 1)
                                e.activation(
                                    out=e_all[:, eoff + 2 * j * QB:eoff + (2 * j + 1) * QB],
                                    in_=ps_p[(2 * j) % 4][:, 0:QB],
                                    func=AF.Exp, scale=0.125)
                                ac2.inc(e.activation(
                                    out=e_all[:, eoff + (2 * j + 1) * QB:eoff + (2 * j + 2) * QB],
                                    in_=ps_p[(2 * j + 1) % 4][:, 0:QB],
                                    func=AF.Exp, scale=0.125))
                        sec("scalar", s_exp)

                        def v_soft(e, g=g, eoff=eoff):
                            W = NH * QB
                            e.wait_ge(ac2.h, (g + 1) * 8)
                            e.drain()
                            e.tensor_tensor(out=dena[:, 0:W // 2],
                                            in0=e_all[:, eoff:eoff + W // 2],
                                            in1=e_all[:, eoff + W // 2:eoff + W],
                                            op=OP.add)
                            e.drain()
                            e.tensor_tensor(out=dena[:, 0:W // 4],
                                            in0=dena[:, 0:W // 4],
                                            in1=dena[:, W // 4:W // 2], op=OP.add)
                            e.drain()
                            e.tensor_tensor(out=dena[:, 0:W // 8],
                                            in0=dena[:, 0:W // 8],
                                            in1=dena[:, W // 8:W // 4], op=OP.add)
                            e.drain()
                            e.tensor_tensor(out=dena[:, 0:QB], in0=dena[:, 0:QB],
                                            in1=dena[:, QB:2 * QB], op=OP.add)
                            e.drain()
                            e.reciprocal(out=rden, in_=dena[:, 0:QB])
                            e.drain()
                            ea3 = e_all[:, eoff:eoff + W].rearrange("p (h q) -> p h q", h=NH)
                            ve2.inc(e.tensor_tensor(
                                out=ea3, in0=ea3,
                                in1=rden.unsqueeze(1).broadcast_to([128, NH, QB]),
                                op=OP.mult))
                        sec("vector", v_soft)

                        def t_av(e, kt=kt, g=g, eoff=eoff, av_gate=av_gate):
                            e.wait_ge(ve2.h, g + 1)
                            if kt == 0:
                                e.wait_ge(ac.h, av_gate)
                            for h in range(NH):
                                for qh in range(2):
                                    ins = e.matmul(
                                        ps_av[qh][:, h * 64:(h + 1) * 64],
                                        e_all[:, eoff + h * QB + qh * 128:
                                              eoff + h * QB + qh * 128 + 128],
                                        vres[:, kt * E + h * 64:kt * E + (h + 1) * 64],
                                        start=(kt == 0 and h in (0, 8)),
                                        stop=(kt == NKT - 1 and h in (7, 15)))
                            pe3.inc(ins)
                        av_q.append(t_av)
                        glob_kt[0] += 1
                    while av_q:
                        sec("tensor", av_q.pop(0))
                    m_pe3_done = pe3.n
                    m_pe_trpre = pe.n
                    m_ve_soft = ve2.n

                    def s_avev(e):
                        e.wait_ge(pe3.h, m_pe3_done)
                        e.wait_ge(ve2.h, m_ve_soft)
                        e.wait_ge(pe.h, m_pe_trpre)
                        for qh in range(2):
                            e.drain()
                            ac.inc(e.tensor_copy(out=atm[:, qh * E:(qh + 1) * E],
                                                 in_=ps_av[qh][:]))
                    sec("vector", s_avev)
                    av_gate = ac.n

                    m_pe_tr0 = pe.n
                    m_ac_tr0 = ac.n
                    m_lg_pre = lg.n

                    def t_atr(e, av_gate=av_gate, m_ac_tr0=m_ac_tr0):
                        e.wait_ge(ac.h, av_gate)
                        n = 0
                        for qh in range(2):
                            for k in range(ET):
                                if n >= 4:
                                    e.wait_ge(ac.h, m_ac_tr0 + n - 3)
                                pe.inc(e.transpose(
                                    ps_p[n % 4][:, 0:128],
                                    atm[:, qh * E + k * 128:qh * E + (k + 1) * 128],
                                    ident[:]))
                                n += 1
                    sec("tensor", t_atr)

                    def s_atrev(e, m_pe_tr0=m_pe_tr0, m_lg_pre=m_lg_pre):
                        e.wait_ge(lg.h, m_lg_pre)
                        n = 0
                        for qh in range(2):
                            for k in range(ET):
                                e.wait_ge(pe.h, m_pe_tr0 + n + 1)
                                e.drain()
                                ac.inc(e.tensor_copy(
                                    out=afm[:, k * QB + qh * 128:k * QB + (qh + 1) * 128],
                                    in_=ps_p[n % 4][:, 0:128]))
                                n += 1
                    sec("vector", s_atrev)
                    m_ac_fm = ac.n
                    psp_gate = m_ac_fm

                    def g_sta(e, qb=qb, m_ac_fm=m_ac_fm):
                        gate(e, *lga)
                        e.wait_ge(ac.h, m_ac_fm)
                        for k in range(ET):
                            lga[qb % 2].dinc(e.dma_start(
                                out=attn_s[b, k * 128:(k + 1) * 128,
                                           qb * QB:(qb + 1) * QB],
                                in_=afm[:, k * QB:(k + 1) * QB]))
                    sec("gpsimd", g_sta)
                    qsta_lg[qb] = lga[qb % 2].n
                    qp_mark[(b, qb)] = pe2.n
                barrier()

        def p25_batch(b):
            c3 = contextlib.ExitStack()
            wof = sb("wof", [128, ET * E], c=c3)
            aload = sb("aload", [128, 2 * ET * 128], c=c3)
            x2t2 = sb("x2t2", [128, 2 * E], c=c3)
            xres2 = sb("xres2", [128, 2 * E], c=c3)
            xct2 = sb("xct2", [128, 2 * E], c=c3)
            strow = sb("strow", [128, 8], c=c3)
            ps_v = [psum(f"ps_o{i}", [128, 512], c=c3) for i in range(2)]
            brow3 = sb("brow3", [1, E], c=c3)

            with c3:
                fill_brep(bvec["o"], brow3, ps_v[0])
                m_fill_ac = ac.n

                def s_loadwo(e):
                    gate(e, ld)
                    for k in range(ET):
                        ld.dinc(e.dma_start(out=wof[:, k * E:(k + 1) * E],
                                            in_=wT["o"][k * 128:(k + 1) * 128, :]))
                sec("sync", s_loadwo)
                m_lwo1 = ld.n

                pe_done = {}
                ve_head = {}
                ac_sq = {}
                ve_tail = {}
                x2_lg = {}
                NT25 = S // 128

                def tail(tt):
                    p = tt % 2
                    m_sq = ac_sq[tt]

                    def v_tail(e, tt=tt, p=p, m_sq=m_sq):
                        e.wait_ge(ac.h, m_sq)
                        e.drain()
                        e.reciprocal(out=strow[:, 4 + p:5 + p],
                                     in_=strow[:, 2 + p:3 + p])
                        e.drain()
                        e.tensor_tensor(out=strow[:, 6 + p:7 + p],
                                        in0=strow[:, 6 + p:7 + p],
                                        in1=strow[:, 4 + p:5 + p], op=OP.mult)
                        e.drain()
                        ve.inc(e.tensor_tensor(out=runmax[:, 0:1],
                                               in0=runmax[:, 0:1],
                                               in1=strow[:, 6 + p:7 + p],
                                               op=OP.max))
                    sec("vector", v_tail)
                    ve_tail[tt] = ve.n
                    m_vt = ve.n

                    def g_stx2(e, tt=tt, p=p, m_vt=m_vt):
                        gate(e, *lgx)
                        e.wait_ge(ve.h, m_vt)
                        lgx[tt % 2].dinc(e.dma_start(
                            out=x2_s[b * S + tt * 128:b * S + (tt + 1) * 128, :],
                            in_=x2t2[:, p * E:(p + 1) * E]))
                    sec("gpsimd", g_stx2)
                    x2_lg[tt] = lgx[tt % 2].n

                for tt in range(NT25):
                    t0 = b * S + tt * 128
                    buf = tt % 2

                    def s_lda(e, tt=tt, buf=buf, t0=t0):
                        gate(e, ld)
                        if tt >= 2:
                            e.wait_ge(pe.h, pe_done[tt - 2])
                            e.wait_ge(lgx[tt % 2].h, x2_lg[tt - 2])
                            e.wait_ge(ve.h, ve_tail[tt - 2])
                        for k in range(ET):
                            ld.dinc(e.dma_start(
                                out=aload[:, buf * ET * 128 + k * 128:
                                          buf * ET * 128 + (k + 1) * 128],
                                in_=attn_s[b, k * 128:(k + 1) * 128,
                                           tt * 128:(tt + 1) * 128]))
                        ld.dinc(e.dma_start(out=xres2[:, buf * E:(buf + 1) * E],
                                            in_=x_in[t0:t0 + 128, :]))
                    sec("sync", s_lda)
                    m_lda1 = ld.n
                    m_ve_pre = ve_head.get(tt - 1, 0)
                    m_pe0 = pe.n

                    def t_oproj(e, buf=buf, m_lda1=m_lda1, m_ve_pre=m_ve_pre):
                        e.wait_ge(ld.h, max(m_lda1, m_lwo1))
                        e.wait_ge(ve.h, m_ve_pre)
                        e.wait_ge(ac.h, m_fill_ac)
                        for oh in range(2):
                            for k in range(ET):
                                ins = e.matmul(
                                    ps_v[oh][:],
                                    aload[:, buf * ET * 128 + k * 128:
                                          buf * ET * 128 + (k + 1) * 128],
                                    wof[:, k * E + oh * 512:k * E + oh * 512 + 512],
                                    start=(k == 0), stop=(k == ET - 1))
                            pe.inc(ins)
                    sec("tensor", t_oproj)
                    pe_done[tt] = pe.n
                    m_pe1 = pe.n

                    def v_main(e, tt=tt, buf=buf, m_pe1=m_pe1):
                        p = buf
                        x2a = x2t2[:, p * E:(p + 1) * E]
                        xca = xct2[:, p * E:(p + 1) * E]
                        for oh in range(2):
                            e.wait_ge(pe.h, m_pe1 - 1 + oh)
                            e.drain()
                            e.tensor_tensor(out=x2a[:, oh * 512:(oh + 1) * 512],
                                            in0=ps_v[oh][:],
                                            in1=brep[:, oh * 512:(oh + 1) * 512],
                                            op=OP.add)
                            e.drain()
                            e.tensor_tensor(out=x2a[:, oh * 512:(oh + 1) * 512],
                                            in0=xres2[:, p * E + oh * 512:
                                                      p * E + (oh + 1) * 512],
                                            in1=x2a[:, oh * 512:(oh + 1) * 512],
                                            op=OP.add)
                        e.drain()
                        e.tensor_reduce(out=mrow[:], in_=x2a, axis=AX, op=OP.add)
                        e.drain()
                        e.tensor_scalar(out=nmrow[:], in0=mrow[:], scalar1=1.0 / E,
                                        scalar2=None, op0=OP.mult)
                        e.drain()
                        e.scalar_tensor_tensor(out=xca, in0=x2a, scalar=1.0,
                                               in1=nmrow[:].broadcast_to([128, E]),
                                               op0=OP.mult, op1=OP.subtract)
                        for hh in range(2):
                            xh = xca[:, hh * (E // 2):(hh + 1) * (E // 2)]
                            e.drain()
                            e.tensor_tensor(out=junk[:], in0=xh, in1=xh, op=OP.mult)
                            e.drain()
                            e.tensor_reduce(out=(vrow[:] if hh == 0 else vrow2[:]),
                                            in_=junk[:], axis=AX, op=OP.add)
                        e.drain()
                        e.tensor_tensor(out=strow[:, 0 + p:1 + p], in0=vrow[:],
                                        in1=vrow2[:], op=OP.add)
                        e.drain()
                        ve.inc(e.tensor_reduce(out=strow[:, 6 + p:7 + p], in_=xca,
                                               axis=AX, op=OP.max,
                                               apply_absolute_value=True))
                    sec("vector", v_main)
                    ve_head[tt] = ve.n
                    m_vh = ve.n
                    m_ac_pre = ac.n

                    def s_sqrt(e, tt=tt, m_vh=m_vh):
                        p = tt % 2
                        e.wait_ge(ve.h, m_vh)
                        ac.inc(e.activation(out=strow[:, 2 + p:3 + p],
                                            in_=strow[:, 0 + p:1 + p],
                                            func=AF.Sqrt, scale=1.0 / E,
                                            bias=epsrow[:]))
                    sec("scalar", s_sqrt)
                    ac_sq[tt] = ac.n

                    if tt >= 1:
                        tail(tt - 1)
                tail(NT25 - 1)
                barrier()

        def dbg_finish():
            def v_z(e):
                e.drain()
                ve.inc(e.memset(junk[:], 0.0))
            sec("vector", v_z)
            m_vz = ve.n

            def g_z(e):
                e.wait_ge(ve.h, m_vz)
                for tt in range(T // 128):
                    sts[0].dinc(e.dma_start(out=out[tt * 128:(tt + 1) * 128, 0:E // 2],
                                            in_=junk[:]))
                    sts[0].dinc(e.dma_start(out=out[tt * 128:(tt + 1) * 128, E // 2:E],
                                            in_=junk[:]))
                e.wait_ge(sts[0].h, sts[0].n)
            sec("gpsimd", g_z)

        def stop_now(tag):
            return stop_after is not None and tag in stop_after
        if _P0_STOP[0]:
            dbg_finish()
            c_att.close()
            return nc
        if stop_now("pa"):
            dbg_finish()
            c_att.close()
            return nc

        with c_att:
            for b in range(B):
                if stop_now(f"p1b{b}"):
                    break
                p1_batch(b)
                if stop_now(f"p2b{b}"):
                    break
                p2_batch(b)
                if stop_now(f"p25b{b}"):
                    break
                p25_batch(b)
        if stop_after is not None and any(
                stop_after.startswith(p) for p in ("p1b", "p2b", "p25b", "att")):
            dbg_finish()
            return nc

        ar_round(runmax[:, 0:1], 1, 0, [1.0 / 32767.0], [(0, 1, 1.0)])
        if stop_after == "ar1":
            dbg_finish()
            return nc
        if stop_after == "skipffn":
            dbg_finish()
            return nc

        c_f1 = contextlib.ExitStack()
        xq_h = sb("xq_h", [128, ET * T], c=c_f1, dt=mybir.dt.bfloat16)
        xq_l = sb("xq_l", [128, ET * T], c=c_f1, dt=mybir.dt.bfloat16)
        x2b = sb("x2b", [128, E], c=c_f1)
        xcb = sb("xcb", [128, E], c=c_f1)
        zqb = sb("zqb", [128, E], c=c_f1)
        xqtm = sb("xqtm", [128, E], c=c_f1)
        wcol_h = sb("wcol_h", [128, 2 * ET * 128], c=c_f1, dt=mybir.dt.bfloat16)
        wcol_l = sb("wcol_l", [128, 2 * ET * 128], c=c_f1, dt=mybir.dt.bfloat16)
        fstage = sb("fstage", [128, 1024], c=c_f1)
        ps_f = [psum(f"ps_f{i}", [128, 512], c=c_f1) for i in range(2)]
        ps_tq = [psum(f"ps_tq{i}", [128, 512], c=c_f1) for i in range(2)]
        p4_pe_mark = {}
        m_ve_x2 = [0]

        with c_f1:
            for tt in range(T // 128):
                m_pe_pre = pe.n

                def s_ldx2(e, tt=tt, m_pe_pre=m_pe_pre):
                    gate(e, ld)
                    e.wait_ge(pe.h, m_pe_pre)
                    e.wait_ge(ve.h, ve.n)
                    ld.dinc(e.dma_start(out=x2b[:],
                                        in_=x2_s[tt * 128:(tt + 1) * 128, :]))
                sec("sync", s_ldx2)
                m_ld1 = ld.n
                emit_ln(x2b[:], xcb[:], wait_ld=m_ld1)

                def v_rsA(e):
                    e.drain()
                    ve.inc(e.tensor_tensor(out=rsA[:], in0=rstd[:],
                                           in1=scrow[:, 0:1], op=OP.mult))
                sec("vector", v_rsA)
                m_ve1 = ve.n
                m_ac_pre = ac.n

                def s_qdq(e, m_ve1=m_ve1):
                    e.wait_ge(ve.h, m_ve1)
                    e.activation(out=zqb[:], in_=xcb[:], func=AF.Identity,
                                 scale=rsA[:], bias=crow[:])
                    e.drain()
                    e.activation(out=zqb[:], in_=zqb[:], func=AF.Identity,
                                 bias=ncrow[:])
                    e.drain()
                    ac.inc(e.activation(out=xqtm[:], in_=zqb[:],
                                        func=AF.Identity, scale=scrow[:, 1:2]))
                sec("scalar", s_qdq)
                m_ac1 = ac.n
                m_pe0 = pe.n

                def t_trq(e, m_ac1=m_ac1, m_ve_x=ve.n):
                    e.wait_ge(ac.h, m_ac1)
                    e.wait_ge(ve.h, m_ve_x)
                    for k in range(ET):
                        if k >= 2:
                            e.wait_ge(ac.h, m_ac1 + k - 1)
                            e.wait_ge(ve.h, m_ve_x2[0] + k - 1)
                        pe.inc(e.transpose(
                            ps_tq[k % 2][:, 0:128],
                            xqtm[:, k * 128:(k + 1) * 128], ident[:]))
                m_ve_x2[0] = ve.n
                sec("tensor", t_trq)

                def s_trqev(e, tt=tt, m_pe0=m_pe0):
                    for k in range(ET):
                        e.wait_ge(pe.h, m_pe0 + k + 1)
                        ac.inc(e.copy(
                            out=xq_h[:, k * T + tt * 128:k * T + (tt + 1) * 128],
                            in_=ps_tq[k % 2][:, 0:128]))
                sec("scalar", s_trqev)
                m_ac_tr = ac.n - ET

                def v_xql(e, tt=tt, m_ac_tr=m_ac_tr):
                    for k in range(ET):
                        e.wait_ge(ac.h, m_ac_tr + k + 1)
                        e.drain()
                        ve.inc(e.tensor_tensor(
                            out=xq_l[:, k * T + tt * 128:k * T + (tt + 1) * 128],
                            in0=ps_tq[k % 2][:, 0:128],
                            in1=xq_h[:, k * T + tt * 128:k * T + (tt + 1) * 128],
                            op=OP.subtract))
                sec("vector", v_xql)
            barrier()
            P4_LG_BASE = (lgf[0].n, lgf[1].n)
            P4_VE_BASE = ve.n

            for wi, (wfsrc, dst_s, gcol) in enumerate(
                    (((wupf_h, wupf_l), up_s, 0),
                     ((wgatef_h, wgatef_l), gate_s, 1))):
                for it_ in range(IT):
                    wtile = wi * IT + it_
                    wbuf = wtile % 2
                    NTC = T // 512

                    def s_ldw(e, it_=it_, wfsrc=wfsrc, wbuf=wbuf, wtile=wtile):
                        gate(e, ld)
                        if wtile >= 2:
                            e.wait_ge(pe.h, p4_pe_mark[wtile - 2])
                        for k in range(ET):
                            ld.dinc(e.dma_start(
                                out=wcol_h[:, wbuf * ET * 128 + k * 128:
                                           wbuf * ET * 128 + (k + 1) * 128],
                                in_=wfsrc[0][k * 128:(k + 1) * 128,
                                             it_ * 128:(it_ + 1) * 128]))
                            ld.dinc(e.dma_start(
                                out=wcol_l[:, wbuf * ET * 128 + k * 128:
                                           wbuf * ET * 128 + (k + 1) * 128],
                                in_=wfsrc[1][k * 128:(k + 1) * 128,
                                             it_ * 128:(it_ + 1) * 128]))
                    sec("sync", s_ldw)
                    m_ldw = ld.n
                    m_pe0 = pe.n
                    m_ac0 = ac.n

                    def t_mm(e, m_ldw=m_ldw, m_ac0=m_ac0, wbuf=wbuf,
                             wtile=wtile):
                        e.wait_ge(ld.h, m_ldw)
                        e.wait_ge(ac.h, m_ac0)
                        for tc in range(NTC):
                            if tc >= 2:
                                e.wait_ge(ac.h, m_ac0 + tc - 1)
                            npass = 3 * ET
                            ip = 0
                            for k in range(ET):
                                wsl = slice(wbuf * ET * 128 + k * 128,
                                            wbuf * ET * 128 + (k + 1) * 128)
                                xsl = slice(k * T + tc * 512, k * T + (tc + 1) * 512)
                                for (wt, xt_) in ((wcol_h, xq_h), (wcol_h, xq_l),
                                                  (wcol_l, xq_h)):
                                    ins = e.matmul(
                                        ps_f[(wtile * NTC + tc) % 2][:],
                                        wt[:, wsl], xt_[:, xsl],
                                        start=(ip == 0), stop=(ip == npass - 1))
                                    ip += 1
                            pe.inc(ins)
                    sec("tensor", t_mm)
                    p4_pe_mark[wtile] = pe.n

                    def s_fev(e, m_pe0=m_pe0, wtile=wtile):
                        for tc in range(NTC):
                            gtc = wtile * NTC + tc
                            e.wait_ge(pe.h, m_pe0 + tc + 1)
                            e.wait_ge(lgf[gtc % 2].h,
                                      P4_LG_BASE[gtc % 2] + 16 * (gtc // 2))
                            if gtc >= 2:
                                e.wait_ge(ve.h, P4_VE_BASE + gtc - 1)
                            ac.inc(e.copy(
                                out=fstage[:, (gtc % 2) * 512:(gtc % 2) * 512 + 512],
                                in_=ps_f[gtc % 2][:]))
                    sec("scalar", s_fev)

                    def v_fmax(e, m_ac0=m_ac0, gcol=gcol, wtile=wtile):
                        for tc in range(NTC):
                            gtc = wtile * NTC + tc
                            e.wait_ge(ac.h, m_ac0 + tc + 1)
                            e.drain()
                            e.tensor_reduce(
                                out=tmax[:],
                                in_=fstage[:, (gtc % 2) * 512:(gtc % 2) * 512 + 512],
                                axis=AX, op=OP.max, apply_absolute_value=True)
                            e.drain()
                            ve.inc(e.tensor_tensor(out=gmax[:, gcol:gcol + 1],
                                                   in0=gmax[:, gcol:gcol + 1],
                                                   in1=tmax[:], op=OP.max))
                    sec("vector", v_fmax)
                    m_ve0 = ve.n - NTC

                    def g_fst(e, it_=it_, dst_s=dst_s, m_ac0=m_ac0, m_ve0=m_ve0,
                              wtile=wtile):
                        gate(e, *lgf)
                        for tc in range(NTC):
                            gtc = wtile * NTC + tc
                            e.wait_ge(ac.h, m_ac0 + tc + 1)
                            e.wait_ge(ve.h, m_ve0 + tc + 1)
                            lgf[gtc % 2].dinc(e.dma_start(
                                out=dst_s[it_ * 128:(it_ + 1) * 128,
                                          tc * 512:(tc + 1) * 512],
                                in_=fstage[:, (gtc % 2) * 512:(gtc % 2) * 512 + 512]))
                    sec("gpsimd", g_fst)
            barrier()

        ar_round(gmax[:, 0:2], 2, 1, [1.0 / 32767.0, 1.0 / 32767.0],
                 [(2, 3, 1.0), (4, 5, 1.0)])
        if stop_after == "ar2":
            dbg_finish()
            return nc

        c_f2 = contextlib.ExitStack()
        sA = sb("sA", [128, 2 * CW], c=c_f2)
        sB = sb("sB", [128, 2 * CW], c=c_f2)
        sZ = sb("sZ", [128, 2 * CW], c=c_f2)
        sG = sb("sG", [128, 2 * CW], c=c_f2)
        sSG = sb("sSG", [128, 2 * CW], c=c_f2)
        sO = sb("sO", [128, 2 * CW], c=c_f2)
        sOh = sb("sOh", [128, 2 * CW], c=c_f2, dt=mybir.dt.bfloat16)
        sOl = sb("sOl", [128, 2 * CW], c=c_f2, dt=mybir.dt.bfloat16)

        c_ac_mark = {}
        c_ve_mark = {}

        def qdq3_scalar(e, src_ap, dst_ap, rs_col, s_col, tmp_ap):
            e.activation(out=tmp_ap, in_=src_ap, func=AF.Identity,
                         scale=scrow[:, rs_col:rs_col + 1], bias=crow[:])
            e.drain()
            e.activation(out=tmp_ap, in_=tmp_ap, func=AF.Identity, bias=ncrow[:])
            e.drain()
            e.activation(out=dst_ap, in_=tmp_ap, func=AF.Identity,
                         scale=scrow[:, s_col:s_col + 1])

        def c_stream(srcs, dst, fn_scalar, fn_vector, gcol):
            c_ac_mark.clear()
            c_ve_mark.clear()
            LG_BASE = (lgc[0].n, lgc[1].n)
            for ti in range(NFT):
                i0 = (ti // (T // CW)) * 128
                cq = (ti % (T // CW)) * CW
                bufi = ti % 2

                def s_ld(e, i0=i0, cq=cq, bufi=bufi, ti=ti):
                    gate(e, ld)
                    if ti >= 2:
                        e.wait_ge(ve.h, c_ve_mark[ti - 2])
                        e.wait_ge(ac.h, c_ac_mark[ti - 2])
                    ld.dinc(e.dma_start(out=sA[:, bufi * CW:(bufi + 1) * CW],
                                        in_=srcs[0][i0:i0 + 128, cq:cq + CW]))
                    if len(srcs) > 1:
                        ld.dinc(e.dma_start(out=sB[:, bufi * CW:(bufi + 1) * CW],
                                            in_=srcs[1][i0:i0 + 128, cq:cq + CW]))
                sec("sync", s_ld)
                m_ld1 = ld.n

                fn_scalar(m_ld1, bufi, ti, LG_BASE)
                c_ac_mark[ti] = ac.n
                fn_vector(bufi, ti)

                def v_gmax(e, bufi=bufi, gcol=gcol):
                    if gcol is not None:
                        e.drain()
                        e.tensor_reduce(out=tmax[:],
                                        in_=sO[:, bufi * CW:(bufi + 1) * CW],
                                        axis=AX, op=OP.max,
                                        apply_absolute_value=True)
                        e.drain()
                        e.tensor_tensor(out=gmax[:, gcol:gcol + 1],
                                        in0=gmax[:, gcol:gcol + 1],
                                        in1=tmax[:], op=OP.max)
                    e.drain()
                    ve.inc(e.nop())
                sec("vector", v_gmax)
                c_ve_mark[ti] = ve.n
                m_ve2 = ve.n

                def g_st(e, i0=i0, cq=cq, bufi=bufi, m_ve2=m_ve2, ti=ti):
                    gate(e, *lgc)
                    e.wait_ge(ve.h, m_ve2)
                    e.wait_ge(ac.h, c_ac_mark[ti])
                    if isinstance(dst, tuple):
                        lgc[ti % 2].dinc(e.dma_start(
                            out=dst[0][i0:i0 + 128, cq:cq + CW],
                            in_=sOh[:, bufi * CW:(bufi + 1) * CW]))
                        lgc[ti % 2].dinc(e.dma_start(
                            out=dst[1][i0:i0 + 128, cq:cq + CW],
                            in_=sOl[:, bufi * CW:(bufi + 1) * CW]))
                    else:
                        lgc[ti % 2].dinc(e.dma_start(
                            out=dst[i0:i0 + 128, cq:cq + CW],
                            in_=sO[:, bufi * CW:(bufi + 1) * CW]))
                sec("gpsimd", g_st)

        with c_f2:
            def c1_scalar(m_ld1, bufi, ti, LG_BASE):
                def s_f(e):
                    e.wait_ge(ld.h, m_ld1)
                    if ti >= 2:
                        e.wait_ge(ve.h, c_ve_mark[ti - 2])
                    ga = sA[:, bufi * CW:(bufi + 1) * CW]
                    sGb = sG[:, bufi * CW:(bufi + 1) * CW]
                    qdq3_scalar(e, ga, sGb, 4, 5, sZ[:, bufi * CW:(bufi + 1) * CW])
                    e.drain()
                    ac.inc(e.activation(out=sSG[:, bufi * CW:(bufi + 1) * CW],
                                        in_=sGb, func=AF.Sigmoid))
                sec("scalar", s_f)

            def c1_vector(bufi, ti):
                def v_f(e):
                    e.wait_ge(ac.h, c_ac_mark[ti])
                    if ti >= 2:
                        e.wait_ge(lgc[ti % 2].h,
                                  c1_lgbase[ti % 2] + 16 * (ti // 2))
                    sGb = sG[:, bufi * CW:(bufi + 1) * CW]
                    sSb = sSG[:, bufi * CW:(bufi + 1) * CW]
                    e.drain()
                    e.tensor_scalar(out=sSb, in0=sSb, scalar1=65536.0,
                                    scalar2=C_RND, op0=OP.mult, op1=OP.add)
                    e.drain()
                    e.tensor_scalar(out=sSb, in0=sSb, scalar1=-C_RND,
                                    scalar2=65535.0, op0=OP.add, op1=OP.min)
                    e.drain()
                    e.tensor_tensor(out=sO[:, bufi * CW:(bufi + 1) * CW],
                                    in0=sGb, in1=sSb, op=OP.mult)
                sec("vector", v_f)

            c1_lgbase = (lgc[0].n, lgc[1].n)
            c_stream([gate_s], t_s, c1_scalar, c1_vector, 2)
            barrier()
            ar_round(gmax[:, 2:3], 1, 2, [2.0 ** -16 / 32767.0],
                     [(6, 7, 2.0 ** -16)])
            if stop_after == "c1":
                dbg_finish()
                return nc

            def c2_scalar(m_ld1, bufi, ti, LG_BASE):
                def s_f(e):
                    e.wait_ge(ld.h, m_ld1)
                    if ti >= 2:
                        e.wait_ge(ve.h, c_ve_mark[ti - 2])
                    ta = sA[:, bufi * CW:(bufi + 1) * CW]
                    qdq3_scalar(e, ta, sG[:, bufi * CW:(bufi + 1) * CW], 6, 7,
                                sZ[:, bufi * CW:(bufi + 1) * CW])
                    ac.inc(e.drain())
                sec("scalar", s_f)

            def c2_vector(bufi, ti):
                def v_f(e):
                    e.wait_ge(ac.h, c_ac_mark[ti])
                    if ti >= 2:
                        e.wait_ge(lgc[ti % 2].h,
                                  c2_lgbase[ti % 2] + 16 * (ti // 2))
                    ua = sB[:, bufi * CW:(bufi + 1) * CW]
                    sSb = sSG[:, bufi * CW:(bufi + 1) * CW]
                    e.drain()
                    e.scalar_tensor_tensor(
                        out=sSb, in0=ua, scalar=1.0,
                        in1=scrow[:, 2:3].broadcast_to([128, CW]),
                        op0=OP.mult, op1=OP.mult)
                    e.drain()
                    e.tensor_scalar(out=sSb, in0=sSb, scalar1=C_RND,
                                    scalar2=-C_RND, op0=OP.add, op1=OP.add)
                    e.drain()
                    e.scalar_tensor_tensor(
                        out=sSb, in0=sSb, scalar=1.0,
                        in1=scrow[:, 3:4].broadcast_to([128, CW]),
                        op0=OP.mult, op1=OP.mult)
                    e.drain()
                    e.tensor_tensor(out=sO[:, bufi * CW:(bufi + 1) * CW],
                                    in0=sG[:, bufi * CW:(bufi + 1) * CW],
                                    in1=sSb, op=OP.mult)
                sec("vector", v_f)

            c2_lgbase = (lgc[0].n, lgc[1].n)
            c_stream([t_s, up_s], t2_s, c2_scalar, c2_vector, 3)
            barrier()
            ar_round(gmax[:, 3:4], 1, 3, [1.0 / 32767.0], [(2, 3, 1.0)])

            def c3_scalar(m_ld1, bufi, ti, LG_BASE):
                def s_f(e):
                    e.wait_ge(ld.h, m_ld1)
                    if ti >= 2:
                        e.wait_ge(lgc[ti % 2].h,
                                  LG_BASE[ti % 2] + 32 * (ti // 2))
                    ta = sA[:, bufi * CW:(bufi + 1) * CW]
                    qdq3_scalar(e, ta, sO[:, bufi * CW:(bufi + 1) * CW], 2, 3,
                                sZ[:, bufi * CW:(bufi + 1) * CW])
                    ac.inc(e.drain())
                sec("scalar", s_f)

            def c3_vector(bufi, ti):
                def v_f(e):
                    e.wait_ge(ac.h, c_ac_mark[ti])
                    sl = slice(bufi * CW, (bufi + 1) * CW)
                    e.drain()
                    e.tensor_copy(out=sOh[:, sl], in_=sO[:, sl])
                    e.drain()
                    e.tensor_tensor(out=sOl[:, sl], in0=sO[:, sl],
                                    in1=sOh[:, sl], op=OP.subtract)
                sec("vector", v_f)

            c_stream([t2_s], (o_h_s, o_l_s), c3_scalar, c3_vector, None)
            barrier()

        c_f3 = contextlib.ExitStack()
        wdfull_h = sb("wdfull_h", [128, IT * E], c=c_f3, dt=mybir.dt.bfloat16)
        wdfull_l = sb("wdfull_l", [128, IT * E], c=c_f3, dt=mybir.dt.bfloat16)
        ocol_h = sb("ocol_h", [128, 2 * IT * 128], c=c_f3, dt=mybir.dt.bfloat16)
        ocol_l = sb("ocol_l", [128, 2 * IT * 128], c=c_f3, dt=mybir.dt.bfloat16)
        x2c3 = sb("x2c3", [128, 2 * E], c=c_f3)
        ostage = sb("ostage", [128, 2 * E], c=c_f3)
        ps_d = [psum(f"ps_d{i}", [128, 512], c=c_f3) for i in range(2)]

        with c_f3:
            def s_ldwd(e):
                gate(e, ld)
                for k in range(IT):
                    ld.dinc(e.dma_start(out=wdfull_h[:, k * E:(k + 1) * E],
                                        in_=wdownf_h[k * 128:(k + 1) * 128, :]))
                    ld.dinc(e.dma_start(out=wdfull_l[:, k * E:(k + 1) * E],
                                        in_=wdownf_l[k * 128:(k + 1) * 128, :]))
            sec("sync", s_ldwd)
            m_lwd = ld.n

            d_pe_mark = {}
            d_ve_mark = {}
            d_st_mark = {}
            for tt in range(T // 128):
                bufi = tt % 2

                def s_ldo(e, tt=tt, bufi=bufi):
                    gate(e, ld)
                    if tt >= 2:
                        e.wait_ge(pe.h, d_pe_mark[tt - 2])
                        e.wait_ge(ve.h, d_ve_mark[tt - 2])
                    for k in range(IT):
                        ld.dinc(e.dma_start(
                            out=ocol_h[:, bufi * IT * 128 + k * 128:
                                       bufi * IT * 128 + (k + 1) * 128],
                            in_=o_h_s[k * 128:(k + 1) * 128, tt * 128:(tt + 1) * 128]))
                        ld.dinc(e.dma_start(
                            out=ocol_l[:, bufi * IT * 128 + k * 128:
                                       bufi * IT * 128 + (k + 1) * 128],
                            in_=o_l_s[k * 128:(k + 1) * 128, tt * 128:(tt + 1) * 128]))
                    ld.dinc(e.dma_start(out=x2c3[:, bufi * E:(bufi + 1) * E],
                                        in_=x2_s[tt * 128:(tt + 1) * 128, :]))
                sec("sync", s_ldo)
                m_ldo = ld.n
                m_ve_pre = ve.n
                m_pe0 = pe.n

                def t_down(e, bufi=bufi, m_ldo=m_ldo, m_ve_pre=m_ve_pre):
                    e.wait_ge(ld.h, max(m_ldo, m_lwd))
                    e.wait_ge(ve.h, m_ve_pre)
                    for oh in range(2):
                        npass = 3 * IT
                        ip = 0
                        for k in range(IT):
                            osl = slice(bufi * IT * 128 + k * 128,
                                        bufi * IT * 128 + (k + 1) * 128)
                            wsl = slice(k * E + oh * 512, k * E + oh * 512 + 512)
                            for (ot, wt) in ((ocol_h, wdfull_h), (ocol_h, wdfull_l),
                                             (ocol_l, wdfull_h)):
                                ins = e.matmul(
                                    ps_d[oh][:], ot[:, osl], wt[:, wsl],
                                    start=(ip == 0), stop=(ip == npass - 1))
                                ip += 1
                        pe.inc(ins)
                sec("tensor", t_down)
                d_pe_mark[tt] = pe.n
                m_pe1 = pe.n

                def v_out(e, bufi=bufi, m_pe1=m_pe1, tt=tt):
                    if tt >= 2:
                        e.wait_ge(sts[tt % 2].h, d_st_mark[tt - 2])
                    for oh in range(2):
                        e.wait_ge(pe.h, m_pe1 - 1 + oh)
                        e.drain()
                        e.tensor_tensor(
                            out=ostage[:, bufi * E + oh * 512:bufi * E + oh * 512 + 512],
                            in0=ps_d[oh][:],
                            in1=x2c3[:, bufi * E + oh * 512:bufi * E + oh * 512 + 512],
                            op=OP.add)
                    e.drain()
                    ve.inc(e.nop())
                sec("vector", v_out)
                d_ve_mark[tt] = ve.n
                m_ve1 = ve.n

                def g_out(e, tt=tt, bufi=bufi, m_ve1=m_ve1):
                    gate(e, *sts)
                    e.wait_ge(ve.h, m_ve1)
                    sts[tt % 2].dinc(e.dma_start(
                        out=out[tt * 128:(tt + 1) * 128, :],
                        in_=ostage[:, bufi * E:(bufi + 1) * E]))
                sec("gpsimd", g_out)
                d_st_mark[tt] = sts[tt % 2].n

            m_st0 = sts[0].n
            m_st1 = sts[1].n

            def g_fin(e):
                e.wait_ge(sts[0].h, m_st0)
                e.wait_ge(sts[1].h, m_st1)
            sec("gpsimd", g_fin)

    return nc


def kernel(L0_attn_x, L0_attn_Wq, L0_attn_bq, L0_attn_Wk, L0_attn_bk,
           L0_attn_Wv, L0_attn_bv, L0_attn_Wo, L0_attn_bo,
           L0_ffn_w_gate, L0_ffn_w_up, L0_ffn_w_down):
    x = np.ascontiguousarray(np.asarray(L0_attn_x, dtype=np.float32))
    Bfull, S, E = x.shape
    per = Bfull // N_CORES

    common = {
        "wqT": np.ascontiguousarray(np.asarray(L0_attn_Wq, np.float32).T),
        "wkT": np.ascontiguousarray(np.asarray(L0_attn_Wk, np.float32).T),
        "wvT": np.ascontiguousarray(np.asarray(L0_attn_Wv, np.float32).T),
        "woT": np.ascontiguousarray(np.asarray(L0_attn_Wo, np.float32).T),
        "bmq": np.ascontiguousarray(
            np.asarray(L0_attn_bq, np.float32).reshape(-1, 128).T),
        "bmk": np.ascontiguousarray(
            np.asarray(L0_attn_bk, np.float32).reshape(-1, 128).T),
        "bv": np.ascontiguousarray(np.asarray(L0_attn_bv, np.float32)),
        "bo": np.ascontiguousarray(np.asarray(L0_attn_bo, np.float32)),
        "w_up": np.ascontiguousarray(np.asarray(L0_ffn_w_up, np.float32)),
        "w_gate": np.ascontiguousarray(np.asarray(L0_ffn_w_gate, np.float32)),
        "w_down": np.ascontiguousarray(np.asarray(L0_ffn_w_down, np.float32)),
        "ident": np.eye(128, dtype=np.float32),
    }
    in_maps = []
    for c in range(N_CORES):
        m = dict(common)
        m["x"] = np.ascontiguousarray(x[c * per:(c + 1) * per].reshape(per * S, E))
        in_maps.append(m)

    nc = build_program(S=S, B=per)
    res = run_bass_kernel_spmd(nc, in_maps, core_ids=list(range(N_CORES)))
    outs = [np.asarray(res.results[c]["out"]).reshape(per, S, E)
            for c in range(N_CORES)]
    return np.concatenate(outs, axis=0)


# version 17 (final)
# speedup vs baseline: 1.0191x; 1.0191x over previous
import contextlib
import numpy as np
import concourse.bass as bass
import concourse.mybir as mybir
from concourse.bass_utils import run_bass_kernel_spmd

F32 = mybir.dt.float32
AF = mybir.ActivationFunctionType
OP = mybir.AluOpType
AX = mybir.AxisListType.X

C_RND = 12582912.0
N_CORES = 8


class Sem:
    def __init__(self, ctx, nc, name):
        self.h = ctx.enter_context(nc.semaphore(name))
        self.n = 0

    def inc(self, instr, k=1):
        instr.then_inc(self.h, k)
        self.n += k
        return instr

    def dinc(self, instr):
        return self.inc(instr, 16)


def build_program(S=2048, B=2, stop_after=None):
    _P0_STOP = [False]
    E, I, NH = 1024, 3072, 16
    T = B * S
    QB = 256
    KSPL = 5
    NQB = S // QB
    NKT = S // 128
    CH = min(256, S)
    NCH = S // CH
    ET = E // 128
    IT = I // 128
    CW = min(2048, T)
    NFT = IT * (T // CW)

    nc = bass.Bass("TRN2", target_bir_lowering=False)

    x_in = nc.dram_tensor("x", [T, E], F32, kind="ExternalInput").ap()
    wT = {p: nc.dram_tensor(f"w{p}T", [E, E], F32, kind="ExternalInput").ap()
          for p in ("q", "k", "v", "o")}
    bvec = {p: nc.dram_tensor(f"b{p}", [E], F32, kind="ExternalInput").ap()
            for p in ("v", "o")}
    bmat = {p: nc.dram_tensor(f"bm{p}", [128, E // 128], F32,
                              kind="ExternalInput").ap()
            for p in ("q", "k")}
    w_up = nc.dram_tensor("w_up", [I, E], F32, kind="ExternalInput").ap()
    w_gate = nc.dram_tensor("w_gate", [I, E], F32, kind="ExternalInput").ap()
    w_down = nc.dram_tensor("w_down", [E, I], F32, kind="ExternalInput").ap()
    ident_in = nc.dram_tensor("ident", [128, 128], F32, kind="ExternalInput").ap()
    out = nc.dram_tensor("out", [T, E], F32, kind="ExternalOutput").ap()

    q_s = nc.dram_tensor("q_s", [B, E, S], F32).ap()
    attn_s = nc.dram_tensor("attn_s", [B, E, S], F32).ap()
    x2_s = nc.dram_tensor("x2_s", [T, E], F32).ap()
    BF16 = mybir.dt.bfloat16
    wupf_h = nc.dram_tensor("wupf_h", [E, I], BF16).ap()
    wupf_l = nc.dram_tensor("wupf_l", [E, I], BF16).ap()
    wgatef_h = nc.dram_tensor("wgatef_h", [E, I], BF16).ap()
    wgatef_l = nc.dram_tensor("wgatef_l", [E, I], BF16).ap()
    wdownf_h = nc.dram_tensor("wdownf_h", [I, E], BF16).ap()
    wdownf_l = nc.dram_tensor("wdownf_l", [I, E], BF16).ap()
    up_s = nc.dram_tensor("up_s", [I, T], F32).ap()
    gate_s = nc.dram_tensor("gate_s", [I, T], F32).ap()
    t_s = nc.dram_tensor("t_s", [I, T], F32).ap()
    t2_s = nc.dram_tensor("t2_s", [I, T], F32).ap()
    o_h_s = nc.dram_tensor("o_h_s", [I, T], BF16).ap()
    o_l_s = nc.dram_tensor("o_l_s", [I, T], BF16).ap()
    cc_in = [nc.dram_tensor(f"cc_in{i}", [1, 32], F32).ap() for i in range(4)]
    cc_out = [nc.dram_tensor(f"cc_out{i}", [1, 32], F32).ap() for i in range(4)]

    ctx = contextlib.ExitStack()
    with ctx:
        _uid = [0]

        def sb(name, shape, c=None, dt=F32):
            _uid[0] += 1
            return (c or ctx).enter_context(
                nc.sbuf_tensor(f"{name}_u{_uid[0]}", shape, dt))

        def psum(name, shape, c=None):
            _uid[0] += 1
            return (c or ctx).enter_context(
                nc.psum_tensor(f"{name}_u{_uid[0]}", shape, F32))

        ident = sb("identt", [128, 128])
        ones1 = sb("ones1", [1, 128])
        btq = sb("btq", [128, ET])
        btk = sb("btk", [128, ET])
        brep = sb("brep", [128, E])
        mrow = sb("mrow", [128, 1])
        nmrow = sb("nmrow", [128, 1])
        vrow = sb("vrow", [128, 1])
        sdrow = sb("sdrow", [128, 1])
        rstd = sb("rstd", [128, 1])
        rsA = sb("rsA", [128, 1])
        tmax = sb("tmax", [128, 1])
        runmax = sb("runmax", [128, 4])
        gmax = sb("gmax", [128, 4])
        junk = sb("junkt", [128, E // 2])
        vrow2 = sb("vrow2", [128, 1])
        tinyA = sb("tinyA", [1, 32])
        tinyB = sb("tinyB", [1, 32])
        tred = sb("tred", [4, 128])
        tred2 = sb("tred2", [4, 1])
        epsrow = sb("epsrow", [128, 1])
        crow = sb("crow", [128, 1])
        ncrow = sb("ncrow", [128, 1])
        scrow = sb("scrow", [128, 8])

        ld = Sem(ctx, nc, "ld")
        lg = Sem(ctx, nc, "lg")
        pe = Sem(ctx, nc, "pe")
        ac = Sem(ctx, nc, "ac")
        ve = Sem(ctx, nc, "ve")
        ccs = Sem(ctx, nc, "ccs")
        pe2 = Sem(ctx, nc, "pe2")
        ac2 = Sem(ctx, nc, "ac2")
        ve2 = Sem(ctx, nc, "ve2")
        pe3 = Sem(ctx, nc, "pe3")
        st = Sem(ctx, nc, "st")
        bar_s = Sem(ctx, nc, "bar_s")
        bar_g = Sem(ctx, nc, "bar_g")
        bar_p = Sem(ctx, nc, "bar_p")
        bar_a = Sem(ctx, nc, "bar_a")
        bar_v = Sem(ctx, nc, "bar_v")
        lgw = [Sem(ctx, nc, f"lgw{i}") for i in range(2)]
        lgq = [Sem(ctx, nc, f"lgq{i}") for i in range(2)]
        lga = [Sem(ctx, nc, f"lga{i}") for i in range(2)]
        lgx = [Sem(ctx, nc, f"lgx{i}") for i in range(2)]
        lgf = [Sem(ctx, nc, f"lgf{i}") for i in range(2)]
        lgc = [Sem(ctx, nc, f"lgc{i}") for i in range(2)]
        lgcc = [Sem(ctx, nc, f"lgcc{i}") for i in range(2)]
        sts = [Sem(ctx, nc, f"sts{i}") for i in range(2)]
        ldps = [Sem(ctx, nc, f"ldp{i}") for i in range(2)]

        block = ctx.enter_context(nc.Block())
        ENGS = ("sync", "gpsimd", "tensor", "scalar", "vector")

        def sec(name, fn):
            getattr(block, name)(fn)

        def gate(e, *sems):
            for s_ in sems:
                if s_.n:
                    e.wait_ge(s_.h, s_.n)

        def barrier():
            def f_sy(e):
                e.wait_ge(ld.h, ld.n)
                for s_ in ldps:
                    e.wait_ge(s_.h, s_.n)
                bar_s.inc(e.nop())

            def f_gp(e):
                for s_ in (lg, *lgw, *lgq, *lga, *lgx, *lgf, *lgc, *lgcc, *sts):
                    if s_.n:
                        e.wait_ge(s_.h, s_.n)
                bar_g.inc(e.nop())
            def f_pe(e):
                e.drain()
                bar_p.inc(e.nop())

            def f_ac(e):
                e.drain()
                bar_a.inc(e.nop())

            def f_ve(e):
                e.drain()
                bar_v.inc(e.nop())
            for en, f in zip(ENGS, (f_sy, f_gp, f_pe, f_ac, f_ve)):
                sec(en, f)
            marks = (bar_s.n, bar_g.n, bar_p.n, bar_a.n, bar_v.n)

            def w(e):
                e.wait_ge(bar_s.h, marks[0]); e.wait_ge(bar_g.h, marks[1])
                e.wait_ge(bar_p.h, marks[2]); e.wait_ge(bar_a.h, marks[3])
                e.wait_ge(bar_v.h, marks[4])
            for en in ENGS:
                sec(en, w)

        def emit_ln(x_ap, xc_ap, wait_ld=None, wait_pe=None):
            def v1(e):
                if wait_ld is not None:
                    e.wait_ge(ld.h, wait_ld)
                if wait_pe is not None:
                    e.wait_ge(pe.h, wait_pe)
                e.drain()
                e.tensor_reduce(out=mrow[:], in_=x_ap, axis=AX, op=OP.add)
                e.drain()
                e.tensor_scalar(out=nmrow[:], in0=mrow[:], scalar1=1.0 / E,
                                scalar2=None, op0=OP.mult)
                e.drain()
                e.scalar_tensor_tensor(out=xc_ap, in0=x_ap, scalar=1.0,
                                       in1=nmrow[:].broadcast_to([128, E]),
                                       op0=OP.mult, op1=OP.subtract)
                for hh in range(2):
                    xh = xc_ap[:, hh * (E // 2):(hh + 1) * (E // 2)]
                    e.drain()
                    e.tensor_tensor(out=junk[:], in0=xh, in1=xh, op=OP.mult)
                    e.drain()
                    e.tensor_reduce(out=(vrow[:] if hh == 0 else vrow2[:]),
                                    in_=junk[:], axis=AX, op=OP.add)
                e.drain()
                ve.inc(e.tensor_tensor(out=vrow[:], in0=vrow[:], in1=vrow2[:],
                                       op=OP.add))
            sec("vector", v1)
            m1 = ve.n

            def s1(e):
                e.wait_ge(ve.h, m1)
                ac.inc(e.activation(out=sdrow[:], in_=vrow[:], func=AF.Sqrt,
                                    scale=1.0 / E, bias=epsrow[:]))
            sec("scalar", s1)
            a1 = ac.n

            def v2(e):
                e.wait_ge(ac.h, a1)
                e.drain()
                ve.inc(e.reciprocal(out=rstd[:], in_=sdrow[:]))
            sec("vector", v2)

        def emit_prt(ps_r, src_ap, n, dst_col):
            m_v0 = ve.n
            m_p0 = pe.n

            def t1(e):
                e.wait_ge(ve.h, m_v0)
                pe.inc(e.transpose(ps_r[0:n, 0:128], src_ap, ident[:]))
            sec("tensor", t1)
            m_a0 = ac.n

            def s1(e):
                e.wait_ge(pe.h, m_p0 + 1)
                ac.inc(e.copy(out=tred[0:n, :], in_=ps_r[0:n, 0:128]))
            sec("scalar", s1)

            def v1(e):
                e.wait_ge(ac.h, m_a0 + 1)
                e.drain()
                ve.inc(e.tensor_reduce(out=tred2[0:n, :], in_=tred[0:n, :],
                                       axis=AX, op=OP.max))
            sec("vector", v1)
            m_v1 = ve.n
            m_p1 = pe.n

            def t2(e):
                e.wait_ge(ve.h, m_v1)
                pe.inc(e.transpose(ps_r[0:1, 128:128 + n], tred2[0:n, :],
                                   ident[0:n, 0:n]))
            sec("tensor", t2)

            def s2(e):
                e.wait_ge(pe.h, m_p1 + 1)
                ac.inc(e.copy(out=tinyA[0:1, dst_col:dst_col + n],
                              in_=ps_r[0:1, 128:128 + n]))
            sec("scalar", s2)

        def emit_allreduce(ps_r, idx, nvals, scale_mults, scrow_cols):
            m_a0 = ac.n

            def g_cc(e):
                gate(e, *lgcc)
                e.wait_ge(ac.h, m_a0)
                lgcc[0].dinc(e.dma_start(out=cc_in[idx], in_=tinyA[:]))
                e.wait_ge(lgcc[0].h, lgcc[0].n)
                ccs.inc(e.collective_compute(
                    "AllReduce", OP.max,
                    replica_groups=[list(range(N_CORES))],
                    ins=[cc_in[idx].opt()], outs=[cc_out[idx].opt()]))
                e.wait_ge(ccs.h, ccs.n)
                lgcc[1].dinc(e.dma_start(out=tinyB[:], in_=cc_out[idx]))
            sec("gpsimd", g_cc)
            m_lg1 = lgcc[1].n

            def v_sc(e):
                e.wait_ge(lgcc[1].h, m_lg1)
                for i in range(nvals):
                    e.drain()
                    e.tensor_scalar(out=tinyB[0:1, 8 + 2 * i:9 + 2 * i],
                                    in0=tinyB[0:1, i:i + 1],
                                    scalar1=scale_mults[i], scalar2=1e-12,
                                    op0=OP.mult, op1=OP.max)
                    e.drain()
                    e.reciprocal(out=tinyB[0:1, 9 + 2 * i:10 + 2 * i],
                                 in_=tinyB[0:1, 8 + 2 * i:9 + 2 * i])
                    if scrow_cols[i][2] != 1.0:
                        e.drain()
                        e.tensor_scalar(out=tinyB[0:1, 9 + 2 * i:10 + 2 * i],
                                        in0=tinyB[0:1, 9 + 2 * i:10 + 2 * i],
                                        scalar1=scrow_cols[i][2], scalar2=None,
                                        op0=OP.mult)
                e.drain()
                ve.inc(e.nop())
            sec("vector", v_sc)
            m_v1 = ve.n
            m_p0 = pe.n

            def t_bc(e):
                e.wait_ge(ve.h, m_v1)
                for i in range(nvals):
                    pe.inc(e.matmul(ps_r[:, i * 2:i * 2 + 1], ones1[:],
                                    tinyB[0:1, 9 + 2 * i:10 + 2 * i],
                                    start=(i == 0), stop=False))
                    pe.inc(e.matmul(ps_r[:, i * 2 + 1:i * 2 + 2], ones1[:],
                                    tinyB[0:1, 8 + 2 * i:9 + 2 * i],
                                    start=False, stop=(i == nvals - 1)))
            sec("tensor", t_bc)
            m_p1 = pe.n

            def s_bc(e):
                e.wait_ge(pe.h, m_p1)
                for i in range(nvals):
                    ac.inc(e.copy(out=scrow[:, scrow_cols[i][0]:scrow_cols[i][0] + 1],
                                  in_=ps_r[:, i * 2:i * 2 + 1]))
                    ac.inc(e.copy(out=scrow[:, scrow_cols[i][1]:scrow_cols[i][1] + 1],
                                  in_=ps_r[:, i * 2 + 1:i * 2 + 2]))
            sec("scalar", s_bc)

        _arn = [0]

        def ar_round(src_ap, n, idx, scale_mults, scrow_cols):
            car = contextlib.ExitStack()
            _arn[0] += 1
            ps_r = psum(f"ps_ar{_arn[0]}", [128, 512], c=car)
            with car:
                emit_prt(ps_r, src_ap, n, 0)
                emit_allreduce(ps_r, idx, n, scale_mults, scrow_cols)
                barrier()

        def pa_sync(e):
            gate(e, ld)
            ld.dinc(e.dma_start(out=ident[:], in_=ident_in))
            ld.dinc(e.dma_start(out=btq[:], in_=bmat["q"]))
            ld.dinc(e.dma_start(out=btk[:], in_=bmat["k"]))
        sec("sync", pa_sync)
        m_lda = ld.n

        def pa_vec(e):
            e.wait_ge(ld.h, m_lda)
            e.memset(ones1[:], 1.0)
            e.memset(runmax[:], 0.0)
            e.memset(tinyA[:], 0.0)
            e.memset(epsrow[:], 1e-5)
            e.memset(crow[:], C_RND)
            e.memset(ncrow[:], -C_RND)
            ve.inc(e.memset(gmax[:], 0.0))
        sec("vector", pa_vec)
        m_vea = ve.n

        barrier()

        def fill_brep(vec_dram, row, ps):
            def s_l(e):
                gate(e, ld)
                ld.dinc(e.dma_start(out=row[0:1, 0:E], in_=vec_dram.unsqueeze(0)))
            sec("sync", s_l)
            m_row_ld = ld.n
            m_p0 = pe.n
            m_a0 = ac.n

            def t_f(e):
                e.wait_ge(ld.h, m_row_ld)
                e.wait_ge(ve.h, ve.n)
                e.wait_ge(ac.h, m_a0)
                for j in range(2):
                    if j == 1:
                        e.wait_ge(ac.h, m_a0 + 1)
                    pe.inc(e.matmul(ps[:], ones1[:],
                                    row[0:1, j * 512:(j + 1) * 512],
                                    start=True, stop=True))
            sec("tensor", t_f)

            def s_f(e):
                for j in range(2):
                    e.wait_ge(pe.h, m_p0 + j + 1)
                    ac.inc(e.copy(out=brep[:, j * 512:(j + 1) * 512], in_=ps[:]))
            sec("scalar", s_f)

        _PRE_P0 = [stop_after == "prep0"]
        c0 = contextlib.ExitStack()
        w_nat = sb("w_nat", [128, I], c=c0)
        w_bm = sb("w_bm", [128, 128], c=c0)
        w_sc = sb("w_sc", [128, 128], c=c0)
        w_fme = sb("w_fme", [128, 256], c=c0)
        w_fmh = sb("w_fmh", [128, 256], c=c0, dt=mybir.dt.bfloat16)
        w_fml = sb("w_fml", [128, 256], c=c0, dt=mybir.dt.bfloat16)
        ps_t = [psum(f"ps_t0{i}", [128, 512], c=c0) for i in range(2)]

        def p0_tile(wsrc, wdst_h, wdst_l, IN, oi):
            nb = IN // 32
            it = IN // 128
            m_pe_prev = pe.n

            def s_load(e):
                gate(e, ld)
                e.wait_ge(pe.h, m_pe_prev)
                ld.dinc(e.dma_start(out=w_nat[:, 0:IN],
                                    in_=wsrc[oi * 128:(oi + 1) * 128, :]))
            sec("sync", s_load)
            m_ld1 = ld.n

            def v_deq(e):
                e.wait_ge(ld.h, m_ld1)
                w3 = w_nat[:, 0:IN].rearrange("p (b k) -> p b k", k=32)
                e.drain()
                e.tensor_reduce(out=w_bm[:, 0:nb], in_=w3, axis=AX, op=OP.max,
                                apply_absolute_value=True)
                e.drain()
                e.tensor_scalar(out=w_bm[:, 0:nb], in0=w_bm[:, 0:nb],
                                scalar1=1.0 / 7.0, scalar2=1e-12,
                                op0=OP.mult, op1=OP.max)
                e.drain()
                e.reciprocal(out=w_sc[:, 0:nb], in_=w_bm[:, 0:nb])
                e.drain()
                rs3 = w_sc[:, 0:nb].unsqueeze(2).broadcast_to([128, nb, 32])
                s3 = w_bm[:, 0:nb].unsqueeze(2).broadcast_to([128, nb, 32])
                e.tensor_tensor(out=w3, in0=w3, in1=rs3, op=OP.mult)
                e.drain()
                e.tensor_scalar(out=w_nat[:, 0:IN], in0=w_nat[:, 0:IN],
                                scalar1=C_RND, scalar2=-C_RND,
                                op0=OP.add, op1=OP.add)
                e.drain()
                ve.inc(e.tensor_tensor(out=w3, in0=w3, in1=s3, op=OP.mult))
            sec("vector", v_deq)
            m_ve1 = ve.n
            m_pe0 = pe.n
            m_ac0 = ac.n
            m_lgw = (lgw[0].n, lgw[1].n)

            def t_tr(e):
                e.wait_ge(ve.h, m_ve1)
                e.wait_ge(ac.h, m_ac0)
                for ii in range(it):
                    if ii >= 2:
                        e.wait_ge(ac.h, m_ac0 + ii - 1)
                    pe.inc(e.transpose(
                        ps_t[ii % 2][:, 0:128],
                        w_nat[:, ii * 128:(ii + 1) * 128], ident[:]))
            sec("tensor", t_tr)

            def s_ev(e):
                for ii in range(it):
                    e.wait_ge(pe.h, m_pe0 + ii + 1)
                    e.wait_ge(lgw[ii % 2].h, m_lgw[ii % 2] + 32 * (ii // 2))
                    ac.inc(e.copy(
                        out=w_fme[:, (ii % 2) * 128:(ii % 2) * 128 + 128],
                        in_=ps_t[ii % 2][:, 0:128]))
            sec("scalar", s_ev)

            m_ve_sp0 = ve.n

            def v_split(e):
                for ii in range(it):
                    e.wait_ge(ac.h, m_ac0 + ii + 1)
                    sl = slice((ii % 2) * 128, (ii % 2) * 128 + 128)
                    e.drain()
                    e.tensor_copy(out=w_fmh[:, sl], in_=w_fme[:, sl])
                    e.drain()
                    ve.inc(e.tensor_tensor(out=w_fml[:, sl], in0=w_fme[:, sl],
                                           in1=w_fmh[:, sl], op=OP.subtract))
            sec("vector", v_split)

            def g_st(e):
                gate(e, *lgw)
                for ii in range(it):
                    e.wait_ge(ve.h, m_ve_sp0 + ii + 1)
                    sl = slice((ii % 2) * 128, (ii % 2) * 128 + 128)
                    lgw[ii % 2].dinc(e.dma_start(
                        out=wdst_h[ii * 128:(ii + 1) * 128, oi * 128:(oi + 1) * 128],
                        in_=w_fmh[:, sl]))
                    lgw[ii % 2].dinc(e.dma_start(
                        out=wdst_l[ii * 128:(ii + 1) * 128, oi * 128:(oi + 1) * 128],
                        in_=w_fml[:, sl]))
            sec("gpsimd", g_st)

        _p0_lim = None
        if stop_after and stop_after.startswith("p0t"):
            _p0_lim = int(stop_after[3:])
            _P0_STOP[0] = True
        with c0:
            if not _PRE_P0[0]:
                _n0 = 0
                for (wsrc, wdst_h, wdst_l, O, IN) in [
                        (w_up, wupf_h, wupf_l, I, E),
                        (w_gate, wgatef_h, wgatef_l, I, E),
                        (w_down, wdownf_h, wdownf_l, E, I)]:
                    for oi in range(O // 128):
                        if _p0_lim is not None and _n0 >= _p0_lim:
                            break
                        p0_tile(wsrc, wdst_h, wdst_l, IN, oi)
                        _n0 += 1
            barrier()
        if stop_after in ("p0", "prep0"):
            _P0_STOP[0] = True

        c_att = contextlib.ExitStack()
        kres = sb("kres", [128, ET * S], c=c_att)
        vres = sb("vres", [128, NKT * E], c=c_att)

        glob_pair = [0]
        glob_kt = [0]
        qp_mark = {}

        def p1_batch(b):
            p1w_lp = {}
            c1 = contextlib.ExitStack()
            wmain = sb("wmain", [128, (ET - KSPL) * E], c=c1)
            wpre = [sb(f"wpre{i}", [128, KSPL * E], c=c1)
                    for i in range(2)]
            xt = sb("xt", [128, (CH // 128) * E], c=c1)
            xlnT = sb("xlnT", [128, ET * CH], c=c1)
            pstage = sb("pstage", [128, 512], c=c1)
            ps_p = [psum(f"ps_p{i}", [128, 512], c=c1) for i in range(4)]
            ps_v = [psum(f"ps_v{i}", [128, 512], c=c1) for i in range(2)]

            brow1 = wmain
            with c1:
                fill_brep(bvec["v"], brow1, ps_p[0])
                m_tr_done = [0]
                for ci in range(NCH):
                    t0 = b * S + ci * CH
                    m_pe_pre = m_tr_done[0]

                    def s_loadx(e, t0=t0, m_pe_pre=m_pe_pre):
                        gate(e, ld)
                        e.wait_ge(pe.h, m_pe_pre)
                        for j in range(CH // 128):
                            ld.dinc(e.dma_start(
                                out=xt[:, j * E:(j + 1) * E],
                                in_=x_in[t0 + j * 128:t0 + (j + 1) * 128, :]))
                    sec("sync", s_loadx)
                    m_ldx = ld.n

                    for j in range(CH // 128):
                        xa = xt[:, j * E:(j + 1) * E]
                        emit_ln(xa, xa, wait_ld=m_ldx)

                        def v_fin(e, xa=xa):
                            e.drain()
                            ve.inc(e.tensor_tensor(
                                out=xa, in0=xa,
                                in1=rstd[:].broadcast_to([128, E]), op=OP.mult))
                        sec("vector", v_fin)
                    m_ln_ve = ve.n

                    m_pe0 = pe.n
                    m_ac1 = ac.n

                    def t_tr(e, m_ac1=m_ac1):
                        e.wait_ge(ve.h, m_ln_ve)
                        e.wait_ge(ac.h, m_ac1)
                        n = 0
                        for j in range(CH // 128):
                            for k in range(ET):
                                if n >= 4:
                                    e.wait_ge(ac.h, m_ac1 + n - 3)
                                pe.inc(e.transpose(
                                    ps_p[n % 4][:, 0:128],
                                    xt[:, j * E + k * 128:j * E + (k + 1) * 128],
                                    ident[:]))
                                n += 1
                    sec("tensor", t_tr)
                    m_tr_done[0] = pe.n

                    def s_trev(e, m_pe0=m_pe0):
                        n = 0
                        for j in range(CH // 128):
                            for k in range(ET):
                                e.wait_ge(pe.h, m_pe0 + n + 1)
                                ac.inc(e.copy(
                                    out=xlnT[:, k * CH + j * 128:k * CH + (j + 1) * 128],
                                    in_=ps_p[n % 4][:, 0:128]))
                                n += 1
                    sec("scalar", s_trev)
                    m_tr_ac = ac.n

                    for pi, proj in enumerate(("q", "k", "v")):
                        n_st = ci * 3 + pi
                        m_pe_wpre = pe.n
                        wpre_cur = wpre[n_st % 2]
                        nxt = ("q", "k", "v")[(pi + 1) % 3] \
                            if n_st + 1 < 3 * NCH else None

                        def s_loadw(e, proj=proj, m_pe_wpre=m_pe_wpre,
                                    n_st=n_st, nxt=nxt):
                            gate(e, ld)
                            gate(e, *ldps)
                            e.wait_ge(pe.h, m_pe_wpre)
                            if n_st not in p1w_lp:
                                for k in range(KSPL):
                                    ldps[n_st % 2].dinc(e.dma_start(
                                        out=wpre[n_st % 2][:, k * E:(k + 1) * E],
                                        in_=wT[proj][k * 128:(k + 1) * 128, :]))
                                p1w_lp[n_st] = ldps[n_st % 2].n
                            for k in range(KSPL, ET):
                                ld.dinc(e.dma_start(
                                    out=wmain[:, (k - KSPL) * E:(k - KSPL + 1) * E],
                                    in_=wT[proj][k * 128:(k + 1) * 128, :]))
                            if nxt is not None:
                                for k in range(KSPL):
                                    ldps[(n_st + 1) % 2].dinc(e.dma_start(
                                        out=wpre[(n_st + 1) % 2][:,
                                                                k * E:(k + 1) * E],
                                        in_=wT[nxt][k * 128:(k + 1) * 128, :]))
                                p1w_lp[n_st + 1] = ldps[(n_st + 1) % 2].n
                        sec("sync", s_loadw)
                        m_lw1 = ld.n
                        m_lp1 = p1w_lp[n_st]
                        lp_sem = ldps[n_st % 2]

                        if proj in ("q", "k"):
                            m_acp = ac.n
                            m_pe1 = pe.n
                            m_lgp = (lgq[0].n, lgq[1].n)

                            def t_proj(e, m_lw1=m_lw1, m_acp=m_acp,
                                       m_lp1=m_lp1, wpre_cur=wpre_cur):
                                e.wait_ge(lp_sem.h, m_lp1)
                                e.wait_ge(ac.h, m_acp)
                                for o in range(ET):
                                    if o >= 4:
                                        e.wait_ge(ac.h, m_acp + (o - 4) + 1)
                                    for k in range(ET):
                                        if o == 0 and k == KSPL:
                                            e.wait_ge(ld.h, m_lw1)
                                        wsrc = wpre_cur if k < KSPL else wmain
                                        wb = k * E if k < KSPL \
                                            else (k - KSPL) * E
                                        ins = e.matmul(
                                            ps_p[o % 4][:, 0:CH],
                                            wsrc[:, wb + o * 128:wb + (o + 1) * 128],
                                            xlnT[:, k * CH:(k + 1) * CH],
                                            start=(k == 0), stop=(k == ET - 1))
                                    pe.inc(ins)
                            sec("tensor", t_proj)

                            if proj == "q":
                                def s_evq(e, m_pe1=m_pe1, m_lgp=m_lgp):
                                    for o in range(ET):
                                        e.wait_ge(pe.h, m_pe1 + o + 1)
                                        e.wait_ge(lgq[o % 2].h,
                                                  m_lgp[o % 2] + 16 * (o // 2))
                                        ac.inc(e.activation(
                                            out=pstage[:, (o % 2) * 256:(o % 2) * 256 + CH]
                                            if CH <= 256 else pstage[:],
                                            in_=ps_p[o % 4][:, 0:CH],
                                            func=AF.Identity,
                                            bias=btq[:, o:o + 1]))
                                sec("scalar", s_evq)

                                def g_stq(e, b=b, ci=ci, m_acp=m_acp):
                                    gate(e, *lgq)
                                    for o in range(ET):
                                        e.wait_ge(ac.h, m_acp + o + 1)
                                        lgq[o % 2].dinc(e.dma_start(
                                            out=q_s[b, o * 128:(o + 1) * 128,
                                                    ci * CH:(ci + 1) * CH],
                                            in_=pstage[:, (o % 2) * 256:(o % 2) * 256 + CH]
                                            if CH <= 256 else pstage[:]))
                                sec("gpsimd", g_stq)
                            else:
                                def s_evk(e, ci=ci, m_pe1=m_pe1):
                                    for o in range(ET):
                                        e.wait_ge(pe.h, m_pe1 + o + 1)
                                        ac.inc(e.activation(
                                            out=kres[:, o * S + ci * CH:o * S + (ci + 1) * CH],
                                            in_=ps_p[o % 4][:, 0:CH],
                                            func=AF.Identity,
                                            bias=btk[:, o:o + 1]))
                                sec("scalar", s_evk)
                        else:
                            m_pe1 = pe.n
                            m_vep = ve.n

                            def t_projv(e, m_lw1=m_lw1, m_vep=m_vep,
                                        m_lp1=m_lp1, wpre_cur=wpre_cur):
                                e.wait_ge(lp_sem.h, m_lp1)
                                e.wait_ge(ve.h, m_vep)
                                n = 0
                                for j in range(CH // 128):
                                    for oh in range(2):
                                        if n >= 2:
                                            e.wait_ge(ve.h, m_vep + n - 1)
                                        for k in range(ET):
                                            if n == 0 and k == KSPL:
                                                e.wait_ge(ld.h, m_lw1)
                                            wsrc = wpre_cur if k < KSPL \
                                                else wmain
                                            wb = k * E if k < KSPL \
                                                else (k - KSPL) * E
                                            ins = e.matmul(
                                                ps_v[n % 2][:],
                                                xlnT[:, k * CH + j * 128:k * CH + (j + 1) * 128],
                                                wsrc[:, wb + oh * 512:wb + oh * 512 + 512],
                                                start=(k == 0), stop=(k == ET - 1))
                                        pe.inc(ins)
                                        n += 1
                            sec("tensor", t_projv)

                            def v_evv(e, ci=ci, m_pe1=m_pe1):
                                n = 0
                                for j in range(CH // 128):
                                    tt = ci * (CH // 128) + j
                                    for oh in range(2):
                                        e.wait_ge(pe.h, m_pe1 + n + 1)
                                        e.drain()
                                        ve.inc(e.tensor_tensor(
                                            out=vres[:, tt * E + oh * 512:tt * E + oh * 512 + 512],
                                            in0=ps_v[n % 2][:],
                                            in1=brep[:, oh * 512:oh * 512 + 512],
                                            op=OP.add))
                                        n += 1
                            sec("vector", v_evv)
                barrier()

        def p2_batch(b):
            c2 = contextlib.ExitStack()
            e_all = sb("e_all", [128, 3 * NH * QB], c=c2)
            qblk = sb("qblk", [128, ET * QB], c=c2)
            dena = sb("dena", [128, NH * QB // 2], c=c2)
            afm = qblk
            rden = dena[:, 4 * QB:5 * QB]
            ps_p = [psum(f"ps_q{i}", [128, 512], c=c2) for i in range(4)]
            ps_av = [psum(f"ps_av{i}", [128, 1024], c=c2) for i in range(2)]
            atm = dena

            psp_gate = ac.n
            av_gate = ac.n
            with c2:
                qsta_lg = {}
                for qb in range(NQB):
                    def s_loadq(e, qb=qb):
                        gate(e, ld)
                        if (b, qb - 1) in qp_mark:
                            e.wait_ge(pe2.h, qp_mark[(b, qb - 1)])
                            e.wait_ge(lga[(qb - 1) % 2].h, qsta_lg[qb - 1])
                        for o in range(ET):
                            ld.dinc(e.dma_start(
                                out=qblk[:, o * QB:(o + 1) * QB],
                                in_=q_s[b, o * 128:(o + 1) * 128,
                                        qb * QB:(qb + 1) * QB]))
                    sec("sync", s_loadq)
                    m_ldq = ld.n

                    av_q = []
                    for kt in range(NKT):
                        g = glob_kt[0]
                        eoff = (g % 3) * NH * QB

                        def t_scores(e, kt=kt, g=g, m_ldq=m_ldq,
                                     psp_gate=psp_gate):
                            if kt == 0:
                                e.wait_ge(ld.h, m_ldq)
                                e.wait_ge(ac.h, psp_gate)
                            for j in range(8):
                                gpi = glob_pair[0]
                                if gpi >= 2:
                                    e.wait_ge(ac2.h, gpi - 1)
                                for h2 in range(2):
                                    h = 2 * j + h2
                                    ins = e.matmul(
                                        ps_p[h % 4][:, 0:QB],
                                        kres[64 * h2:64 * h2 + 64,
                                             (h // 2) * S + kt * 128:
                                             (h // 2) * S + (kt + 1) * 128],
                                        qblk[64 * h2:64 * h2 + 64,
                                             (h // 2) * QB:(h // 2) * QB + QB],
                                        start=True, stop=True,
                                        tile_position=(64 * h2, 0))
                                pe2.inc(ins)
                                glob_pair[0] += 1
                        sec("tensor", t_scores)

                        if len(av_q) >= 2:
                            sec("tensor", av_q.pop(0))

                        def s_exp(e, g=g, eoff=eoff):
                            if g >= 3:
                                e.wait_ge(pe3.h, g - 2)
                            for j in range(8):
                                e.wait_ge(pe2.h, g * 8 + j + 1)
                                e.activation(
                                    out=e_all[:, eoff + 2 * j * QB:eoff + (2 * j + 1) * QB],
                                    in_=ps_p[(2 * j) % 4][:, 0:QB],
                                    func=AF.Exp, scale=0.125)
                                ac2.inc(e.activation(
                                    out=e_all[:, eoff + (2 * j + 1) * QB:eoff + (2 * j + 2) * QB],
                                    in_=ps_p[(2 * j + 1) % 4][:, 0:QB],
                                    func=AF.Exp, scale=0.125))
                        sec("scalar", s_exp)

                        def v_soft(e, g=g, eoff=eoff):
                            W = NH * QB
                            e.wait_ge(ac2.h, (g + 1) * 8)
                            e.drain()
                            e.tensor_tensor(out=dena[:, 0:W // 2],
                                            in0=e_all[:, eoff:eoff + W // 2],
                                            in1=e_all[:, eoff + W // 2:eoff + W],
                                            op=OP.add)
                            e.drain()
                            e.tensor_tensor(out=dena[:, 0:W // 4],
                                            in0=dena[:, 0:W // 4],
                                            in1=dena[:, W // 4:W // 2], op=OP.add)
                            e.drain()
                            e.tensor_tensor(out=dena[:, 0:W // 8],
                                            in0=dena[:, 0:W // 8],
                                            in1=dena[:, W // 8:W // 4], op=OP.add)
                            e.drain()
                            e.tensor_tensor(out=dena[:, 0:QB], in0=dena[:, 0:QB],
                                            in1=dena[:, QB:2 * QB], op=OP.add)
                            e.drain()
                            e.reciprocal(out=rden, in_=dena[:, 0:QB])
                            e.drain()
                            ea3 = e_all[:, eoff:eoff + W].rearrange("p (h q) -> p h q", h=NH)
                            ve2.inc(e.tensor_tensor(
                                out=ea3, in0=ea3,
                                in1=rden.unsqueeze(1).broadcast_to([128, NH, QB]),
                                op=OP.mult))
                        sec("vector", v_soft)

                        def t_av(e, kt=kt, g=g, eoff=eoff, av_gate=av_gate):
                            e.wait_ge(ve2.h, g + 1)
                            if kt == 0:
                                e.wait_ge(ac.h, av_gate)
                            for h in range(NH):
                                for qh in range(2):
                                    ins = e.matmul(
                                        ps_av[qh][:, h * 64:(h + 1) * 64],
                                        e_all[:, eoff + h * QB + qh * 128:
                                              eoff + h * QB + qh * 128 + 128],
                                        vres[:, kt * E + h * 64:kt * E + (h + 1) * 64],
                                        start=(kt == 0 and h in (0, 8)),
                                        stop=(kt == NKT - 1 and h in (7, 15)))
                            pe3.inc(ins)
                        av_q.append(t_av)
                        glob_kt[0] += 1
                    while av_q:
                        sec("tensor", av_q.pop(0))
                    m_pe3_done = pe3.n
                    m_pe_trpre = pe.n
                    m_ve_soft = ve2.n

                    def s_avev(e):
                        e.wait_ge(pe3.h, m_pe3_done)
                        e.wait_ge(ve2.h, m_ve_soft)
                        e.wait_ge(pe.h, m_pe_trpre)
                        for qh in range(2):
                            e.drain()
                            ac.inc(e.tensor_copy(out=atm[:, qh * E:(qh + 1) * E],
                                                 in_=ps_av[qh][:]))
                    sec("vector", s_avev)
                    av_gate = ac.n

                    m_pe_tr0 = pe.n
                    m_ac_tr0 = ac.n
                    m_lg_pre = lg.n

                    def t_atr(e, av_gate=av_gate, m_ac_tr0=m_ac_tr0):
                        e.wait_ge(ac.h, av_gate)
                        n = 0
                        for qh in range(2):
                            for k in range(ET):
                                if n >= 4:
                                    e.wait_ge(ac.h, m_ac_tr0 + n - 3)
                                pe.inc(e.transpose(
                                    ps_p[n % 4][:, 0:128],
                                    atm[:, qh * E + k * 128:qh * E + (k + 1) * 128],
                                    ident[:]))
                                n += 1
                    sec("tensor", t_atr)

                    def s_atrev(e, m_pe_tr0=m_pe_tr0, m_lg_pre=m_lg_pre):
                        e.wait_ge(lg.h, m_lg_pre)
                        n = 0
                        for qh in range(2):
                            for k in range(ET):
                                e.wait_ge(pe.h, m_pe_tr0 + n + 1)
                                e.drain()
                                ac.inc(e.tensor_copy(
                                    out=afm[:, k * QB + qh * 128:k * QB + (qh + 1) * 128],
                                    in_=ps_p[n % 4][:, 0:128]))
                                n += 1
                    sec("vector", s_atrev)
                    m_ac_fm = ac.n
                    psp_gate = m_ac_fm

                    def g_sta(e, qb=qb, m_ac_fm=m_ac_fm):
                        gate(e, *lga)
                        e.wait_ge(ac.h, m_ac_fm)
                        for k in range(ET):
                            lga[qb % 2].dinc(e.dma_start(
                                out=attn_s[b, k * 128:(k + 1) * 128,
                                           qb * QB:(qb + 1) * QB],
                                in_=afm[:, k * QB:(k + 1) * QB]))
                    sec("gpsimd", g_sta)
                    qsta_lg[qb] = lga[qb % 2].n
                    qp_mark[(b, qb)] = pe2.n
                barrier()

        def p25_batch(b):
            c3 = contextlib.ExitStack()
            wof = sb("wof", [128, ET * E], c=c3)
            aload = sb("aload", [128, 2 * ET * 128], c=c3)
            x2t2 = sb("x2t2", [128, 2 * E], c=c3)
            xres2 = sb("xres2", [128, 2 * E], c=c3)
            xct2 = sb("xct2", [128, 2 * E], c=c3)
            strow = sb("strow", [128, 8], c=c3)
            ps_v = [psum(f"ps_o{i}", [128, 512], c=c3) for i in range(2)]
            brow3 = sb("brow3", [1, E], c=c3)

            with c3:
                fill_brep(bvec["o"], brow3, ps_v[0])
                m_fill_ac = ac.n

                def s_loadwo(e):
                    gate(e, ld)
                    for k in range(ET):
                        ld.dinc(e.dma_start(out=wof[:, k * E:(k + 1) * E],
                                            in_=wT["o"][k * 128:(k + 1) * 128, :]))
                sec("sync", s_loadwo)
                m_lwo1 = ld.n

                pe_done = {}
                ve_head = {}
                ac_sq = {}
                ve_tail = {}
                x2_lg = {}
                NT25 = S // 128

                def tail(tt):
                    p = tt % 2
                    m_sq = ac_sq[tt]

                    def v_tail(e, tt=tt, p=p, m_sq=m_sq):
                        e.wait_ge(ac.h, m_sq)
                        e.drain()
                        e.reciprocal(out=strow[:, 4 + p:5 + p],
                                     in_=strow[:, 2 + p:3 + p])
                        e.drain()
                        e.tensor_tensor(out=strow[:, 6 + p:7 + p],
                                        in0=strow[:, 6 + p:7 + p],
                                        in1=strow[:, 4 + p:5 + p], op=OP.mult)
                        e.drain()
                        ve.inc(e.tensor_tensor(out=runmax[:, 0:1],
                                               in0=runmax[:, 0:1],
                                               in1=strow[:, 6 + p:7 + p],
                                               op=OP.max))
                    sec("vector", v_tail)
                    ve_tail[tt] = ve.n
                    m_vt = ve.n

                    def g_stx2(e, tt=tt, p=p, m_vt=m_vt):
                        gate(e, *lgx)
                        e.wait_ge(ve.h, m_vt)
                        lgx[tt % 2].dinc(e.dma_start(
                            out=x2_s[b * S + tt * 128:b * S + (tt + 1) * 128, :],
                            in_=x2t2[:, p * E:(p + 1) * E]))
                    sec("gpsimd", g_stx2)
                    x2_lg[tt] = lgx[tt % 2].n

                for tt in range(NT25):
                    t0 = b * S + tt * 128
                    buf = tt % 2

                    def s_lda(e, tt=tt, buf=buf, t0=t0):
                        gate(e, ld)
                        if tt >= 2:
                            e.wait_ge(pe.h, pe_done[tt - 2])
                            e.wait_ge(lgx[tt % 2].h, x2_lg[tt - 2])
                            e.wait_ge(ve.h, ve_tail[tt - 2])
                        for k in range(ET):
                            ld.dinc(e.dma_start(
                                out=aload[:, buf * ET * 128 + k * 128:
                                          buf * ET * 128 + (k + 1) * 128],
                                in_=attn_s[b, k * 128:(k + 1) * 128,
                                           tt * 128:(tt + 1) * 128]))
                        ld.dinc(e.dma_start(out=xres2[:, buf * E:(buf + 1) * E],
                                            in_=x_in[t0:t0 + 128, :]))
                    sec("sync", s_lda)
                    m_lda1 = ld.n
                    m_ve_pre = ve_head.get(tt - 1, 0)
                    m_pe0 = pe.n

                    def t_oproj(e, buf=buf, m_lda1=m_lda1, m_ve_pre=m_ve_pre):
                        e.wait_ge(ld.h, max(m_lda1, m_lwo1))
                        e.wait_ge(ve.h, m_ve_pre)
                        e.wait_ge(ac.h, m_fill_ac)
                        for oh in range(2):
                            for k in range(ET):
                                ins = e.matmul(
                                    ps_v[oh][:],
                                    aload[:, buf * ET * 128 + k * 128:
                                          buf * ET * 128 + (k + 1) * 128],
                                    wof[:, k * E + oh * 512:k * E + oh * 512 + 512],
                                    start=(k == 0), stop=(k == ET - 1))
                            pe.inc(ins)
                    sec("tensor", t_oproj)
                    pe_done[tt] = pe.n
                    m_pe1 = pe.n

                    def v_main(e, tt=tt, buf=buf, m_pe1=m_pe1):
                        p = buf
                        x2a = x2t2[:, p * E:(p + 1) * E]
                        xca = xct2[:, p * E:(p + 1) * E]
                        for oh in range(2):
                            e.wait_ge(pe.h, m_pe1 - 1 + oh)
                            e.drain()
                            e.tensor_tensor(out=x2a[:, oh * 512:(oh + 1) * 512],
                                            in0=ps_v[oh][:],
                                            in1=brep[:, oh * 512:(oh + 1) * 512],
                                            op=OP.add)
                            e.drain()
                            e.tensor_tensor(out=x2a[:, oh * 512:(oh + 1) * 512],
                                            in0=xres2[:, p * E + oh * 512:
                                                      p * E + (oh + 1) * 512],
                                            in1=x2a[:, oh * 512:(oh + 1) * 512],
                                            op=OP.add)
                        e.drain()
                        e.tensor_reduce(out=mrow[:], in_=x2a, axis=AX, op=OP.add)
                        e.drain()
                        e.tensor_scalar(out=nmrow[:], in0=mrow[:], scalar1=1.0 / E,
                                        scalar2=None, op0=OP.mult)
                        e.drain()
                        e.scalar_tensor_tensor(out=xca, in0=x2a, scalar=1.0,
                                               in1=nmrow[:].broadcast_to([128, E]),
                                               op0=OP.mult, op1=OP.subtract)
                        for hh in range(2):
                            xh = xca[:, hh * (E // 2):(hh + 1) * (E // 2)]
                            e.drain()
                            e.tensor_tensor(out=junk[:], in0=xh, in1=xh, op=OP.mult)
                            e.drain()
                            e.tensor_reduce(out=(vrow[:] if hh == 0 else vrow2[:]),
                                            in_=junk[:], axis=AX, op=OP.add)
                        e.drain()
                        e.tensor_tensor(out=strow[:, 0 + p:1 + p], in0=vrow[:],
                                        in1=vrow2[:], op=OP.add)
                        e.drain()
                        ve.inc(e.tensor_reduce(out=strow[:, 6 + p:7 + p], in_=xca,
                                               axis=AX, op=OP.max,
                                               apply_absolute_value=True))
                    sec("vector", v_main)
                    ve_head[tt] = ve.n
                    m_vh = ve.n
                    m_ac_pre = ac.n

                    def s_sqrt(e, tt=tt, m_vh=m_vh):
                        p = tt % 2
                        e.wait_ge(ve.h, m_vh)
                        ac.inc(e.activation(out=strow[:, 2 + p:3 + p],
                                            in_=strow[:, 0 + p:1 + p],
                                            func=AF.Sqrt, scale=1.0 / E,
                                            bias=epsrow[:]))
                    sec("scalar", s_sqrt)
                    ac_sq[tt] = ac.n

                    if tt >= 1:
                        tail(tt - 1)
                tail(NT25 - 1)
                barrier()

        def dbg_finish():
            def v_z(e):
                e.drain()
                ve.inc(e.memset(junk[:], 0.0))
            sec("vector", v_z)
            m_vz = ve.n

            def g_z(e):
                e.wait_ge(ve.h, m_vz)
                for tt in range(T // 128):
                    sts[0].dinc(e.dma_start(out=out[tt * 128:(tt + 1) * 128, 0:E // 2],
                                            in_=junk[:]))
                    sts[0].dinc(e.dma_start(out=out[tt * 128:(tt + 1) * 128, E // 2:E],
                                            in_=junk[:]))
                e.wait_ge(sts[0].h, sts[0].n)
            sec("gpsimd", g_z)

        def stop_now(tag):
            return stop_after is not None and tag in stop_after
        if _P0_STOP[0]:
            dbg_finish()
            c_att.close()
            return nc
        if stop_now("pa"):
            dbg_finish()
            c_att.close()
            return nc

        with c_att:
            for b in range(B):
                if stop_now(f"p1b{b}"):
                    break
                p1_batch(b)
                if stop_now(f"p2b{b}"):
                    break
                p2_batch(b)
                if stop_now(f"p25b{b}"):
                    break
                p25_batch(b)
        if stop_after is not None and any(
                stop_after.startswith(p) for p in ("p1b", "p2b", "p25b", "att")):
            dbg_finish()
            return nc

        ar_round(runmax[:, 0:1], 1, 0, [1.0 / 32767.0], [(0, 1, 1.0)])
        if stop_after == "ar1":
            dbg_finish()
            return nc
        if stop_after == "skipffn":
            dbg_finish()
            return nc

        c_f1 = contextlib.ExitStack()
        xq_h = sb("xq_h", [128, ET * T], c=c_f1, dt=mybir.dt.bfloat16)
        xq_l = sb("xq_l", [128, ET * T], c=c_f1, dt=mybir.dt.bfloat16)
        x2b = sb("x2b", [128, E], c=c_f1)
        xcb = sb("xcb", [128, E], c=c_f1)
        zqb = sb("zqb", [128, E], c=c_f1)
        xqtm = sb("xqtm", [128, E], c=c_f1)
        wcol_h = sb("wcol_h", [128, 2 * ET * 128], c=c_f1, dt=mybir.dt.bfloat16)
        wcol_l = sb("wcol_l", [128, 2 * ET * 128], c=c_f1, dt=mybir.dt.bfloat16)
        fstage = sb("fstage", [128, 1024], c=c_f1)
        ps_f = [psum(f"ps_f{i}", [128, 512], c=c_f1) for i in range(2)]
        ps_tq = [psum(f"ps_tq{i}", [128, 512], c=c_f1) for i in range(2)]
        p4_pe_mark = {}
        m_ve_x2 = [0]

        with c_f1:
            for tt in range(T // 128):
                m_pe_pre = pe.n

                def s_ldx2(e, tt=tt, m_pe_pre=m_pe_pre):
                    gate(e, ld)
                    e.wait_ge(pe.h, m_pe_pre)
                    e.wait_ge(ve.h, ve.n)
                    ld.dinc(e.dma_start(out=x2b[:],
                                        in_=x2_s[tt * 128:(tt + 1) * 128, :]))
                sec("sync", s_ldx2)
                m_ld1 = ld.n
                emit_ln(x2b[:], xcb[:], wait_ld=m_ld1)

                def v_rsA(e):
                    e.drain()
                    ve.inc(e.tensor_tensor(out=rsA[:], in0=rstd[:],
                                           in1=scrow[:, 0:1], op=OP.mult))
                sec("vector", v_rsA)
                m_ve1 = ve.n
                m_ac_pre = ac.n

                def s_qdq(e, m_ve1=m_ve1):
                    e.wait_ge(ve.h, m_ve1)
                    e.activation(out=zqb[:], in_=xcb[:], func=AF.Identity,
                                 scale=rsA[:], bias=crow[:])
                    e.drain()
                    e.activation(out=zqb[:], in_=zqb[:], func=AF.Identity,
                                 bias=ncrow[:])
                    e.drain()
                    ac.inc(e.activation(out=xqtm[:], in_=zqb[:],
                                        func=AF.Identity, scale=scrow[:, 1:2]))
                sec("scalar", s_qdq)
                m_ac1 = ac.n
                m_pe0 = pe.n

                def t_trq(e, m_ac1=m_ac1, m_ve_x=ve.n):
                    e.wait_ge(ac.h, m_ac1)
                    e.wait_ge(ve.h, m_ve_x)
                    for k in range(ET):
                        if k >= 2:
                            e.wait_ge(ac.h, m_ac1 + k - 1)
                            e.wait_ge(ve.h, m_ve_x2[0] + k - 1)
                        pe.inc(e.transpose(
                            ps_tq[k % 2][:, 0:128],
                            xqtm[:, k * 128:(k + 1) * 128], ident[:]))
                m_ve_x2[0] = ve.n
                sec("tensor", t_trq)

                def s_trqev(e, tt=tt, m_pe0=m_pe0):
                    for k in range(ET):
                        e.wait_ge(pe.h, m_pe0 + k + 1)
                        ac.inc(e.copy(
                            out=xq_h[:, k * T + tt * 128:k * T + (tt + 1) * 128],
                            in_=ps_tq[k % 2][:, 0:128]))
                sec("scalar", s_trqev)
                m_ac_tr = ac.n - ET

                def v_xql(e, tt=tt, m_ac_tr=m_ac_tr):
                    for k in range(ET):
                        e.wait_ge(ac.h, m_ac_tr + k + 1)
                        e.drain()
                        ve.inc(e.tensor_tensor(
                            out=xq_l[:, k * T + tt * 128:k * T + (tt + 1) * 128],
                            in0=ps_tq[k % 2][:, 0:128],
                            in1=xq_h[:, k * T + tt * 128:k * T + (tt + 1) * 128],
                            op=OP.subtract))
                sec("vector", v_xql)
            barrier()
            P4_LG_BASE = (lgf[0].n, lgf[1].n)
            P4_VE_BASE = ve.n

            for wi, (wfsrc, dst_s, gcol) in enumerate(
                    (((wupf_h, wupf_l), up_s, 0),
                     ((wgatef_h, wgatef_l), gate_s, 1))):
                for it_ in range(IT):
                    wtile = wi * IT + it_
                    wbuf = wtile % 2
                    NTC = T // 512

                    def s_ldw(e, it_=it_, wfsrc=wfsrc, wbuf=wbuf, wtile=wtile):
                        gate(e, ld)
                        if wtile >= 2:
                            e.wait_ge(pe.h, p4_pe_mark[wtile - 2])
                        for k in range(ET):
                            ld.dinc(e.dma_start(
                                out=wcol_h[:, wbuf * ET * 128 + k * 128:
                                           wbuf * ET * 128 + (k + 1) * 128],
                                in_=wfsrc[0][k * 128:(k + 1) * 128,
                                             it_ * 128:(it_ + 1) * 128]))
                            ld.dinc(e.dma_start(
                                out=wcol_l[:, wbuf * ET * 128 + k * 128:
                                           wbuf * ET * 128 + (k + 1) * 128],
                                in_=wfsrc[1][k * 128:(k + 1) * 128,
                                             it_ * 128:(it_ + 1) * 128]))
                    sec("sync", s_ldw)
                    m_ldw = ld.n
                    m_pe0 = pe.n
                    m_ac0 = ac.n

                    def t_mm(e, m_ldw=m_ldw, m_ac0=m_ac0, wbuf=wbuf,
                             wtile=wtile):
                        e.wait_ge(ld.h, m_ldw)
                        e.wait_ge(ac.h, m_ac0)
                        for tc in range(NTC):
                            if tc >= 2:
                                e.wait_ge(ac.h, m_ac0 + tc - 1)
                            npass = 3 * ET
                            ip = 0
                            for k in range(ET):
                                wsl = slice(wbuf * ET * 128 + k * 128,
                                            wbuf * ET * 128 + (k + 1) * 128)
                                xsl = slice(k * T + tc * 512, k * T + (tc + 1) * 512)
                                for (wt, xt_) in ((wcol_h, xq_h), (wcol_h, xq_l),
                                                  (wcol_l, xq_h)):
                                    ins = e.matmul(
                                        ps_f[(wtile * NTC + tc) % 2][:],
                                        wt[:, wsl], xt_[:, xsl],
                                        start=(ip == 0), stop=(ip == npass - 1))
                                    ip += 1
                            pe.inc(ins)
                    sec("tensor", t_mm)
                    p4_pe_mark[wtile] = pe.n

                    def s_fev(e, m_pe0=m_pe0, wtile=wtile):
                        for tc in range(NTC):
                            gtc = wtile * NTC + tc
                            e.wait_ge(pe.h, m_pe0 + tc + 1)
                            e.wait_ge(lgf[gtc % 2].h,
                                      P4_LG_BASE[gtc % 2] + 16 * (gtc // 2))
                            if gtc >= 2:
                                e.wait_ge(ve.h, P4_VE_BASE + gtc - 1)
                            ac.inc(e.copy(
                                out=fstage[:, (gtc % 2) * 512:(gtc % 2) * 512 + 512],
                                in_=ps_f[gtc % 2][:]))
                    sec("scalar", s_fev)

                    def v_fmax(e, m_ac0=m_ac0, gcol=gcol, wtile=wtile):
                        for tc in range(NTC):
                            gtc = wtile * NTC + tc
                            e.wait_ge(ac.h, m_ac0 + tc + 1)
                            e.drain()
                            e.tensor_reduce(
                                out=tmax[:],
                                in_=fstage[:, (gtc % 2) * 512:(gtc % 2) * 512 + 512],
                                axis=AX, op=OP.max, apply_absolute_value=True)
                            e.drain()
                            ve.inc(e.tensor_tensor(out=gmax[:, gcol:gcol + 1],
                                                   in0=gmax[:, gcol:gcol + 1],
                                                   in1=tmax[:], op=OP.max))
                    sec("vector", v_fmax)
                    m_ve0 = ve.n - NTC

                    def g_fst(e, it_=it_, dst_s=dst_s, m_ac0=m_ac0, m_ve0=m_ve0,
                              wtile=wtile):
                        gate(e, *lgf)
                        for tc in range(NTC):
                            gtc = wtile * NTC + tc
                            e.wait_ge(ac.h, m_ac0 + tc + 1)
                            e.wait_ge(ve.h, m_ve0 + tc + 1)
                            lgf[gtc % 2].dinc(e.dma_start(
                                out=dst_s[it_ * 128:(it_ + 1) * 128,
                                          tc * 512:(tc + 1) * 512],
                                in_=fstage[:, (gtc % 2) * 512:(gtc % 2) * 512 + 512]))
                    sec("gpsimd", g_fst)
            barrier()

        ar_round(gmax[:, 0:2], 2, 1, [1.0 / 32767.0, 1.0 / 32767.0],
                 [(2, 3, 1.0), (4, 5, 1.0)])
        if stop_after == "ar2":
            dbg_finish()
            return nc

        c_f2 = contextlib.ExitStack()
        sA = sb("sA", [128, 2 * CW], c=c_f2)
        sB = sb("sB", [128, 2 * CW], c=c_f2)
        sZ = sb("sZ", [128, 2 * CW], c=c_f2)
        sG = sb("sG", [128, 2 * CW], c=c_f2)
        sSG = sb("sSG", [128, 2 * CW], c=c_f2)
        sO = sb("sO", [128, 2 * CW], c=c_f2)
        sOh = sb("sOh", [128, 2 * CW], c=c_f2, dt=mybir.dt.bfloat16)
        sOl = sb("sOl", [128, 2 * CW], c=c_f2, dt=mybir.dt.bfloat16)

        c_ac_mark = {}
        c_ve_mark = {}

        def qdq3_scalar(e, src_ap, dst_ap, rs_col, s_col, tmp_ap):
            e.activation(out=tmp_ap, in_=src_ap, func=AF.Identity,
                         scale=scrow[:, rs_col:rs_col + 1], bias=crow[:])
            e.drain()
            e.activation(out=tmp_ap, in_=tmp_ap, func=AF.Identity, bias=ncrow[:])
            e.drain()
            e.activation(out=dst_ap, in_=tmp_ap, func=AF.Identity,
                         scale=scrow[:, s_col:s_col + 1])

        def c_stream(srcs, dst, fn_scalar, fn_vector, gcol):
            c_ac_mark.clear()
            c_ve_mark.clear()
            LG_BASE = (lgc[0].n, lgc[1].n)
            for ti in range(NFT):
                i0 = (ti // (T // CW)) * 128
                cq = (ti % (T // CW)) * CW
                bufi = ti % 2

                def s_ld(e, i0=i0, cq=cq, bufi=bufi, ti=ti):
                    gate(e, ld)
                    if ti >= 2:
                        e.wait_ge(ve.h, c_ve_mark[ti - 2])
                        e.wait_ge(ac.h, c_ac_mark[ti - 2])
                    ld.dinc(e.dma_start(out=sA[:, bufi * CW:(bufi + 1) * CW],
                                        in_=srcs[0][i0:i0 + 128, cq:cq + CW]))
                    if len(srcs) > 1:
                        ld.dinc(e.dma_start(out=sB[:, bufi * CW:(bufi + 1) * CW],
                                            in_=srcs[1][i0:i0 + 128, cq:cq + CW]))
                sec("sync", s_ld)
                m_ld1 = ld.n

                fn_scalar(m_ld1, bufi, ti, LG_BASE)
                c_ac_mark[ti] = ac.n
                fn_vector(bufi, ti)

                def v_gmax(e, bufi=bufi, gcol=gcol):
                    if gcol is not None:
                        e.drain()
                        e.tensor_reduce(out=tmax[:],
                                        in_=sO[:, bufi * CW:(bufi + 1) * CW],
                                        axis=AX, op=OP.max,
                                        apply_absolute_value=True)
                        e.drain()
                        e.tensor_tensor(out=gmax[:, gcol:gcol + 1],
                                        in0=gmax[:, gcol:gcol + 1],
                                        in1=tmax[:], op=OP.max)
                    e.drain()
                    ve.inc(e.nop())
                sec("vector", v_gmax)
                c_ve_mark[ti] = ve.n
                m_ve2 = ve.n

                def g_st(e, i0=i0, cq=cq, bufi=bufi, m_ve2=m_ve2, ti=ti):
                    gate(e, *lgc)
                    e.wait_ge(ve.h, m_ve2)
                    e.wait_ge(ac.h, c_ac_mark[ti])
                    if isinstance(dst, tuple):
                        lgc[ti % 2].dinc(e.dma_start(
                            out=dst[0][i0:i0 + 128, cq:cq + CW],
                            in_=sOh[:, bufi * CW:(bufi + 1) * CW]))
                        lgc[ti % 2].dinc(e.dma_start(
                            out=dst[1][i0:i0 + 128, cq:cq + CW],
                            in_=sOl[:, bufi * CW:(bufi + 1) * CW]))
                    else:
                        lgc[ti % 2].dinc(e.dma_start(
                            out=dst[i0:i0 + 128, cq:cq + CW],
                            in_=sO[:, bufi * CW:(bufi + 1) * CW]))
                sec("gpsimd", g_st)

        with c_f2:
            def c1_scalar(m_ld1, bufi, ti, LG_BASE):
                def s_f(e):
                    e.wait_ge(ld.h, m_ld1)
                    if ti >= 2:
                        e.wait_ge(ve.h, c_ve_mark[ti - 2])
                    ga = sA[:, bufi * CW:(bufi + 1) * CW]
                    sGb = sG[:, bufi * CW:(bufi + 1) * CW]
                    qdq3_scalar(e, ga, sGb, 4, 5, sZ[:, bufi * CW:(bufi + 1) * CW])
                    e.drain()
                    ac.inc(e.activation(out=sSG[:, bufi * CW:(bufi + 1) * CW],
                                        in_=sGb, func=AF.Sigmoid))
                sec("scalar", s_f)

            def c1_vector(bufi, ti):
                def v_f(e):
                    e.wait_ge(ac.h, c_ac_mark[ti])
                    if ti >= 2:
                        e.wait_ge(lgc[ti % 2].h,
                                  c1_lgbase[ti % 2] + 16 * (ti // 2))
                    sGb = sG[:, bufi * CW:(bufi + 1) * CW]
                    sSb = sSG[:, bufi * CW:(bufi + 1) * CW]
                    e.drain()
                    e.tensor_scalar(out=sSb, in0=sSb, scalar1=65536.0,
                                    scalar2=C_RND, op0=OP.mult, op1=OP.add)
                    e.drain()
                    e.tensor_scalar(out=sSb, in0=sSb, scalar1=-C_RND,
                                    scalar2=65535.0, op0=OP.add, op1=OP.min)
                    e.drain()
                    e.tensor_tensor(out=sO[:, bufi * CW:(bufi + 1) * CW],
                                    in0=sGb, in1=sSb, op=OP.mult)
                sec("vector", v_f)

            c1_lgbase = (lgc[0].n, lgc[1].n)
            c_stream([gate_s], t_s, c1_scalar, c1_vector, 2)
            barrier()
            ar_round(gmax[:, 2:3], 1, 2, [2.0 ** -16 / 32767.0],
                     [(6, 7, 2.0 ** -16)])
            if stop_after == "c1":
                dbg_finish()
                return nc

            def c2_scalar(m_ld1, bufi, ti, LG_BASE):
                def s_f(e):
                    e.wait_ge(ld.h, m_ld1)
                    if ti >= 2:
                        e.wait_ge(ve.h, c_ve_mark[ti - 2])
                    ta = sA[:, bufi * CW:(bufi + 1) * CW]
                    qdq3_scalar(e, ta, sG[:, bufi * CW:(bufi + 1) * CW], 6, 7,
                                sZ[:, bufi * CW:(bufi + 1) * CW])
                    ac.inc(e.drain())
                sec("scalar", s_f)

            def c2_vector(bufi, ti):
                def v_f(e):
                    e.wait_ge(ac.h, c_ac_mark[ti])
                    if ti >= 2:
                        e.wait_ge(lgc[ti % 2].h,
                                  c2_lgbase[ti % 2] + 16 * (ti // 2))
                    ua = sB[:, bufi * CW:(bufi + 1) * CW]
                    sSb = sSG[:, bufi * CW:(bufi + 1) * CW]
                    e.drain()
                    e.scalar_tensor_tensor(
                        out=sSb, in0=ua, scalar=1.0,
                        in1=scrow[:, 2:3].broadcast_to([128, CW]),
                        op0=OP.mult, op1=OP.mult)
                    e.drain()
                    e.tensor_scalar(out=sSb, in0=sSb, scalar1=C_RND,
                                    scalar2=-C_RND, op0=OP.add, op1=OP.add)
                    e.drain()
                    e.scalar_tensor_tensor(
                        out=sSb, in0=sSb, scalar=1.0,
                        in1=scrow[:, 3:4].broadcast_to([128, CW]),
                        op0=OP.mult, op1=OP.mult)
                    e.drain()
                    e.tensor_tensor(out=sO[:, bufi * CW:(bufi + 1) * CW],
                                    in0=sG[:, bufi * CW:(bufi + 1) * CW],
                                    in1=sSb, op=OP.mult)
                sec("vector", v_f)

            c2_lgbase = (lgc[0].n, lgc[1].n)
            c_stream([t_s, up_s], t2_s, c2_scalar, c2_vector, 3)
            barrier()
            ar_round(gmax[:, 3:4], 1, 3, [1.0 / 32767.0], [(2, 3, 1.0)])

            def c3_scalar(m_ld1, bufi, ti, LG_BASE):
                def s_f(e):
                    e.wait_ge(ld.h, m_ld1)
                    if ti >= 2:
                        e.wait_ge(lgc[ti % 2].h,
                                  LG_BASE[ti % 2] + 32 * (ti // 2))
                    ta = sA[:, bufi * CW:(bufi + 1) * CW]
                    qdq3_scalar(e, ta, sO[:, bufi * CW:(bufi + 1) * CW], 2, 3,
                                sZ[:, bufi * CW:(bufi + 1) * CW])
                    ac.inc(e.drain())
                sec("scalar", s_f)

            def c3_vector(bufi, ti):
                def v_f(e):
                    e.wait_ge(ac.h, c_ac_mark[ti])
                    sl = slice(bufi * CW, (bufi + 1) * CW)
                    e.drain()
                    e.tensor_copy(out=sOh[:, sl], in_=sO[:, sl])
                    e.drain()
                    e.tensor_tensor(out=sOl[:, sl], in0=sO[:, sl],
                                    in1=sOh[:, sl], op=OP.subtract)
                sec("vector", v_f)

            c_stream([t2_s], (o_h_s, o_l_s), c3_scalar, c3_vector, None)
            barrier()

        c_f3 = contextlib.ExitStack()
        wdfull_h = sb("wdfull_h", [128, IT * E], c=c_f3, dt=mybir.dt.bfloat16)
        wdfull_l = sb("wdfull_l", [128, IT * E], c=c_f3, dt=mybir.dt.bfloat16)
        ocol_h = sb("ocol_h", [128, 2 * IT * 128], c=c_f3, dt=mybir.dt.bfloat16)
        ocol_l = sb("ocol_l", [128, 2 * IT * 128], c=c_f3, dt=mybir.dt.bfloat16)
        x2c3 = sb("x2c3", [128, 2 * E], c=c_f3)
        ostage = sb("ostage", [128, 2 * E], c=c_f3)
        ps_d = [psum(f"ps_d{i}", [128, 512], c=c_f3) for i in range(2)]

        with c_f3:
            def s_ldwd(e):
                gate(e, ld)
                for k in range(IT):
                    ld.dinc(e.dma_start(out=wdfull_h[:, k * E:(k + 1) * E],
                                        in_=wdownf_h[k * 128:(k + 1) * 128, :]))
                    ld.dinc(e.dma_start(out=wdfull_l[:, k * E:(k + 1) * E],
                                        in_=wdownf_l[k * 128:(k + 1) * 128, :]))
            sec("sync", s_ldwd)
            m_lwd = ld.n

            d_pe_mark = {}
            d_ve_mark = {}
            d_st_mark = {}
            for tt in range(T // 128):
                bufi = tt % 2

                def s_ldo(e, tt=tt, bufi=bufi):
                    gate(e, ld)
                    if tt >= 2:
                        e.wait_ge(pe.h, d_pe_mark[tt - 2])
                        e.wait_ge(ve.h, d_ve_mark[tt - 2])
                    for k in range(IT):
                        ld.dinc(e.dma_start(
                            out=ocol_h[:, bufi * IT * 128 + k * 128:
                                       bufi * IT * 128 + (k + 1) * 128],
                            in_=o_h_s[k * 128:(k + 1) * 128, tt * 128:(tt + 1) * 128]))
                        ld.dinc(e.dma_start(
                            out=ocol_l[:, bufi * IT * 128 + k * 128:
                                       bufi * IT * 128 + (k + 1) * 128],
                            in_=o_l_s[k * 128:(k + 1) * 128, tt * 128:(tt + 1) * 128]))
                    ld.dinc(e.dma_start(out=x2c3[:, bufi * E:(bufi + 1) * E],
                                        in_=x2_s[tt * 128:(tt + 1) * 128, :]))
                sec("sync", s_ldo)
                m_ldo = ld.n
                m_ve_pre = ve.n
                m_pe0 = pe.n

                def t_down(e, bufi=bufi, m_ldo=m_ldo, m_ve_pre=m_ve_pre):
                    e.wait_ge(ld.h, max(m_ldo, m_lwd))
                    e.wait_ge(ve.h, m_ve_pre)
                    for oh in range(2):
                        npass = 3 * IT
                        ip = 0
                        for k in range(IT):
                            osl = slice(bufi * IT * 128 + k * 128,
                                        bufi * IT * 128 + (k + 1) * 128)
                            wsl = slice(k * E + oh * 512, k * E + oh * 512 + 512)
                            for (ot, wt) in ((ocol_h, wdfull_h), (ocol_h, wdfull_l),
                                             (ocol_l, wdfull_h)):
                                ins = e.matmul(
                                    ps_d[oh][:], ot[:, osl], wt[:, wsl],
                                    start=(ip == 0), stop=(ip == npass - 1))
                                ip += 1
                        pe.inc(ins)
                sec("tensor", t_down)
                d_pe_mark[tt] = pe.n
                m_pe1 = pe.n

                def v_out(e, bufi=bufi, m_pe1=m_pe1, tt=tt):
                    if tt >= 2:
                        e.wait_ge(sts[tt % 2].h, d_st_mark[tt - 2])
                    for oh in range(2):
                        e.wait_ge(pe.h, m_pe1 - 1 + oh)
                        e.drain()
                        e.tensor_tensor(
                            out=ostage[:, bufi * E + oh * 512:bufi * E + oh * 512 + 512],
                            in0=ps_d[oh][:],
                            in1=x2c3[:, bufi * E + oh * 512:bufi * E + oh * 512 + 512],
                            op=OP.add)
                    e.drain()
                    ve.inc(e.nop())
                sec("vector", v_out)
                d_ve_mark[tt] = ve.n
                m_ve1 = ve.n

                def g_out(e, tt=tt, bufi=bufi, m_ve1=m_ve1):
                    gate(e, *sts)
                    e.wait_ge(ve.h, m_ve1)
                    sts[tt % 2].dinc(e.dma_start(
                        out=out[tt * 128:(tt + 1) * 128, :],
                        in_=ostage[:, bufi * E:(bufi + 1) * E]))
                sec("gpsimd", g_out)
                d_st_mark[tt] = sts[tt % 2].n

            m_st0 = sts[0].n
            m_st1 = sts[1].n

            def g_fin(e):
                e.wait_ge(sts[0].h, m_st0)
                e.wait_ge(sts[1].h, m_st1)
            sec("gpsimd", g_fin)

    return nc


def kernel(L0_attn_x, L0_attn_Wq, L0_attn_bq, L0_attn_Wk, L0_attn_bk,
           L0_attn_Wv, L0_attn_bv, L0_attn_Wo, L0_attn_bo,
           L0_ffn_w_gate, L0_ffn_w_up, L0_ffn_w_down):
    x = np.ascontiguousarray(np.asarray(L0_attn_x, dtype=np.float32))
    Bfull, S, E = x.shape
    per = Bfull // N_CORES

    common = {
        "wqT": np.ascontiguousarray(np.asarray(L0_attn_Wq, np.float32).T),
        "wkT": np.ascontiguousarray(np.asarray(L0_attn_Wk, np.float32).T),
        "wvT": np.ascontiguousarray(np.asarray(L0_attn_Wv, np.float32).T),
        "woT": np.ascontiguousarray(np.asarray(L0_attn_Wo, np.float32).T),
        "bmq": np.ascontiguousarray(
            np.asarray(L0_attn_bq, np.float32).reshape(-1, 128).T),
        "bmk": np.ascontiguousarray(
            np.asarray(L0_attn_bk, np.float32).reshape(-1, 128).T),
        "bv": np.ascontiguousarray(np.asarray(L0_attn_bv, np.float32)),
        "bo": np.ascontiguousarray(np.asarray(L0_attn_bo, np.float32)),
        "w_up": np.ascontiguousarray(np.asarray(L0_ffn_w_up, np.float32)),
        "w_gate": np.ascontiguousarray(np.asarray(L0_ffn_w_gate, np.float32)),
        "w_down": np.ascontiguousarray(np.asarray(L0_ffn_w_down, np.float32)),
        "ident": np.eye(128, dtype=np.float32),
    }
    in_maps = []
    for c in range(N_CORES):
        m = dict(common)
        m["x"] = np.ascontiguousarray(x[c * per:(c + 1) * per].reshape(per * S, E))
        in_maps.append(m)

    nc = build_program(S=S, B=per)
    res = run_bass_kernel_spmd(nc, in_maps, core_ids=list(range(N_CORES)))
    outs = [np.asarray(res.results[c]["out"]).reshape(per, S, E)
            for c in range(N_CORES)]
    return np.concatenate(outs, axis=0)


# version 18
# speedup vs baseline: 1.0211x; 1.0020x over previous
import contextlib
import numpy as np
import concourse.bass as bass
import concourse.mybir as mybir
from concourse.bass_utils import run_bass_kernel_spmd

F32 = mybir.dt.float32
AF = mybir.ActivationFunctionType
OP = mybir.AluOpType
AX = mybir.AxisListType.X

C_RND = 12582912.0
N_CORES = 8


class Sem:
    def __init__(self, ctx, nc, name):
        self.h = ctx.enter_context(nc.semaphore(name))
        self.n = 0

    def inc(self, instr, k=1):
        instr.then_inc(self.h, k)
        self.n += k
        return instr

    def dinc(self, instr):
        return self.inc(instr, 16)


def build_program(S=2048, B=2, stop_after=None):
    _P0_STOP = [False]
    E, I, NH = 1024, 3072, 16
    T = B * S
    QB = 256
    KSPL = 5
    NQB = S // QB
    NKT = S // 128
    CH = min(256, S)
    NCH = S // CH
    ET = E // 128
    IT = I // 128
    CW = min(2048, T)
    NFT = IT * (T // CW)

    nc = bass.Bass("TRN2", target_bir_lowering=False)

    x_in = nc.dram_tensor("x", [T, E], F32, kind="ExternalInput").ap()
    wT = {p: nc.dram_tensor(f"w{p}T", [E, E], F32, kind="ExternalInput").ap()
          for p in ("q", "k", "v", "o")}
    bvec = {p: nc.dram_tensor(f"b{p}", [E], F32, kind="ExternalInput").ap()
            for p in ("v", "o")}
    bmat = {p: nc.dram_tensor(f"bm{p}", [128, E // 128], F32,
                              kind="ExternalInput").ap()
            for p in ("q", "k")}
    w_up = nc.dram_tensor("w_up", [I, E], F32, kind="ExternalInput").ap()
    w_gate = nc.dram_tensor("w_gate", [I, E], F32, kind="ExternalInput").ap()
    w_down = nc.dram_tensor("w_down", [E, I], F32, kind="ExternalInput").ap()
    ident_in = nc.dram_tensor("ident", [128, 128], F32, kind="ExternalInput").ap()
    out = nc.dram_tensor("out", [T, E], F32, kind="ExternalOutput").ap()

    q_s = nc.dram_tensor("q_s", [B, E, S], F32).ap()
    attn_s = nc.dram_tensor("attn_s", [B, E, S], F32).ap()
    x2_s = nc.dram_tensor("x2_s", [T, E], F32).ap()
    BF16 = mybir.dt.bfloat16
    wupf_h = nc.dram_tensor("wupf_h", [E, I], BF16).ap()
    wupf_l = nc.dram_tensor("wupf_l", [E, I], BF16).ap()
    wgatef_h = nc.dram_tensor("wgatef_h", [E, I], BF16).ap()
    wgatef_l = nc.dram_tensor("wgatef_l", [E, I], BF16).ap()
    wdownf_h = nc.dram_tensor("wdownf_h", [I, E], BF16).ap()
    wdownf_l = nc.dram_tensor("wdownf_l", [I, E], BF16).ap()
    up_s = nc.dram_tensor("up_s", [I, T], F32).ap()
    gate_s = nc.dram_tensor("gate_s", [I, T], F32).ap()
    t_s = nc.dram_tensor("t_s", [I, T], F32).ap()
    t2_s = nc.dram_tensor("t2_s", [I, T], F32).ap()
    o_h_s = nc.dram_tensor("o_h_s", [I, T], BF16).ap()
    o_l_s = nc.dram_tensor("o_l_s", [I, T], BF16).ap()
    cc_in = [nc.dram_tensor(f"cc_in{i}", [1, 32], F32).ap() for i in range(4)]
    cc_out = [nc.dram_tensor(f"cc_out{i}", [1, 32], F32).ap() for i in range(4)]

    ctx = contextlib.ExitStack()
    with ctx:
        _uid = [0]

        def sb(name, shape, c=None, dt=F32):
            _uid[0] += 1
            return (c or ctx).enter_context(
                nc.sbuf_tensor(f"{name}_u{_uid[0]}", shape, dt))

        def psum(name, shape, c=None):
            _uid[0] += 1
            return (c or ctx).enter_context(
                nc.psum_tensor(f"{name}_u{_uid[0]}", shape, F32))

        ident = sb("identt", [128, 128])
        ones1 = sb("ones1", [1, 128])
        btq = sb("btq", [128, ET])
        btk = sb("btk", [128, ET])
        brep = sb("brep", [128, E])
        mrow = sb("mrow", [128, 1])
        nmrow = sb("nmrow", [128, 1])
        vrow = sb("vrow", [128, 1])
        sdrow = sb("sdrow", [128, 1])
        rstd = sb("rstd", [128, 1])
        rsA = sb("rsA", [128, 1])
        tmax = sb("tmax", [128, 1])
        runmax = sb("runmax", [128, 4])
        gmax = sb("gmax", [128, 4])
        junk = sb("junkt", [128, E // 2])
        vrow2 = sb("vrow2", [128, 1])
        tinyA = sb("tinyA", [1, 32])
        tinyB = sb("tinyB", [1, 32])
        tred = sb("tred", [4, 128])
        tred2 = sb("tred2", [4, 1])
        epsrow = sb("epsrow", [128, 1])
        crow = sb("crow", [128, 1])
        ncrow = sb("ncrow", [128, 1])
        scrow = sb("scrow", [128, 8])

        ld = Sem(ctx, nc, "ld")
        lg = Sem(ctx, nc, "lg")
        pe = Sem(ctx, nc, "pe")
        ac = Sem(ctx, nc, "ac")
        ve = Sem(ctx, nc, "ve")
        ccs = Sem(ctx, nc, "ccs")
        pe2 = Sem(ctx, nc, "pe2")
        ac2 = Sem(ctx, nc, "ac2")
        ve2 = Sem(ctx, nc, "ve2")
        pe3 = Sem(ctx, nc, "pe3")
        st = Sem(ctx, nc, "st")
        bar_s = Sem(ctx, nc, "bar_s")
        bar_g = Sem(ctx, nc, "bar_g")
        bar_p = Sem(ctx, nc, "bar_p")
        bar_a = Sem(ctx, nc, "bar_a")
        bar_v = Sem(ctx, nc, "bar_v")
        lgw = [Sem(ctx, nc, f"lgw{i}") for i in range(2)]
        lgq = [Sem(ctx, nc, f"lgq{i}") for i in range(2)]
        lga = [Sem(ctx, nc, f"lga{i}") for i in range(2)]
        lgx = [Sem(ctx, nc, f"lgx{i}") for i in range(2)]
        lgf = [Sem(ctx, nc, f"lgf{i}") for i in range(2)]
        lgc = [Sem(ctx, nc, f"lgc{i}") for i in range(2)]
        lgcc = [Sem(ctx, nc, f"lgcc{i}") for i in range(2)]
        sts = [Sem(ctx, nc, f"sts{i}") for i in range(2)]
        ldps = [Sem(ctx, nc, f"ldp{i}") for i in range(2)]

        block = ctx.enter_context(nc.Block())
        ENGS = ("sync", "gpsimd", "tensor", "scalar", "vector")

        def sec(name, fn):
            getattr(block, name)(fn)

        def gate(e, *sems):
            for s_ in sems:
                if s_.n:
                    e.wait_ge(s_.h, s_.n)

        def barrier():
            def f_sy(e):
                e.wait_ge(ld.h, ld.n)
                for s_ in ldps:
                    e.wait_ge(s_.h, s_.n)
                bar_s.inc(e.nop())

            def f_gp(e):
                for s_ in (lg, *lgw, *lgq, *lga, *lgx, *lgf, *lgc, *lgcc, *sts):
                    if s_.n:
                        e.wait_ge(s_.h, s_.n)
                bar_g.inc(e.nop())
            def f_pe(e):
                e.drain()
                bar_p.inc(e.nop())

            def f_ac(e):
                e.drain()
                bar_a.inc(e.nop())

            def f_ve(e):
                e.drain()
                bar_v.inc(e.nop())
            for en, f in zip(ENGS, (f_sy, f_gp, f_pe, f_ac, f_ve)):
                sec(en, f)
            marks = (bar_s.n, bar_g.n, bar_p.n, bar_a.n, bar_v.n)

            def w(e):
                e.wait_ge(bar_s.h, marks[0]); e.wait_ge(bar_g.h, marks[1])
                e.wait_ge(bar_p.h, marks[2]); e.wait_ge(bar_a.h, marks[3])
                e.wait_ge(bar_v.h, marks[4])
            for en in ENGS:
                sec(en, w)

        def emit_ln(x_ap, xc_ap, wait_ld=None, wait_pe=None):
            def v1(e):
                if wait_ld is not None:
                    e.wait_ge(ld.h, wait_ld)
                if wait_pe is not None:
                    e.wait_ge(pe.h, wait_pe)
                e.drain()
                e.tensor_reduce(out=mrow[:], in_=x_ap, axis=AX, op=OP.add)
                e.drain()
                e.tensor_scalar(out=nmrow[:], in0=mrow[:], scalar1=1.0 / E,
                                scalar2=None, op0=OP.mult)
                e.drain()
                e.scalar_tensor_tensor(out=xc_ap, in0=x_ap, scalar=1.0,
                                       in1=nmrow[:].broadcast_to([128, E]),
                                       op0=OP.mult, op1=OP.subtract)
                for hh in range(2):
                    xh = xc_ap[:, hh * (E // 2):(hh + 1) * (E // 2)]
                    e.drain()
                    e.tensor_tensor(out=junk[:], in0=xh, in1=xh, op=OP.mult)
                    e.drain()
                    e.tensor_reduce(out=(vrow[:] if hh == 0 else vrow2[:]),
                                    in_=junk[:], axis=AX, op=OP.add)
                e.drain()
                ve.inc(e.tensor_tensor(out=vrow[:], in0=vrow[:], in1=vrow2[:],
                                       op=OP.add))
            sec("vector", v1)
            m1 = ve.n

            def s1(e):
                e.wait_ge(ve.h, m1)
                ac.inc(e.activation(out=sdrow[:], in_=vrow[:], func=AF.Sqrt,
                                    scale=1.0 / E, bias=epsrow[:]))
            sec("scalar", s1)
            a1 = ac.n

            def v2(e):
                e.wait_ge(ac.h, a1)
                e.drain()
                ve.inc(e.reciprocal(out=rstd[:], in_=sdrow[:]))
            sec("vector", v2)

        def emit_prt(ps_r, src_ap, n, dst_col):
            m_v0 = ve.n
            m_p0 = pe.n

            def t1(e):
                e.wait_ge(ve.h, m_v0)
                pe.inc(e.transpose(ps_r[0:n, 0:128], src_ap, ident[:]))
            sec("tensor", t1)
            m_a0 = ac.n

            def s1(e):
                e.wait_ge(pe.h, m_p0 + 1)
                ac.inc(e.copy(out=tred[0:n, :], in_=ps_r[0:n, 0:128]))
            sec("scalar", s1)

            def v1(e):
                e.wait_ge(ac.h, m_a0 + 1)
                e.drain()
                ve.inc(e.tensor_reduce(out=tred2[0:n, :], in_=tred[0:n, :],
                                       axis=AX, op=OP.max))
            sec("vector", v1)
            m_v1 = ve.n
            m_p1 = pe.n

            def t2(e):
                e.wait_ge(ve.h, m_v1)
                pe.inc(e.transpose(ps_r[0:1, 128:128 + n], tred2[0:n, :],
                                   ident[0:n, 0:n]))
            sec("tensor", t2)

            def s2(e):
                e.wait_ge(pe.h, m_p1 + 1)
                ac.inc(e.copy(out=tinyA[0:1, dst_col:dst_col + n],
                              in_=ps_r[0:1, 128:128 + n]))
            sec("scalar", s2)

        def emit_allreduce(ps_r, idx, nvals, scale_mults, scrow_cols):
            m_a0 = ac.n

            def g_cc(e):
                gate(e, *lgcc)
                e.wait_ge(ac.h, m_a0)
                lgcc[0].dinc(e.dma_start(out=cc_in[idx], in_=tinyA[:]))
                e.wait_ge(lgcc[0].h, lgcc[0].n)
                ccs.inc(e.collective_compute(
                    "AllReduce", OP.max,
                    replica_groups=[list(range(N_CORES))],
                    ins=[cc_in[idx].opt()], outs=[cc_out[idx].opt()]))
                e.wait_ge(ccs.h, ccs.n)
                lgcc[1].dinc(e.dma_start(out=tinyB[:], in_=cc_out[idx]))
            sec("gpsimd", g_cc)
            m_lg1 = lgcc[1].n

            def v_sc(e):
                e.wait_ge(lgcc[1].h, m_lg1)
                for i in range(nvals):
                    e.drain()
                    e.tensor_scalar(out=tinyB[0:1, 8 + 2 * i:9 + 2 * i],
                                    in0=tinyB[0:1, i:i + 1],
                                    scalar1=scale_mults[i], scalar2=1e-12,
                                    op0=OP.mult, op1=OP.max)
                    e.drain()
                    e.reciprocal(out=tinyB[0:1, 9 + 2 * i:10 + 2 * i],
                                 in_=tinyB[0:1, 8 + 2 * i:9 + 2 * i])
                    if scrow_cols[i][2] != 1.0:
                        e.drain()
                        e.tensor_scalar(out=tinyB[0:1, 9 + 2 * i:10 + 2 * i],
                                        in0=tinyB[0:1, 9 + 2 * i:10 + 2 * i],
                                        scalar1=scrow_cols[i][2], scalar2=None,
                                        op0=OP.mult)
                e.drain()
                ve.inc(e.nop())
            sec("vector", v_sc)
            m_v1 = ve.n
            m_p0 = pe.n

            def t_bc(e):
                e.wait_ge(ve.h, m_v1)
                for i in range(nvals):
                    pe.inc(e.matmul(ps_r[:, i * 2:i * 2 + 1], ones1[:],
                                    tinyB[0:1, 9 + 2 * i:10 + 2 * i],
                                    start=(i == 0), stop=False))
                    pe.inc(e.matmul(ps_r[:, i * 2 + 1:i * 2 + 2], ones1[:],
                                    tinyB[0:1, 8 + 2 * i:9 + 2 * i],
                                    start=False, stop=(i == nvals - 1)))
            sec("tensor", t_bc)
            m_p1 = pe.n

            def s_bc(e):
                e.wait_ge(pe.h, m_p1)
                for i in range(nvals):
                    ac.inc(e.copy(out=scrow[:, scrow_cols[i][0]:scrow_cols[i][0] + 1],
                                  in_=ps_r[:, i * 2:i * 2 + 1]))
                    ac.inc(e.copy(out=scrow[:, scrow_cols[i][1]:scrow_cols[i][1] + 1],
                                  in_=ps_r[:, i * 2 + 1:i * 2 + 2]))
            sec("scalar", s_bc)

        _arn = [0]

        def ar_round(src_ap, n, idx, scale_mults, scrow_cols):
            car = contextlib.ExitStack()
            _arn[0] += 1
            ps_r = psum(f"ps_ar{_arn[0]}", [128, 512], c=car)
            with car:
                emit_prt(ps_r, src_ap, n, 0)
                emit_allreduce(ps_r, idx, n, scale_mults, scrow_cols)
                barrier()

        def pa_sync(e):
            gate(e, ld)
            ld.dinc(e.dma_start(out=ident[:], in_=ident_in))
            ld.dinc(e.dma_start(out=btq[:], in_=bmat["q"]))
            ld.dinc(e.dma_start(out=btk[:], in_=bmat["k"]))
        sec("sync", pa_sync)
        m_lda = ld.n

        def pa_vec(e):
            e.wait_ge(ld.h, m_lda)
            e.memset(ones1[:], 1.0)
            e.memset(runmax[:], 0.0)
            e.memset(tinyA[:], 0.0)
            e.memset(epsrow[:], 1e-5)
            e.memset(crow[:], C_RND)
            e.memset(ncrow[:], -C_RND)
            ve.inc(e.memset(gmax[:], 0.0))
        sec("vector", pa_vec)
        m_vea = ve.n

        barrier()

        def fill_brep(vec_dram, row, ps):
            def s_l(e):
                gate(e, ld)
                ld.dinc(e.dma_start(out=row[0:1, 0:E], in_=vec_dram.unsqueeze(0)))
            sec("sync", s_l)
            m_row_ld = ld.n
            m_p0 = pe.n
            m_a0 = ac.n

            def t_f(e):
                e.wait_ge(ld.h, m_row_ld)
                e.wait_ge(ve.h, ve.n)
                e.wait_ge(ac.h, m_a0)
                for j in range(2):
                    if j == 1:
                        e.wait_ge(ac.h, m_a0 + 1)
                    pe.inc(e.matmul(ps[:], ones1[:],
                                    row[0:1, j * 512:(j + 1) * 512],
                                    start=True, stop=True))
            sec("tensor", t_f)

            def s_f(e):
                for j in range(2):
                    e.wait_ge(pe.h, m_p0 + j + 1)
                    ac.inc(e.copy(out=brep[:, j * 512:(j + 1) * 512], in_=ps[:]))
            sec("scalar", s_f)

        _PRE_P0 = [stop_after == "prep0"]
        c0 = contextlib.ExitStack()
        w_nat = sb("w_nat", [128, I], c=c0)
        w_bm = sb("w_bm", [128, 128], c=c0)
        w_sc = sb("w_sc", [128, 128], c=c0)
        w_fme = sb("w_fme", [128, 256], c=c0)
        w_fmh = sb("w_fmh", [128, 256], c=c0, dt=mybir.dt.bfloat16)
        w_fml = sb("w_fml", [128, 256], c=c0, dt=mybir.dt.bfloat16)
        ps_t = [psum(f"ps_t0{i}", [128, 512], c=c0) for i in range(2)]

        def p0_tile(wsrc, wdst_h, wdst_l, IN, oi):
            nb = IN // 32
            it = IN // 128
            m_pe_prev = pe.n

            def s_load(e):
                gate(e, ld)
                e.wait_ge(pe.h, m_pe_prev)
                ld.dinc(e.dma_start(out=w_nat[:, 0:IN],
                                    in_=wsrc[oi * 128:(oi + 1) * 128, :]))
            sec("sync", s_load)
            m_ld1 = ld.n

            def v_deq(e):
                e.wait_ge(ld.h, m_ld1)
                w3 = w_nat[:, 0:IN].rearrange("p (b k) -> p b k", k=32)
                e.drain()
                e.tensor_reduce(out=w_bm[:, 0:nb], in_=w3, axis=AX, op=OP.max,
                                apply_absolute_value=True)
                e.drain()
                e.tensor_scalar(out=w_bm[:, 0:nb], in0=w_bm[:, 0:nb],
                                scalar1=1.0 / 7.0, scalar2=1e-12,
                                op0=OP.mult, op1=OP.max)
                e.drain()
                e.reciprocal(out=w_sc[:, 0:nb], in_=w_bm[:, 0:nb])
                e.drain()
                rs3 = w_sc[:, 0:nb].unsqueeze(2).broadcast_to([128, nb, 32])
                s3 = w_bm[:, 0:nb].unsqueeze(2).broadcast_to([128, nb, 32])
                e.tensor_tensor(out=w3, in0=w3, in1=rs3, op=OP.mult)
                e.drain()
                e.tensor_scalar(out=w_nat[:, 0:IN], in0=w_nat[:, 0:IN],
                                scalar1=C_RND, scalar2=-C_RND,
                                op0=OP.add, op1=OP.add)
                e.drain()
                ve.inc(e.tensor_tensor(out=w3, in0=w3, in1=s3, op=OP.mult))
            sec("vector", v_deq)
            m_ve1 = ve.n
            m_pe0 = pe.n
            m_ac0 = ac.n
            m_lgw = (lgw[0].n, lgw[1].n)

            def t_tr(e):
                e.wait_ge(ve.h, m_ve1)
                e.wait_ge(ac.h, m_ac0)
                for ii in range(it):
                    if ii >= 2:
                        e.wait_ge(ac.h, m_ac0 + ii - 1)
                    pe.inc(e.transpose(
                        ps_t[ii % 2][:, 0:128],
                        w_nat[:, ii * 128:(ii + 1) * 128], ident[:]))
            sec("tensor", t_tr)

            def s_ev(e):
                for ii in range(it):
                    e.wait_ge(pe.h, m_pe0 + ii + 1)
                    e.wait_ge(lgw[ii % 2].h, m_lgw[ii % 2] + 32 * (ii // 2))
                    ac.inc(e.copy(
                        out=w_fme[:, (ii % 2) * 128:(ii % 2) * 128 + 128],
                        in_=ps_t[ii % 2][:, 0:128]))
            sec("scalar", s_ev)

            m_ve_sp0 = ve.n

            def v_split(e):
                for ii in range(it):
                    e.wait_ge(ac.h, m_ac0 + ii + 1)
                    sl = slice((ii % 2) * 128, (ii % 2) * 128 + 128)
                    e.drain()
                    e.tensor_copy(out=w_fmh[:, sl], in_=w_fme[:, sl])
                    e.drain()
                    ve.inc(e.tensor_tensor(out=w_fml[:, sl], in0=w_fme[:, sl],
                                           in1=w_fmh[:, sl], op=OP.subtract))
            sec("vector", v_split)

            def g_st(e):
                gate(e, *lgw)
                for ii in range(it):
                    e.wait_ge(ve.h, m_ve_sp0 + ii + 1)
                    sl = slice((ii % 2) * 128, (ii % 2) * 128 + 128)
                    lgw[ii % 2].dinc(e.dma_start(
                        out=wdst_h[ii * 128:(ii + 1) * 128, oi * 128:(oi + 1) * 128],
                        in_=w_fmh[:, sl]))
                    lgw[ii % 2].dinc(e.dma_start(
                        out=wdst_l[ii * 128:(ii + 1) * 128, oi * 128:(oi + 1) * 128],
                        in_=w_fml[:, sl]))
            sec("gpsimd", g_st)

        _p0_lim = None
        if stop_after and stop_after.startswith("p0t"):
            _p0_lim = int(stop_after[3:])
            _P0_STOP[0] = True
        with c0:
            if not _PRE_P0[0]:
                _n0 = 0
                for (wsrc, wdst_h, wdst_l, O, IN) in [
                        (w_up, wupf_h, wupf_l, I, E),
                        (w_gate, wgatef_h, wgatef_l, I, E),
                        (w_down, wdownf_h, wdownf_l, E, I)]:
                    for oi in range(O // 128):
                        if _p0_lim is not None and _n0 >= _p0_lim:
                            break
                        p0_tile(wsrc, wdst_h, wdst_l, IN, oi)
                        _n0 += 1
            barrier()
        if stop_after in ("p0", "prep0"):
            _P0_STOP[0] = True

        c_att = contextlib.ExitStack()
        kres = sb("kres", [128, ET * S], c=c_att)
        vres = sb("vres", [128, NKT * E], c=c_att)

        glob_pair = [0]
        glob_kt = [0]
        qp_mark = {}

        def p1_batch(b):
            p1w_lp = {}
            c1 = contextlib.ExitStack()
            wmain = sb("wmain", [128, (ET - KSPL) * E], c=c1)
            wpre = [sb(f"wpre{i}", [128, KSPL * E], c=c1)
                    for i in range(2)]
            xt = sb("xt", [128, (CH // 128) * E], c=c1)
            xlnT = sb("xlnT", [128, ET * CH], c=c1)
            pstage = sb("pstage", [128, 512], c=c1)
            ps_p = [psum(f"ps_p{i}", [128, 512], c=c1) for i in range(4)]
            ps_v = [psum(f"ps_v{i}", [128, 512], c=c1) for i in range(2)]

            brow1 = wmain
            with c1:
                fill_brep(bvec["v"], brow1, ps_p[0])
                m_tr_done = [0]
                for ci in range(NCH):
                    t0 = b * S + ci * CH
                    m_pe_pre = m_tr_done[0]

                    def s_loadx(e, t0=t0, m_pe_pre=m_pe_pre):
                        gate(e, ld)
                        e.wait_ge(pe.h, m_pe_pre)
                        for j in range(CH // 128):
                            ld.dinc(e.dma_start(
                                out=xt[:, j * E:(j + 1) * E],
                                in_=x_in[t0 + j * 128:t0 + (j + 1) * 128, :]))
                    sec("sync", s_loadx)
                    m_ldx = ld.n

                    for j in range(CH // 128):
                        xa = xt[:, j * E:(j + 1) * E]
                        emit_ln(xa, xa, wait_ld=m_ldx)

                        def v_fin(e, xa=xa):
                            e.drain()
                            ve.inc(e.tensor_tensor(
                                out=xa, in0=xa,
                                in1=rstd[:].broadcast_to([128, E]), op=OP.mult))
                        sec("vector", v_fin)
                    m_ln_ve = ve.n

                    m_pe0 = pe.n
                    m_ac1 = ac.n

                    def t_tr(e, m_ac1=m_ac1):
                        e.wait_ge(ve.h, m_ln_ve)
                        e.wait_ge(ac.h, m_ac1)
                        n = 0
                        for j in range(CH // 128):
                            for k in range(ET):
                                if n >= 4:
                                    e.wait_ge(ac.h, m_ac1 + n - 3)
                                pe.inc(e.transpose(
                                    ps_p[n % 4][:, 0:128],
                                    xt[:, j * E + k * 128:j * E + (k + 1) * 128],
                                    ident[:]))
                                n += 1
                    sec("tensor", t_tr)
                    m_tr_done[0] = pe.n

                    def s_trev(e, m_pe0=m_pe0):
                        n = 0
                        for j in range(CH // 128):
                            for k in range(ET):
                                e.wait_ge(pe.h, m_pe0 + n + 1)
                                ac.inc(e.copy(
                                    out=xlnT[:, k * CH + j * 128:k * CH + (j + 1) * 128],
                                    in_=ps_p[n % 4][:, 0:128]))
                                n += 1
                    sec("scalar", s_trev)
                    m_tr_ac = ac.n

                    for pi, proj in enumerate(("q", "k", "v")):
                        n_st = ci * 3 + pi
                        m_pe_wpre = pe.n
                        wpre_cur = wpre[n_st % 2]
                        nxt = ("q", "k", "v")[(pi + 1) % 3] \
                            if n_st + 1 < 3 * NCH else None

                        def s_loadw(e, proj=proj, m_pe_wpre=m_pe_wpre,
                                    n_st=n_st, nxt=nxt):
                            gate(e, ld)
                            gate(e, *ldps)
                            e.wait_ge(pe.h, m_pe_wpre)
                            if n_st not in p1w_lp:
                                for k in range(KSPL):
                                    ldps[n_st % 2].dinc(e.dma_start(
                                        out=wpre[n_st % 2][:, k * E:(k + 1) * E],
                                        in_=wT[proj][k * 128:(k + 1) * 128, :]))
                                p1w_lp[n_st] = ldps[n_st % 2].n
                            for k in range(KSPL, ET):
                                ld.dinc(e.dma_start(
                                    out=wmain[:, (k - KSPL) * E:(k - KSPL + 1) * E],
                                    in_=wT[proj][k * 128:(k + 1) * 128, :]))
                            if nxt is not None:
                                for k in range(KSPL):
                                    ldps[(n_st + 1) % 2].dinc(e.dma_start(
                                        out=wpre[(n_st + 1) % 2][:,
                                                                k * E:(k + 1) * E],
                                        in_=wT[nxt][k * 128:(k + 1) * 128, :]))
                                p1w_lp[n_st + 1] = ldps[(n_st + 1) % 2].n
                        sec("sync", s_loadw)
                        m_lw1 = ld.n
                        m_lp1 = p1w_lp[n_st]
                        lp_sem = ldps[n_st % 2]

                        if proj in ("q", "k"):
                            m_acp = ac.n
                            m_pe1 = pe.n
                            m_lgp = (lgq[0].n, lgq[1].n)

                            def t_proj(e, m_lw1=m_lw1, m_acp=m_acp,
                                       m_lp1=m_lp1, wpre_cur=wpre_cur):
                                e.wait_ge(lp_sem.h, m_lp1)
                                e.wait_ge(ac.h, m_acp)
                                for o in range(ET):
                                    if o >= 4:
                                        e.wait_ge(ac.h, m_acp + (o - 4) + 1)
                                    for k in range(ET):
                                        if o == 0 and k == KSPL:
                                            e.wait_ge(ld.h, m_lw1)
                                        wsrc = wpre_cur if k < KSPL else wmain
                                        wb = k * E if k < KSPL \
                                            else (k - KSPL) * E
                                        ins = e.matmul(
                                            ps_p[o % 4][:, 0:CH],
                                            wsrc[:, wb + o * 128:wb + (o + 1) * 128],
                                            xlnT[:, k * CH:(k + 1) * CH],
                                            start=(k == 0), stop=(k == ET - 1))
                                    pe.inc(ins)
                            sec("tensor", t_proj)

                            if proj == "q":
                                def s_evq(e, m_pe1=m_pe1, m_lgp=m_lgp):
                                    for o in range(ET):
                                        e.wait_ge(pe.h, m_pe1 + o + 1)
                                        e.wait_ge(lgq[o % 2].h,
                                                  m_lgp[o % 2] + 16 * (o // 2))
                                        ac.inc(e.activation(
                                            out=pstage[:, (o % 2) * 256:(o % 2) * 256 + CH]
                                            if CH <= 256 else pstage[:],
                                            in_=ps_p[o % 4][:, 0:CH],
                                            func=AF.Identity,
                                            bias=btq[:, o:o + 1]))
                                sec("scalar", s_evq)

                                def g_stq(e, b=b, ci=ci, m_acp=m_acp):
                                    gate(e, *lgq)
                                    for o in range(ET):
                                        e.wait_ge(ac.h, m_acp + o + 1)
                                        lgq[o % 2].dinc(e.dma_start(
                                            out=q_s[b, o * 128:(o + 1) * 128,
                                                    ci * CH:(ci + 1) * CH],
                                            in_=pstage[:, (o % 2) * 256:(o % 2) * 256 + CH]
                                            if CH <= 256 else pstage[:]))
                                sec("gpsimd", g_stq)
                            else:
                                def s_evk(e, ci=ci, m_pe1=m_pe1):
                                    for o in range(ET):
                                        e.wait_ge(pe.h, m_pe1 + o + 1)
                                        ac.inc(e.activation(
                                            out=kres[:, o * S + ci * CH:o * S + (ci + 1) * CH],
                                            in_=ps_p[o % 4][:, 0:CH],
                                            func=AF.Identity,
                                            bias=btk[:, o:o + 1]))
                                sec("scalar", s_evk)
                        else:
                            m_pe1 = pe.n
                            m_vep = ve.n

                            def t_projv(e, m_lw1=m_lw1, m_vep=m_vep,
                                        m_lp1=m_lp1, wpre_cur=wpre_cur):
                                e.wait_ge(lp_sem.h, m_lp1)
                                e.wait_ge(ve.h, m_vep)
                                n = 0
                                for j in range(CH // 128):
                                    for oh in range(2):
                                        if n >= 2:
                                            e.wait_ge(ve.h, m_vep + n - 1)
                                        for k in range(ET):
                                            if n == 0 and k == KSPL:
                                                e.wait_ge(ld.h, m_lw1)
                                            wsrc = wpre_cur if k < KSPL \
                                                else wmain
                                            wb = k * E if k < KSPL \
                                                else (k - KSPL) * E
                                            ins = e.matmul(
                                                ps_v[n % 2][:],
                                                xlnT[:, k * CH + j * 128:k * CH + (j + 1) * 128],
                                                wsrc[:, wb + oh * 512:wb + oh * 512 + 512],
                                                start=(k == 0), stop=(k == ET - 1))
                                        pe.inc(ins)
                                        n += 1
                            sec("tensor", t_projv)

                            def v_evv(e, ci=ci, m_pe1=m_pe1):
                                n = 0
                                for j in range(CH // 128):
                                    tt = ci * (CH // 128) + j
                                    for oh in range(2):
                                        e.wait_ge(pe.h, m_pe1 + n + 1)
                                        e.drain()
                                        ve.inc(e.tensor_tensor(
                                            out=vres[:, tt * E + oh * 512:tt * E + oh * 512 + 512],
                                            in0=ps_v[n % 2][:],
                                            in1=brep[:, oh * 512:oh * 512 + 512],
                                            op=OP.add))
                                        n += 1
                            sec("vector", v_evv)
                barrier()

        def p2_batch(b):
            c2 = contextlib.ExitStack()
            e_all = sb("e_all", [128, 3 * NH * QB], c=c2)
            qblk = sb("qblk", [128, ET * QB], c=c2)
            dena = sb("dena", [128, NH * QB // 2], c=c2)
            afm = qblk
            rden = dena[:, 4 * QB:5 * QB]
            ps_p = [psum(f"ps_q{i}", [128, 512], c=c2) for i in range(4)]
            ps_av = [psum(f"ps_av{i}", [128, 1024], c=c2) for i in range(2)]
            atm = dena

            psp_gate = ac.n
            av_gate = ac.n
            with c2:
                qsta_lg = {}
                for qb in range(NQB):
                    def s_loadq(e, qb=qb):
                        gate(e, ld)
                        if (b, qb - 1) in qp_mark:
                            e.wait_ge(pe2.h, qp_mark[(b, qb - 1)])
                            e.wait_ge(lga[(qb - 1) % 2].h, qsta_lg[qb - 1])
                        for o in range(ET):
                            ld.dinc(e.dma_start(
                                out=qblk[:, o * QB:(o + 1) * QB],
                                in_=q_s[b, o * 128:(o + 1) * 128,
                                        qb * QB:(qb + 1) * QB]))
                    sec("sync", s_loadq)
                    m_ldq = ld.n

                    av_q = []
                    for kt in range(NKT):
                        g = glob_kt[0]
                        eoff = (g % 3) * NH * QB

                        def t_scores(e, kt=kt, g=g, m_ldq=m_ldq,
                                     psp_gate=psp_gate):
                            if kt == 0:
                                e.wait_ge(ld.h, m_ldq)
                                e.wait_ge(ac.h, psp_gate)
                            for j in range(8):
                                gpi = glob_pair[0]
                                if gpi >= 2:
                                    e.wait_ge(ac2.h, gpi - 1)
                                for h2 in range(2):
                                    h = 2 * j + h2
                                    ins = e.matmul(
                                        ps_p[h % 4][:, 0:QB],
                                        kres[64 * h2:64 * h2 + 64,
                                             (h // 2) * S + kt * 128:
                                             (h // 2) * S + (kt + 1) * 128],
                                        qblk[64 * h2:64 * h2 + 64,
                                             (h // 2) * QB:(h // 2) * QB + QB],
                                        start=True, stop=True,
                                        tile_position=(64 * h2, 0))
                                pe2.inc(ins)
                                glob_pair[0] += 1
                        sec("tensor", t_scores)

                        if len(av_q) >= 2:
                            sec("tensor", av_q.pop(0))

                        def s_exp(e, g=g, eoff=eoff):
                            if g >= 3:
                                e.wait_ge(pe3.h, g - 2)
                            for j in range(8):
                                e.wait_ge(pe2.h, g * 8 + j + 1)
                                e.activation(
                                    out=e_all[:, eoff + 2 * j * QB:eoff + (2 * j + 1) * QB],
                                    in_=ps_p[(2 * j) % 4][:, 0:QB],
                                    func=AF.Exp, scale=0.125)
                                ac2.inc(e.activation(
                                    out=e_all[:, eoff + (2 * j + 1) * QB:eoff + (2 * j + 2) * QB],
                                    in_=ps_p[(2 * j + 1) % 4][:, 0:QB],
                                    func=AF.Exp, scale=0.125))
                        sec("scalar", s_exp)

                        def v_soft(e, g=g, eoff=eoff):
                            W = NH * QB
                            e.wait_ge(ac2.h, (g + 1) * 8)
                            e.drain()
                            e.tensor_tensor(out=dena[:, 0:W // 2],
                                            in0=e_all[:, eoff:eoff + W // 2],
                                            in1=e_all[:, eoff + W // 2:eoff + W],
                                            op=OP.add)
                            e.drain()
                            e.tensor_tensor(out=dena[:, 0:W // 4],
                                            in0=dena[:, 0:W // 4],
                                            in1=dena[:, W // 4:W // 2], op=OP.add)
                            e.drain()
                            e.tensor_tensor(out=dena[:, 0:W // 8],
                                            in0=dena[:, 0:W // 8],
                                            in1=dena[:, W // 8:W // 4], op=OP.add)
                            e.drain()
                            e.tensor_tensor(out=dena[:, 0:QB], in0=dena[:, 0:QB],
                                            in1=dena[:, QB:2 * QB], op=OP.add)
                            e.drain()
                            e.reciprocal(out=rden, in_=dena[:, 0:QB])
                            e.drain()
                            ea3 = e_all[:, eoff:eoff + W].rearrange("p (h q) -> p h q", h=NH)
                            ve2.inc(e.tensor_tensor(
                                out=ea3, in0=ea3,
                                in1=rden.unsqueeze(1).broadcast_to([128, NH, QB]),
                                op=OP.mult))
                        sec("vector", v_soft)

                        def t_av(e, kt=kt, g=g, eoff=eoff, av_gate=av_gate):
                            e.wait_ge(ve2.h, g + 1)
                            if kt == 0:
                                e.wait_ge(ac.h, av_gate)
                            for h in range(NH):
                                for qh in range(2):
                                    ins = e.matmul(
                                        ps_av[qh][:, h * 64:(h + 1) * 64],
                                        e_all[:, eoff + h * QB + qh * 128:
                                              eoff + h * QB + qh * 128 + 128],
                                        vres[:, kt * E + h * 64:kt * E + (h + 1) * 64],
                                        start=(kt == 0 and h in (0, 8)),
                                        stop=(kt == NKT - 1 and h in (7, 15)))
                            pe3.inc(ins)
                        av_q.append(t_av)
                        glob_kt[0] += 1
                    while av_q:
                        sec("tensor", av_q.pop(0))
                    m_pe3_done = pe3.n
                    m_pe_trpre = pe.n
                    m_ve_soft = ve2.n

                    def s_avev(e):
                        e.wait_ge(pe3.h, m_pe3_done)
                        e.wait_ge(ve2.h, m_ve_soft)
                        e.wait_ge(pe.h, m_pe_trpre)
                        for qh in range(2):
                            e.drain()
                            ac.inc(e.tensor_copy(out=atm[:, qh * E:(qh + 1) * E],
                                                 in_=ps_av[qh][:]))
                    sec("vector", s_avev)
                    av_gate = ac.n

                    m_pe_tr0 = pe.n
                    m_ac_tr0 = ac.n
                    m_lg_pre = lg.n

                    def t_atr(e, av_gate=av_gate, m_ac_tr0=m_ac_tr0):
                        e.wait_ge(ac.h, av_gate)
                        n = 0
                        for qh in range(2):
                            for k in range(ET):
                                if n >= 4:
                                    e.wait_ge(ac.h, m_ac_tr0 + n - 3)
                                pe.inc(e.transpose(
                                    ps_p[n % 4][:, 0:128],
                                    atm[:, qh * E + k * 128:qh * E + (k + 1) * 128],
                                    ident[:]))
                                n += 1
                    sec("tensor", t_atr)

                    def s_atrev(e, m_pe_tr0=m_pe_tr0, m_lg_pre=m_lg_pre):
                        e.wait_ge(lg.h, m_lg_pre)
                        n = 0
                        for qh in range(2):
                            for k in range(ET):
                                e.wait_ge(pe.h, m_pe_tr0 + n + 1)
                                e.drain()
                                ac.inc(e.tensor_copy(
                                    out=afm[:, k * QB + qh * 128:k * QB + (qh + 1) * 128],
                                    in_=ps_p[n % 4][:, 0:128]))
                                n += 1
                    sec("vector", s_atrev)
                    m_ac_fm = ac.n
                    psp_gate = m_ac_fm

                    def g_sta(e, qb=qb, m_ac_fm=m_ac_fm):
                        gate(e, *lga)
                        e.wait_ge(ac.h, m_ac_fm)
                        for k in range(ET):
                            lga[qb % 2].dinc(e.dma_start(
                                out=attn_s[b, k * 128:(k + 1) * 128,
                                           qb * QB:(qb + 1) * QB],
                                in_=afm[:, k * QB:(k + 1) * QB]))
                    sec("gpsimd", g_sta)
                    qsta_lg[qb] = lga[qb % 2].n
                    qp_mark[(b, qb)] = pe2.n
                barrier()

        def p25_batch(b):
            c3 = contextlib.ExitStack()
            wof = sb("wof", [128, ET * E], c=c3)
            aload = sb("aload", [128, 2 * ET * 128], c=c3)
            x2t2 = sb("x2t2", [128, 2 * E], c=c3)
            xres2 = sb("xres2", [128, 2 * E], c=c3)
            xct2 = sb("xct2", [128, 2 * E], c=c3)
            strow = sb("strow", [128, 8], c=c3)
            ps_v = [psum(f"ps_o{i}", [128, 512], c=c3) for i in range(2)]
            brow3 = sb("brow3", [1, E], c=c3)

            with c3:
                fill_brep(bvec["o"], brow3, ps_v[0])
                m_fill_ac = ac.n

                def s_loadwo(e):
                    gate(e, ld)
                    for k in range(ET):
                        ld.dinc(e.dma_start(out=wof[:, k * E:(k + 1) * E],
                                            in_=wT["o"][k * 128:(k + 1) * 128, :]))
                sec("sync", s_loadwo)
                m_lwo1 = ld.n

                pe_done = {}
                ve_head = {}
                ac_sq = {}
                ve_tail = {}
                x2_lg = {}
                NT25 = S // 128

                def tail(tt):
                    p = tt % 2
                    m_sq = ac_sq[tt]

                    def v_tail(e, tt=tt, p=p, m_sq=m_sq):
                        e.wait_ge(ac.h, m_sq)
                        e.drain()
                        e.reciprocal(out=strow[:, 4 + p:5 + p],
                                     in_=strow[:, 2 + p:3 + p])
                        e.drain()
                        e.tensor_tensor(out=strow[:, 6 + p:7 + p],
                                        in0=strow[:, 6 + p:7 + p],
                                        in1=strow[:, 4 + p:5 + p], op=OP.mult)
                        e.drain()
                        ve.inc(e.tensor_tensor(out=runmax[:, 0:1],
                                               in0=runmax[:, 0:1],
                                               in1=strow[:, 6 + p:7 + p],
                                               op=OP.max))
                    sec("vector", v_tail)
                    ve_tail[tt] = ve.n
                    m_vt = ve.n

                    def g_stx2(e, tt=tt, p=p, m_vt=m_vt):
                        gate(e, *lgx)
                        e.wait_ge(ve.h, m_vt)
                        lgx[tt % 2].dinc(e.dma_start(
                            out=x2_s[b * S + tt * 128:b * S + (tt + 1) * 128, :],
                            in_=x2t2[:, p * E:(p + 1) * E]))
                    sec("gpsimd", g_stx2)
                    x2_lg[tt] = lgx[tt % 2].n

                for tt in range(NT25):
                    t0 = b * S + tt * 128
                    buf = tt % 2

                    def s_lda(e, tt=tt, buf=buf, t0=t0):
                        gate(e, ld)
                        if tt >= 2:
                            e.wait_ge(pe.h, pe_done[tt - 2])
                            e.wait_ge(lgx[tt % 2].h, x2_lg[tt - 2])
                            e.wait_ge(ve.h, ve_tail[tt - 2])
                        for k in range(ET):
                            ld.dinc(e.dma_start(
                                out=aload[:, buf * ET * 128 + k * 128:
                                          buf * ET * 128 + (k + 1) * 128],
                                in_=attn_s[b, k * 128:(k + 1) * 128,
                                           tt * 128:(tt + 1) * 128]))
                        ld.dinc(e.dma_start(out=xres2[:, buf * E:(buf + 1) * E],
                                            in_=x_in[t0:t0 + 128, :]))
                    sec("sync", s_lda)
                    m_lda1 = ld.n
                    m_ve_pre = ve_head.get(tt - 1, 0)
                    m_pe0 = pe.n

                    def t_oproj(e, buf=buf, m_lda1=m_lda1, m_ve_pre=m_ve_pre):
                        e.wait_ge(ld.h, max(m_lda1, m_lwo1))
                        e.wait_ge(ve.h, m_ve_pre)
                        e.wait_ge(ac.h, m_fill_ac)
                        for oh in range(2):
                            for k in range(ET):
                                ins = e.matmul(
                                    ps_v[oh][:],
                                    aload[:, buf * ET * 128 + k * 128:
                                          buf * ET * 128 + (k + 1) * 128],
                                    wof[:, k * E + oh * 512:k * E + oh * 512 + 512],
                                    start=(k == 0), stop=(k == ET - 1))
                            pe.inc(ins)
                    sec("tensor", t_oproj)
                    pe_done[tt] = pe.n
                    m_pe1 = pe.n

                    def v_main(e, tt=tt, buf=buf, m_pe1=m_pe1):
                        p = buf
                        x2a = x2t2[:, p * E:(p + 1) * E]
                        xca = xct2[:, p * E:(p + 1) * E]
                        for oh in range(2):
                            e.wait_ge(pe.h, m_pe1 - 1 + oh)
                            e.drain()
                            e.tensor_tensor(out=x2a[:, oh * 512:(oh + 1) * 512],
                                            in0=ps_v[oh][:],
                                            in1=brep[:, oh * 512:(oh + 1) * 512],
                                            op=OP.add)
                            e.drain()
                            e.tensor_tensor(out=x2a[:, oh * 512:(oh + 1) * 512],
                                            in0=xres2[:, p * E + oh * 512:
                                                      p * E + (oh + 1) * 512],
                                            in1=x2a[:, oh * 512:(oh + 1) * 512],
                                            op=OP.add)
                        e.drain()
                        e.tensor_reduce(out=mrow[:], in_=x2a, axis=AX, op=OP.add)
                        e.drain()
                        e.tensor_scalar(out=nmrow[:], in0=mrow[:], scalar1=1.0 / E,
                                        scalar2=None, op0=OP.mult)
                        e.drain()
                        e.scalar_tensor_tensor(out=xca, in0=x2a, scalar=1.0,
                                               in1=nmrow[:].broadcast_to([128, E]),
                                               op0=OP.mult, op1=OP.subtract)
                        for hh in range(2):
                            xh = xca[:, hh * (E // 2):(hh + 1) * (E // 2)]
                            e.drain()
                            e.tensor_tensor(out=junk[:], in0=xh, in1=xh, op=OP.mult)
                            e.drain()
                            e.tensor_reduce(out=(vrow[:] if hh == 0 else vrow2[:]),
                                            in_=junk[:], axis=AX, op=OP.add)
                        e.drain()
                        e.tensor_tensor(out=strow[:, 0 + p:1 + p], in0=vrow[:],
                                        in1=vrow2[:], op=OP.add)
                        e.drain()
                        ve.inc(e.tensor_reduce(out=strow[:, 6 + p:7 + p], in_=xca,
                                               axis=AX, op=OP.max,
                                               apply_absolute_value=True))
                    sec("vector", v_main)
                    ve_head[tt] = ve.n
                    m_vh = ve.n
                    m_ac_pre = ac.n

                    def s_sqrt(e, tt=tt, m_vh=m_vh):
                        p = tt % 2
                        e.wait_ge(ve.h, m_vh)
                        ac.inc(e.activation(out=strow[:, 2 + p:3 + p],
                                            in_=strow[:, 0 + p:1 + p],
                                            func=AF.Sqrt, scale=1.0 / E,
                                            bias=epsrow[:]))
                    sec("scalar", s_sqrt)
                    ac_sq[tt] = ac.n

                    if tt >= 1:
                        tail(tt - 1)
                tail(NT25 - 1)
                barrier()

        def dbg_finish():
            def v_z(e):
                e.drain()
                ve.inc(e.memset(junk[:], 0.0))
            sec("vector", v_z)
            m_vz = ve.n

            def g_z(e):
                e.wait_ge(ve.h, m_vz)
                for tt in range(T // 128):
                    sts[0].dinc(e.dma_start(out=out[tt * 128:(tt + 1) * 128, 0:E // 2],
                                            in_=junk[:]))
                    sts[0].dinc(e.dma_start(out=out[tt * 128:(tt + 1) * 128, E // 2:E],
                                            in_=junk[:]))
                e.wait_ge(sts[0].h, sts[0].n)
            sec("gpsimd", g_z)

        def stop_now(tag):
            return stop_after is not None and tag in stop_after
        if _P0_STOP[0]:
            dbg_finish()
            c_att.close()
            return nc
        if stop_now("pa"):
            dbg_finish()
            c_att.close()
            return nc

        with c_att:
            for b in range(B):
                if stop_now(f"p1b{b}"):
                    break
                p1_batch(b)
                if stop_now(f"p2b{b}"):
                    break
                p2_batch(b)
                if stop_now(f"p25b{b}"):
                    break
                p25_batch(b)
        if stop_after is not None and any(
                stop_after.startswith(p) for p in ("p1b", "p2b", "p25b", "att")):
            dbg_finish()
            return nc

        ar_round(runmax[:, 0:1], 1, 0, [1.0 / 32767.0], [(0, 1, 1.0)])
        if stop_after == "ar1":
            dbg_finish()
            return nc
        if stop_after == "skipffn":
            dbg_finish()
            return nc

        c_f1 = contextlib.ExitStack()
        xq_h = sb("xq_h", [128, ET * T], c=c_f1, dt=mybir.dt.bfloat16)
        xq_l = sb("xq_l", [128, ET * T], c=c_f1, dt=mybir.dt.bfloat16)
        x2b = sb("x2b", [128, E], c=c_f1)
        xcb = sb("xcb", [128, E], c=c_f1)
        zqb = sb("zqb", [128, E], c=c_f1)
        xqtm = sb("xqtm", [128, E], c=c_f1)
        wcol_h = sb("wcol_h", [128, 2 * ET * 128], c=c_f1, dt=mybir.dt.bfloat16)
        wcol_l = sb("wcol_l", [128, 2 * ET * 128], c=c_f1, dt=mybir.dt.bfloat16)
        fstage = sb("fstage", [128, 1024], c=c_f1)
        ps_f = [psum(f"ps_f{i}", [128, 512], c=c_f1) for i in range(2)]
        ps_tq = [psum(f"ps_tq{i}", [128, 512], c=c_f1) for i in range(2)]
        p4_pe_mark = {}
        m_ve_x2 = [0]

        with c_f1:
            for tt in range(T // 128):
                m_pe_pre = pe.n

                def s_ldx2(e, tt=tt, m_pe_pre=m_pe_pre):
                    gate(e, ld)
                    e.wait_ge(pe.h, m_pe_pre)
                    e.wait_ge(ve.h, ve.n)
                    ld.dinc(e.dma_start(out=x2b[:],
                                        in_=x2_s[tt * 128:(tt + 1) * 128, :]))
                sec("sync", s_ldx2)
                m_ld1 = ld.n
                emit_ln(x2b[:], xcb[:], wait_ld=m_ld1)

                def v_rsA(e):
                    e.drain()
                    ve.inc(e.tensor_tensor(out=rsA[:], in0=rstd[:],
                                           in1=scrow[:, 0:1], op=OP.mult))
                sec("vector", v_rsA)
                m_ve1 = ve.n
                m_ac_pre = ac.n

                def s_qdq(e, m_ve1=m_ve1):
                    e.wait_ge(ve.h, m_ve1)
                    e.activation(out=zqb[:], in_=xcb[:], func=AF.Identity,
                                 scale=rsA[:], bias=crow[:])
                    e.drain()
                    e.activation(out=zqb[:], in_=zqb[:], func=AF.Identity,
                                 bias=ncrow[:])
                    e.drain()
                    ac.inc(e.activation(out=xqtm[:], in_=zqb[:],
                                        func=AF.Identity, scale=scrow[:, 1:2]))
                sec("scalar", s_qdq)
                m_ac1 = ac.n
                m_pe0 = pe.n

                def t_trq(e, m_ac1=m_ac1, m_ve_x=ve.n):
                    e.wait_ge(ac.h, m_ac1)
                    e.wait_ge(ve.h, m_ve_x)
                    for k in range(ET):
                        if k >= 2:
                            e.wait_ge(ac.h, m_ac1 + k - 1)
                            e.wait_ge(ve.h, m_ve_x2[0] + k - 1)
                        pe.inc(e.transpose(
                            ps_tq[k % 2][:, 0:128],
                            xqtm[:, k * 128:(k + 1) * 128], ident[:]))
                m_ve_x2[0] = ve.n
                sec("tensor", t_trq)

                def s_trqev(e, tt=tt, m_pe0=m_pe0):
                    for k in range(ET):
                        e.wait_ge(pe.h, m_pe0 + k + 1)
                        ac.inc(e.copy(
                            out=xq_h[:, k * T + tt * 128:k * T + (tt + 1) * 128],
                            in_=ps_tq[k % 2][:, 0:128]))
                sec("scalar", s_trqev)
                m_ac_tr = ac.n - ET

                def v_xql(e, tt=tt, m_ac_tr=m_ac_tr):
                    for k in range(ET):
                        e.wait_ge(ac.h, m_ac_tr + k + 1)
                        e.drain()
                        ve.inc(e.tensor_tensor(
                            out=xq_l[:, k * T + tt * 128:k * T + (tt + 1) * 128],
                            in0=ps_tq[k % 2][:, 0:128],
                            in1=xq_h[:, k * T + tt * 128:k * T + (tt + 1) * 128],
                            op=OP.subtract))
                sec("vector", v_xql)
            barrier()
            P4_LG_BASE = (lgf[0].n, lgf[1].n)
            P4_VE_BASE = ve.n

            for wi, (wfsrc, dst_s, gcol) in enumerate(
                    (((wupf_h, wupf_l), up_s, 0),
                     ((wgatef_h, wgatef_l), gate_s, 1))):
                for it_ in range(IT):
                    wtile = wi * IT + it_
                    wbuf = wtile % 2
                    NTC = T // 512

                    def s_ldw(e, it_=it_, wfsrc=wfsrc, wbuf=wbuf, wtile=wtile):
                        gate(e, ld)
                        if wtile >= 2:
                            e.wait_ge(pe.h, p4_pe_mark[wtile - 2])
                        for k in range(ET):
                            ld.dinc(e.dma_start(
                                out=wcol_h[:, wbuf * ET * 128 + k * 128:
                                           wbuf * ET * 128 + (k + 1) * 128],
                                in_=wfsrc[0][k * 128:(k + 1) * 128,
                                             it_ * 128:(it_ + 1) * 128]))
                            ld.dinc(e.dma_start(
                                out=wcol_l[:, wbuf * ET * 128 + k * 128:
                                           wbuf * ET * 128 + (k + 1) * 128],
                                in_=wfsrc[1][k * 128:(k + 1) * 128,
                                             it_ * 128:(it_ + 1) * 128]))
                    sec("sync", s_ldw)
                    m_ldw = ld.n
                    m_pe0 = pe.n
                    m_ac0 = ac.n

                    def t_mm(e, m_ldw=m_ldw, m_ac0=m_ac0, wbuf=wbuf,
                             wtile=wtile):
                        e.wait_ge(ld.h, m_ldw)
                        e.wait_ge(ac.h, m_ac0)
                        for tc in range(NTC):
                            if tc >= 2:
                                e.wait_ge(ac.h, m_ac0 + tc - 1)
                            npass = 3 * ET
                            ip = 0
                            for k in range(ET):
                                wsl = slice(wbuf * ET * 128 + k * 128,
                                            wbuf * ET * 128 + (k + 1) * 128)
                                xsl = slice(k * T + tc * 512, k * T + (tc + 1) * 512)
                                for (wt, xt_) in ((wcol_h, xq_h), (wcol_h, xq_l),
                                                  (wcol_l, xq_h)):
                                    ins = e.matmul(
                                        ps_f[(wtile * NTC + tc) % 2][:],
                                        wt[:, wsl], xt_[:, xsl],
                                        start=(ip == 0), stop=(ip == npass - 1))
                                    ip += 1
                            pe.inc(ins)
                    sec("tensor", t_mm)
                    p4_pe_mark[wtile] = pe.n

                    def s_fev(e, m_pe0=m_pe0, wtile=wtile):
                        for tc in range(NTC):
                            gtc = wtile * NTC + tc
                            e.wait_ge(pe.h, m_pe0 + tc + 1)
                            e.wait_ge(lgf[gtc % 2].h,
                                      P4_LG_BASE[gtc % 2] + 16 * (gtc // 2))
                            if gtc >= 2:
                                e.wait_ge(ve.h, P4_VE_BASE + gtc - 1)
                            ac.inc(e.copy(
                                out=fstage[:, (gtc % 2) * 512:(gtc % 2) * 512 + 512],
                                in_=ps_f[gtc % 2][:]))
                    sec("scalar", s_fev)

                    def v_fmax(e, m_ac0=m_ac0, gcol=gcol, wtile=wtile):
                        for tc in range(NTC):
                            gtc = wtile * NTC + tc
                            e.wait_ge(ac.h, m_ac0 + tc + 1)
                            e.drain()
                            e.tensor_reduce(
                                out=tmax[:],
                                in_=fstage[:, (gtc % 2) * 512:(gtc % 2) * 512 + 512],
                                axis=AX, op=OP.max, apply_absolute_value=True)
                            e.drain()
                            ve.inc(e.tensor_tensor(out=gmax[:, gcol:gcol + 1],
                                                   in0=gmax[:, gcol:gcol + 1],
                                                   in1=tmax[:], op=OP.max))
                    sec("vector", v_fmax)
                    m_ve0 = ve.n - NTC

                    def g_fst(e, it_=it_, dst_s=dst_s, m_ac0=m_ac0, m_ve0=m_ve0,
                              wtile=wtile):
                        gate(e, *lgf)
                        for tc in range(NTC):
                            gtc = wtile * NTC + tc
                            e.wait_ge(ac.h, m_ac0 + tc + 1)
                            e.wait_ge(ve.h, m_ve0 + tc + 1)
                            lgf[gtc % 2].dinc(e.dma_start(
                                out=dst_s[it_ * 128:(it_ + 1) * 128,
                                          tc * 512:(tc + 1) * 512],
                                in_=fstage[:, (gtc % 2) * 512:(gtc % 2) * 512 + 512]))
                    sec("gpsimd", g_fst)
            barrier()

        ar_round(gmax[:, 0:2], 2, 1, [1.0 / 32767.0, 1.0 / 32767.0],
                 [(2, 3, 1.0), (4, 5, 1.0)])
        if stop_after == "ar2":
            dbg_finish()
            return nc

        c_f2 = contextlib.ExitStack()
        sA = sb("sA", [128, 2 * CW], c=c_f2)
        sB = sb("sB", [128, 2 * CW], c=c_f2)
        sZ = sb("sZ", [128, 2 * CW], c=c_f2)
        sG = sb("sG", [128, 2 * CW], c=c_f2)
        sSG = sb("sSG", [128, 2 * CW], c=c_f2)
        sO = sb("sO", [128, 2 * CW], c=c_f2)
        sOh = sb("sOh", [128, 2 * CW], c=c_f2, dt=mybir.dt.bfloat16)
        sOl = sb("sOl", [128, 2 * CW], c=c_f2, dt=mybir.dt.bfloat16)

        c_ac_mark = {}
        c_ve_mark = {}

        def qdq3_scalar(e, src_ap, dst_ap, rs_col, s_col, tmp_ap):
            e.activation(out=tmp_ap, in_=src_ap, func=AF.Identity,
                         scale=scrow[:, rs_col:rs_col + 1], bias=crow[:])
            e.drain()
            e.activation(out=tmp_ap, in_=tmp_ap, func=AF.Identity, bias=ncrow[:])
            e.drain()
            e.activation(out=dst_ap, in_=tmp_ap, func=AF.Identity,
                         scale=scrow[:, s_col:s_col + 1])

        def c_stream(srcs, dst, fn_scalar, fn_vector, gcol):
            c_ac_mark.clear()
            c_ve_mark.clear()
            LG_BASE = (lgc[0].n, lgc[1].n)
            for ti in range(NFT):
                i0 = (ti // (T // CW)) * 128
                cq = (ti % (T // CW)) * CW
                bufi = ti % 2

                def s_ld(e, i0=i0, cq=cq, bufi=bufi, ti=ti):
                    gate(e, ld)
                    if ti >= 2:
                        e.wait_ge(ve.h, c_ve_mark[ti - 2])
                        e.wait_ge(ac.h, c_ac_mark[ti - 2])
                    ld.dinc(e.dma_start(out=sA[:, bufi * CW:(bufi + 1) * CW],
                                        in_=srcs[0][i0:i0 + 128, cq:cq + CW]))
                    if len(srcs) > 1:
                        ld.dinc(e.dma_start(out=sB[:, bufi * CW:(bufi + 1) * CW],
                                            in_=srcs[1][i0:i0 + 128, cq:cq + CW]))
                sec("sync", s_ld)
                m_ld1 = ld.n

                fn_scalar(m_ld1, bufi, ti, LG_BASE)
                c_ac_mark[ti] = ac.n
                fn_vector(bufi, ti)

                def v_gmax(e, bufi=bufi, gcol=gcol):
                    if gcol is not None:
                        e.drain()
                        e.tensor_reduce(out=tmax[:],
                                        in_=sO[:, bufi * CW:(bufi + 1) * CW],
                                        axis=AX, op=OP.max,
                                        apply_absolute_value=True)
                        e.drain()
                        e.tensor_tensor(out=gmax[:, gcol:gcol + 1],
                                        in0=gmax[:, gcol:gcol + 1],
                                        in1=tmax[:], op=OP.max)
                    e.drain()
                    ve.inc(e.nop())
                sec("vector", v_gmax)
                c_ve_mark[ti] = ve.n
                m_ve2 = ve.n

                def g_st(e, i0=i0, cq=cq, bufi=bufi, m_ve2=m_ve2, ti=ti):
                    gate(e, *lgc)
                    e.wait_ge(ve.h, m_ve2)
                    e.wait_ge(ac.h, c_ac_mark[ti])
                    if isinstance(dst, tuple):
                        lgc[ti % 2].dinc(e.dma_start(
                            out=dst[0][i0:i0 + 128, cq:cq + CW],
                            in_=sOh[:, bufi * CW:(bufi + 1) * CW]))
                        lgc[ti % 2].dinc(e.dma_start(
                            out=dst[1][i0:i0 + 128, cq:cq + CW],
                            in_=sOl[:, bufi * CW:(bufi + 1) * CW]))
                    else:
                        lgc[ti % 2].dinc(e.dma_start(
                            out=dst[i0:i0 + 128, cq:cq + CW],
                            in_=sO[:, bufi * CW:(bufi + 1) * CW]))
                sec("gpsimd", g_st)

        with c_f2:
            def c1_scalar(m_ld1, bufi, ti, LG_BASE):
                def s_f(e):
                    e.wait_ge(ld.h, m_ld1)
                    if ti >= 2:
                        e.wait_ge(ve.h, c_ve_mark[ti - 2])
                    ga = sA[:, bufi * CW:(bufi + 1) * CW]
                    sGb = sG[:, bufi * CW:(bufi + 1) * CW]
                    qdq3_scalar(e, ga, sGb, 4, 5, sZ[:, bufi * CW:(bufi + 1) * CW])
                    e.drain()
                    ac.inc(e.activation(out=sSG[:, bufi * CW:(bufi + 1) * CW],
                                        in_=sGb, func=AF.Sigmoid))
                sec("scalar", s_f)

            def c1_vector(bufi, ti):
                def v_f(e):
                    e.wait_ge(ac.h, c_ac_mark[ti])
                    if ti >= 2:
                        e.wait_ge(lgc[ti % 2].h,
                                  c1_lgbase[ti % 2] + 16 * (ti // 2))
                    sGb = sG[:, bufi * CW:(bufi + 1) * CW]
                    sSb = sSG[:, bufi * CW:(bufi + 1) * CW]
                    e.drain()
                    e.tensor_scalar(out=sSb, in0=sSb, scalar1=65536.0,
                                    scalar2=C_RND, op0=OP.mult, op1=OP.add)
                    e.drain()
                    e.tensor_scalar(out=sSb, in0=sSb, scalar1=-C_RND,
                                    scalar2=65535.0, op0=OP.add, op1=OP.min)
                    e.drain()
                    e.tensor_tensor(out=sO[:, bufi * CW:(bufi + 1) * CW],
                                    in0=sGb, in1=sSb, op=OP.mult)
                sec("vector", v_f)

            c1_lgbase = (lgc[0].n, lgc[1].n)
            c_stream([gate_s], t_s, c1_scalar, c1_vector, 2)
            barrier()
            ar_round(gmax[:, 2:3], 1, 2, [2.0 ** -16 / 32767.0],
                     [(6, 7, 2.0 ** -16)])
            if stop_after == "c1":
                dbg_finish()
                return nc

            def c2_scalar(m_ld1, bufi, ti, LG_BASE):
                def s_f(e):
                    e.wait_ge(ld.h, m_ld1)
                    if ti >= 2:
                        e.wait_ge(ve.h, c_ve_mark[ti - 2])
                    ta = sA[:, bufi * CW:(bufi + 1) * CW]
                    qdq3_scalar(e, ta, sG[:, bufi * CW:(bufi + 1) * CW], 6, 7,
                                sZ[:, bufi * CW:(bufi + 1) * CW])
                    ac.inc(e.drain())
                sec("scalar", s_f)

            def c2_vector(bufi, ti):
                def v_f(e):
                    e.wait_ge(ac.h, c_ac_mark[ti])
                    if ti >= 2:
                        e.wait_ge(lgc[ti % 2].h,
                                  c2_lgbase[ti % 2] + 16 * (ti // 2))
                    ua = sB[:, bufi * CW:(bufi + 1) * CW]
                    sSb = sSG[:, bufi * CW:(bufi + 1) * CW]
                    e.drain()
                    e.scalar_tensor_tensor(
                        out=sSb, in0=ua, scalar=1.0,
                        in1=scrow[:, 2:3].broadcast_to([128, CW]),
                        op0=OP.mult, op1=OP.mult)
                    e.drain()
                    e.tensor_scalar(out=sSb, in0=sSb, scalar1=C_RND,
                                    scalar2=-C_RND, op0=OP.add, op1=OP.add)
                    e.drain()
                    e.scalar_tensor_tensor(
                        out=sSb, in0=sSb, scalar=1.0,
                        in1=scrow[:, 3:4].broadcast_to([128, CW]),
                        op0=OP.mult, op1=OP.mult)
                    e.drain()
                    e.tensor_tensor(out=sO[:, bufi * CW:(bufi + 1) * CW],
                                    in0=sG[:, bufi * CW:(bufi + 1) * CW],
                                    in1=sSb, op=OP.mult)
                sec("vector", v_f)

            c2_lgbase = (lgc[0].n, lgc[1].n)
            c_stream([t_s, up_s], t2_s, c2_scalar, c2_vector, 3)
            barrier()
            ar_round(gmax[:, 3:4], 1, 3, [1.0 / 32767.0], [(2, 3, 1.0)])

            def c3_scalar(m_ld1, bufi, ti, LG_BASE):
                def s_f(e):
                    e.wait_ge(ld.h, m_ld1)
                    if ti >= 2:
                        e.wait_ge(lgc[ti % 2].h,
                                  LG_BASE[ti % 2] + 32 * (ti // 2))
                    ta = sA[:, bufi * CW:(bufi + 1) * CW]
                    qdq3_scalar(e, ta, sO[:, bufi * CW:(bufi + 1) * CW], 2, 3,
                                sZ[:, bufi * CW:(bufi + 1) * CW])
                    ac.inc(e.drain())
                sec("scalar", s_f)

            def c3_vector(bufi, ti):
                def v_f(e):
                    e.wait_ge(ac.h, c_ac_mark[ti])
                    sl = slice(bufi * CW, (bufi + 1) * CW)
                    e.drain()
                    e.tensor_copy(out=sOh[:, sl], in_=sO[:, sl])
                    e.drain()
                    e.tensor_tensor(out=sOl[:, sl], in0=sO[:, sl],
                                    in1=sOh[:, sl], op=OP.subtract)
                sec("vector", v_f)

            c_stream([t2_s], (o_h_s, o_l_s), c3_scalar, c3_vector, None)
            barrier()

        c_f3 = contextlib.ExitStack()
        wdfull_h = sb("wdfull_h", [128, IT * E], c=c_f3, dt=mybir.dt.bfloat16)
        wdfull_l = sb("wdfull_l", [128, IT * E], c=c_f3, dt=mybir.dt.bfloat16)
        ocol_h = sb("ocol_h", [128, 2 * IT * 128], c=c_f3, dt=mybir.dt.bfloat16)
        ocol_l = sb("ocol_l", [128, 2 * IT * 128], c=c_f3, dt=mybir.dt.bfloat16)
        x2c3 = sb("x2c3", [128, 2 * E], c=c_f3)
        ostage = sb("ostage", [128, 2 * E], c=c_f3)
        ps_d = [psum(f"ps_d{i}", [128, 512], c=c_f3) for i in range(4)]

        with c_f3:
            def s_ldwd(e):
                gate(e, ld)
                for k in range(IT):
                    ld.dinc(e.dma_start(out=wdfull_h[:, k * E:(k + 1) * E],
                                        in_=wdownf_h[k * 128:(k + 1) * 128, :]))
                    ld.dinc(e.dma_start(out=wdfull_l[:, k * E:(k + 1) * E],
                                        in_=wdownf_l[k * 128:(k + 1) * 128, :]))
            sec("sync", s_ldwd)
            m_lwd = ld.n

            d_pe_mark = {}
            d_ve_mark = {}
            d_st_mark = {}
            for tt in range(T // 128):
                bufi = tt % 2

                def s_ldo(e, tt=tt, bufi=bufi):
                    gate(e, ld)
                    if tt >= 2:
                        e.wait_ge(pe.h, d_pe_mark[tt - 2])
                        e.wait_ge(ve.h, d_ve_mark[tt - 2])
                    for k in range(IT):
                        ld.dinc(e.dma_start(
                            out=ocol_h[:, bufi * IT * 128 + k * 128:
                                       bufi * IT * 128 + (k + 1) * 128],
                            in_=o_h_s[k * 128:(k + 1) * 128, tt * 128:(tt + 1) * 128]))
                        ld.dinc(e.dma_start(
                            out=ocol_l[:, bufi * IT * 128 + k * 128:
                                       bufi * IT * 128 + (k + 1) * 128],
                            in_=o_l_s[k * 128:(k + 1) * 128, tt * 128:(tt + 1) * 128]))
                    ld.dinc(e.dma_start(out=x2c3[:, bufi * E:(bufi + 1) * E],
                                        in_=x2_s[tt * 128:(tt + 1) * 128, :]))
                sec("sync", s_ldo)
                m_ldo = ld.n
                m_ve_pre = d_ve_mark.get(tt - 2, 0)
                m_pe0 = pe.n

                def t_down(e, bufi=bufi, m_ldo=m_ldo, m_ve_pre=m_ve_pre):
                    e.wait_ge(ld.h, max(m_ldo, m_lwd))
                    e.wait_ge(ve.h, m_ve_pre)
                    for oh in range(2):
                        npass = 3 * IT
                        ip = 0
                        for k in range(IT):
                            osl = slice(bufi * IT * 128 + k * 128,
                                        bufi * IT * 128 + (k + 1) * 128)
                            wsl = slice(k * E + oh * 512, k * E + oh * 512 + 512)
                            for (ot, wt) in ((ocol_h, wdfull_h), (ocol_h, wdfull_l),
                                             (ocol_l, wdfull_h)):
                                ins = e.matmul(
                                    ps_d[bufi * 2 + oh][:], ot[:, osl],
                                    wt[:, wsl],
                                    start=(ip == 0), stop=(ip == npass - 1))
                                ip += 1
                        pe.inc(ins)
                sec("tensor", t_down)
                d_pe_mark[tt] = pe.n
                m_pe1 = pe.n

                def v_out(e, bufi=bufi, m_pe1=m_pe1, tt=tt):
                    if tt >= 2:
                        e.wait_ge(sts[tt % 2].h, d_st_mark[tt - 2])
                    for oh in range(2):
                        e.wait_ge(pe.h, m_pe1 - 1 + oh)
                        e.drain()
                        e.tensor_tensor(
                            out=ostage[:, bufi * E + oh * 512:bufi * E + oh * 512 + 512],
                            in0=ps_d[bufi * 2 + oh][:],
                            in1=x2c3[:, bufi * E + oh * 512:bufi * E + oh * 512 + 512],
                            op=OP.add)
                    e.drain()
                    ve.inc(e.nop())
                sec("vector", v_out)
                d_ve_mark[tt] = ve.n
                m_ve1 = ve.n

                def g_out(e, tt=tt, bufi=bufi, m_ve1=m_ve1):
                    gate(e, *sts)
                    e.wait_ge(ve.h, m_ve1)
                    sts[tt % 2].dinc(e.dma_start(
                        out=out[tt * 128:(tt + 1) * 128, :],
                        in_=ostage[:, bufi * E:(bufi + 1) * E]))
                sec("gpsimd", g_out)
                d_st_mark[tt] = sts[tt % 2].n

            m_st0 = sts[0].n
            m_st1 = sts[1].n

            def g_fin(e):
                e.wait_ge(sts[0].h, m_st0)
                e.wait_ge(sts[1].h, m_st1)
            sec("gpsimd", g_fin)

    return nc


def kernel(L0_attn_x, L0_attn_Wq, L0_attn_bq, L0_attn_Wk, L0_attn_bk,
           L0_attn_Wv, L0_attn_bv, L0_attn_Wo, L0_attn_bo,
           L0_ffn_w_gate, L0_ffn_w_up, L0_ffn_w_down):
    x = np.ascontiguousarray(np.asarray(L0_attn_x, dtype=np.float32))
    Bfull, S, E = x.shape
    per = Bfull // N_CORES

    common = {
        "wqT": np.ascontiguousarray(np.asarray(L0_attn_Wq, np.float32).T),
        "wkT": np.ascontiguousarray(np.asarray(L0_attn_Wk, np.float32).T),
        "wvT": np.ascontiguousarray(np.asarray(L0_attn_Wv, np.float32).T),
        "woT": np.ascontiguousarray(np.asarray(L0_attn_Wo, np.float32).T),
        "bmq": np.ascontiguousarray(
            np.asarray(L0_attn_bq, np.float32).reshape(-1, 128).T),
        "bmk": np.ascontiguousarray(
            np.asarray(L0_attn_bk, np.float32).reshape(-1, 128).T),
        "bv": np.ascontiguousarray(np.asarray(L0_attn_bv, np.float32)),
        "bo": np.ascontiguousarray(np.asarray(L0_attn_bo, np.float32)),
        "w_up": np.ascontiguousarray(np.asarray(L0_ffn_w_up, np.float32)),
        "w_gate": np.ascontiguousarray(np.asarray(L0_ffn_w_gate, np.float32)),
        "w_down": np.ascontiguousarray(np.asarray(L0_ffn_w_down, np.float32)),
        "ident": np.eye(128, dtype=np.float32),
    }
    in_maps = []
    for c in range(N_CORES):
        m = dict(common)
        m["x"] = np.ascontiguousarray(x[c * per:(c + 1) * per].reshape(per * S, E))
        in_maps.append(m)

    nc = build_program(S=S, B=per)
    res = run_bass_kernel_spmd(nc, in_maps, core_ids=list(range(N_CORES)))
    outs = [np.asarray(res.results[c]["out"]).reshape(per, S, E)
            for c in range(N_CORES)]
    return np.concatenate(outs, axis=0)
